# version 06
# baseline (speedup 1.0000x reference)
import numpy as np
import ml_dtypes
from contextlib import ExitStack

import concourse.bass as bass
import concourse.mybir as mybir
from concourse.bass_utils import run_bass_kernel_spmd

F32 = mybir.dt.float32
BF16 = mybir.dt.bfloat16
AF = mybir.ActivationFunctionType
ALU = mybir.AluOpType

D, T, H, HD = 1024, 4096, 16, 64
NCH = 8
EPS = np.float32(1e-5)
LOC = 3840
NL = 6
NJ = 22
QT = [(i * 512, 512) for i in range(7)] + [(3584, 256)]
NW = LOC // 256

_2PI = 6.283185307179586


def _split12(x):
    f = np.float32(x)
    m, e = np.frexp(np.float64(f))
    return np.float32(np.ldexp(np.round(np.ldexp(m, 12)), int(e) - 12))


D1c = _split12(_2PI)
D2c = _split12(_2PI - np.float64(D1c))
D3c = np.float32(_2PI - np.float64(D1c) - np.float64(D2c))
INV2PI = np.float32(1.0 / _2PI)
MAGIC = np.float32(3 << 22)
PI = np.float32(np.pi)
HALFPI = np.float32(np.pi / 2)


def freqs_base_f32():
    hd2 = HD // 2
    top = np.float32(2595.0) * np.log10(np.float32(1.0 + 4000.0 / 200.0))
    lin = np.linspace(np.float32(0.0), np.float32(top), hd2, dtype=np.float32)
    mel = np.power(np.float32(10.0), lin / np.float32(2595.0), dtype=np.float32) - np.float32(1.0)
    return (np.float32(200.0) * mel / np.float32(1000.0)).astype(np.float32)


def inner_rotary_tables():
    WINDOW = 256
    half = np.arange(0, HD, 2, dtype=np.float32)
    inv_freq = (np.float32(1.0) / np.power(np.float32(10000.0), half / np.float32(HD), dtype=np.float32)).astype(np.float32)
    t = np.arange(T, dtype=np.float32)
    freqs = t[:, None] * inv_freq[None, :]
    freqs = np.concatenate([freqs, freqs], -1).astype(np.float32)
    base_scale = ((half + np.float32(0.4 * HD)) / np.float32(1.4 * HD)).astype(np.float32)
    power = ((t - np.float32(T // 2)) / np.float32(WINDOW // 2)).astype(np.float32)
    scale = np.power(base_scale[None, :], power[:, None], dtype=np.float32)
    scale = np.concatenate([scale, scale], -1).astype(np.float32)
    return freqs, scale


def rot_lhsT():
    M = np.zeros((128, 128), np.float32)
    for m in range(128):
        if m % 64 < 32:
            M[m, m + 32] = -1.0
        else:
            M[m, m - 32] = 1.0
    return np.ascontiguousarray(M.T)


def head_indicator():
    ind = np.zeros((128, 128), np.float32)
    for cc in range(NCH):
        for p in range(128):
            ind[p, cc * 16 + 2 * cc + (p // 64)] = 1.0
    return ind


def head_expander():
    e = np.zeros((16, NCH * 128), np.float32)
    for cc in range(NCH):
        for c in range(128):
            e[2 * cc + (c // 64), cc * 128 + c] = 1.0
    return e


def tile_weight(w, kc=None, mc=None):
    K, M = w.shape
    kc = (K + 127) // 128 if kc is None else kc
    mc = (M + 127) // 128 if mc is None else mc
    wp = np.zeros((kc * 128, mc * 128), np.float32)
    wp[:K, :M] = w
    return np.ascontiguousarray(
        wp.reshape(kc, 128, mc, 128).transpose(2, 1, 0, 3).reshape(mc, 128, kc * 128))


FB = freqs_base_f32()
IFREQS, ISCALE = inner_rotary_tables()
ICOS = np.cos(IFREQS).astype(np.float32)
ISIN = np.sin(IFREQS).astype(np.float32)
permA = np.concatenate([np.arange(0, HD, 2), np.arange(1, HD, 2)])
PERM_A = np.concatenate([h * HD + permA for h in range(H)])


class Sem:
    def __init__(self, nc, ctx, name):
        self.h = ctx.enter_context(nc.semaphore(name))
        self.n = 0

    def inc(self, instr, k=1):
        instr.then_inc(self.h, k)
        self.n += k
        return instr

    def wait(self, eng, at=None):
        eng.wait_ge(self.h, self.n if at is None else at)


class Builder:
    def __init__(self, debug=None):
        self.debug = debug
        self.nc = bass.Bass("TRN2", target_bir_lowering=False, debug=False,
                            detect_race_conditions=False)
        self.g = {}

    def declare(self):
        nc, g = self.nc, self.g

        def din(name, shape, dt=F32):
            g[name] = nc.dram_tensor(name, shape, dt, kind="ExternalInput").ap()

        def dint(name, shape, dt=F32):
            g[name] = nc.dram_tensor(name, shape, dt, kind="Internal").ap()

        din("x_fm", [D, T])
        din("x_loc", [D, LOC])
        din("wqA", [NCH, 128, D])
        din("wkA", [NCH, 128, D])
        din("wvA", [D, D])
        din("woA", [NCH, 128, D])
        din("lnA", [128, 16])
        din("theta", [1, 1])
        din("fb", [1, 32])
        din("t_q", [1, LOC])
        din("t_k", [1, T])
        din("Rm", [128, 128])
        din("IdeM", [64, 64])
        din("IND", [128, 128])
        din("EXPD", [16, NCH * 128])
        din("Wqk", [NL, 16, 128, D])
        din("Wv6", [NL, D, D])
        din("Wo6", [NL, NCH, 128, D])
        din("W1t", [NL, 44, 128, D])
        din("W2t", [NL, NCH, 128, NJ * 128])
        din("b1c", [NL, 128, 44])
        din("b2c", [NL, 128, 8])
        din("lng", [NL, 128, 8])
        din("lnb6", [NL, 128, 8])
        din("qkv4", [NL, 128, 4])
        din("qkabs", [NL, 1, 128])
        din("rpAq", [128, LOC], BF16)
        din("rpBq", [128, LOC], BF16)
        din("rpAk", [128, LOC], BF16)
        din("rpBk", [128, LOC], BF16)

        g["out"] = nc.dram_tensor("out", [D, LOC], F32, kind="ExternalOutput").ap()
        if self.debug in ("x1", "deep", "inner"):
            g["dbg"] = nc.dram_tensor("dbg", [D, LOC], F32, kind="ExternalOutput").ap()
        if self.debug == "inner":
            for li in range(NL):
                g[f"d_ya{li}"] = nc.dram_tensor(f"d_ya{li}", [D, LOC], F32, kind="ExternalOutput").ap()
                g[f"d_y{li}"] = nc.dram_tensor(f"d_y{li}", [D, LOC], F32, kind="ExternalOutput").ap()
        if self.debug == "deep":
            g["d_tabCk"] = nc.dram_tensor("d_tabCk", [128, T], BF16, kind="ExternalOutput").ap()
            g["d_tabSk"] = nc.dram_tensor("d_tabSk", [128, T], BF16, kind="ExternalOutput").ap()
            g["d_khat"] = nc.dram_tensor("d_khat", [128, NCH * T], BF16, kind="ExternalOutput").ap()
            g["d_vext"] = nc.dram_tensor("d_vext", [128, 32 * 1040], BF16, kind="ExternalOutput").ap()
            g["d_vsum"] = nc.dram_tensor("d_vsum", [1, 1056], BF16, kind="ExternalOutput").ap()
            g["d_qhat"] = nc.dram_tensor("d_qhat", [128, NCH * 512], BF16, kind="ExternalOutput").ap()
            g["d_osb"] = nc.dram_tensor("d_osb", [128, NCH * 512], BF16, kind="ExternalOutput").ap()

        dint("h_dram", [128, NCH * T], BF16)
        dint("y_dram", [D, LOC])
        dint("ya_dram", [D, LOC])
        dint("x1_dram", [D, LOC])
        dint("hm_dram", [128, NJ * LOC], BF16)

    def bk(self, i, p0=0, p1=128, c0=0, c1=512):
        t = self.psA if i < 4 else self.psB
        return t[p0:p1, (i % 4) * 512 + c0:(i % 4) * 512 + c1]

    def emit_ln(self, qw, src_f32, src_bf, dst, gamma=None, beta=None, pre_wait=None):
        S, blk, bk = self.S, self.blk, self.bk
        rows = self.rows
        t1b, v1b, s1b, v2b = S["t1"].n, S["v1"].n, S["s1"].n, S["v2"].n

        @blk.vector
        def _(e):
            if pre_wait is not None:
                pre_wait(e)
            for cc in range(NCH):
                if cc >= 2:
                    S["t1"].wait(e, t1b + cc - 1)
                if src_f32 is not None:
                    e.tensor_copy(src_bf(cc), src_f32(cc))
                sq = self.scr1 if cc % 2 == 0 else self.scr2
                S["v1"].inc(e.tensor_tensor(sq[:, 0:qw], src_bf(cc), src_bf(cc), ALU.mult))
            S["t1"].wait(e, t1b + 8)
            e.tensor_single_scalar(rows[0:1, 0:qw], bk(6, 0, 1, 0, qw), float(np.float32(1.0 / D)), ALU.mult)
            e.tensor_single_scalar(rows[0:1, 512:512 + qw], bk(5, 0, 1, 0, qw), float(np.float32(1.0 / D)), ALU.mult)
            e.tensor_tensor(rows[0:1, 1024:1024 + qw], rows[0:1, 0:qw], rows[0:1, 0:qw], ALU.mult)
            S["v1"].inc(e.tensor_tensor(rows[0:1, 512:512 + qw], rows[0:1, 512:512 + qw],
                                        rows[0:1, 1024:1024 + qw], ALU.subtract))

        @blk.scalar
        def _(e):
            S["v1"].wait(e, v1b + 9)
            e.activation(rows[0:1, 512:512 + qw], rows[0:1, 512:512 + qw], AF.Ln,
                         bias=self.misc[0:1, 201:202])
            S["s1"].inc(e.activation(rows[0:1, 512:512 + qw], rows[0:1, 512:512 + qw],
                                     AF.Exp, scale=self.misc[0:1, 202:203]))

        @blk.vector
        def _(e):
            S["s1"].wait(e, s1b + 1)
            S["v2"].inc(e.tensor_tensor(rows[0:1, 1024:1024 + qw], rows[0:1, 0:qw],
                                        rows[0:1, 512:512 + qw], ALU.mult))

        @blk.tensor
        def _(e):
            S["s0"].wait(e)
            S["s2"].wait(e)
            S["v0"].wait(e)
            for cc in range(NCH):
                S["v1"].wait(e, v1b + cc + 1)
                sq = self.scr1 if cc % 2 == 0 else self.scr2
                e.matmul(bk(6, 0, 1, 0, qw), self.ones_col, src_bf(cc),
                         start=(cc == 0), stop=(cc == NCH - 1))
                S["t1"].inc(e.matmul(bk(5, 0, 1, 0, qw), self.onesc_f, sq[:, 0:qw],
                                     start=(cc == 0), stop=(cc == NCH - 1)))
            S["v1"].wait(e, v1b + 9)
            S["v2"].wait(e, v2b + 1)
            e.matmul(bk(7, 0, 128, 0, qw), self.onesf2[0:1, :], rows[0:1, 512:512 + qw],
                     start=True, stop=True)
            S["t1"].inc(e.matmul(bk(5, 0, 128, 0, qw), self.onesf2[0:1, :],
                                 rows[0:1, 1024:1024 + qw], start=True, stop=True))

        va = S["v1"].n

        @blk.vector
        def _(e):
            S["t1"].wait(e, t1b + 9)
            for cc in range(NCH):
                src = src_f32(cc) if src_f32 is not None else src_bf(cc)
                e.tensor_tensor(self.scr1[:, 0:qw], src, bk(7, 0, 128, 0, qw), ALU.mult)
                if gamma is None:
                    ins = e.tensor_tensor(dst(cc), self.scr1[:, 0:qw],
                                          bk(5, 0, 128, 0, qw), ALU.subtract)
                else:
                    e.tensor_tensor(self.scr1[:, 0:qw], self.scr1[:, 0:qw],
                                    bk(5, 0, 128, 0, qw), ALU.subtract)
                    ins = e.tensor_scalar(dst(cc), self.scr1[:, 0:qw],
                                          gamma(cc), beta(cc), ALU.mult, ALU.add)
            S["v1"].inc(ins)

        return va + 1

    def angle_tables(self, trow, ncols, dstC, dstS, pre=None):
        S, blk, bk = self.S, self.blk, self.bk
        nt = (ncols + 511) // 512
        tb, vb, sb_ = S["t0"].n, S["v0"].n, S["s0"].n
        v1b = S["v1"].n

        @blk.tensor
        def _(e):
            if pre is not None:
                pre(e)
            S["v0"].wait(e, vb)
            S["v1"].wait(e, v1b)
            for i in range(nt):
                t0, w = i * 512, min(512, ncols - i * 512)
                if i > 0:
                    S["v0"].wait(e, vb + i)
                S["t0"].inc(e.matmul(bk(7, 0, 128, 0, w), self.fdup,
                                     trow[0:1, t0:t0 + w], start=True, stop=True))

        @blk.vector
        def _(e):
            for i in range(nt):
                t0, w = i * 512, min(512, ncols - i * 512)
                S["t0"].wait(e, tb + i + 1)
                if i > 0:
                    S["s0"].wait(e, sb_ + 2 * i)
                a = bk(7, 0, 128, 0, w)
                r = bk(6, 0, 128, 0, w)
                ab = bk(5, 0, 128, 0, w)
                nn = self.scr1[:, 0:w]
                nh = self.scr2[:, 0:w]
                e.tensor_scalar(nn, a, float(INV2PI), float(MAGIC), ALU.mult, ALU.add)
                e.tensor_single_scalar(nn, nn, float(MAGIC), ALU.subtract)
                e.tensor_scalar(nh, nn, float(np.float32(1.0 / 64.0)), float(MAGIC),
                                ALU.mult, ALU.add)
                e.tensor_scalar(nh, nh, float(MAGIC), 64.0, ALU.subtract, ALU.mult)
                e.scalar_tensor_tensor(r, nh, float(-D1c), a, ALU.mult, ALU.add)
                e.scalar_tensor_tensor(r, nh, float(-D2c), r, ALU.mult, ALU.add)
                e.tensor_tensor(nh, nn, nh, ALU.subtract)
                e.scalar_tensor_tensor(r, nh, float(-D1c), r, ALU.mult, ALU.add)
                e.scalar_tensor_tensor(r, nh, float(-D2c), r, ALU.mult, ALU.add)
                e.scalar_tensor_tensor(r, nn, float(-D3c), r, ALU.mult, ALU.add)
                e.tensor_single_scalar(r, r, float(PI), ALU.min)
                e.tensor_single_scalar(r, r, float(-PI), ALU.max)
                e.tensor_single_scalar(nn, r, -1.0, ALU.mult)
                S["v0"].inc(e.tensor_max(ab, nn, r))

        @blk.scalar
        def _(e):
            for i in range(nt):
                t0, w = i * 512, min(512, ncols - i * 512)
                S["v0"].wait(e, vb + i + 1)
                S["s0"].inc(e.activation(dstS[:, t0:t0 + w], bk(6, 0, 128, 0, w), AF.Sin))
                S["s0"].inc(e.activation(dstC[:, t0:t0 + w], bk(5, 0, 128, 0, w),
                                         AF.Sin, scale=self.misc[:, 203:204],
                                         bias=self.misc[:, 200:201]))

    def build(self):
        self.declare()
        nc, g = self.nc, self.g
        with ExitStack() as ctx:
            self.ctx = ctx
            S = self.S = {n: Sem(nc, ctx, n) for n in
                          ["dW", "dH", "dO", "t0", "t1", "t2", "t3", "s0", "s1",
                           "s2", "v0", "v1", "v2", "v3", "g0"]}
            self.psA = ctx.enter_context(nc.psum_tensor("psA", [128, 2048], F32))
            self.psB = ctx.enter_context(nc.psum_tensor("psB", [128, 2048], F32))

            def sbuf(name, shape, dt=F32):
                return ctx.enter_context(nc.sbuf_tensor(name, shape, dt))
            self.sbuf = sbuf
            self.blk = ctx.enter_context(nc.Block())
            blk = self.blk

            cst = sbuf("cst", [128, 200], BF16)
            self.R_sb = cst[:, 0:128]
            self.ones_col = cst[:, 128:129]
            self.IDE = cst[0:64, 132:196]
            self.onesrow = sbuf("onesrow", [1, 512], BF16)
            self.onesf2 = sbuf("onesf2", [128, 128], F32)
            self.onesc_f = self.onesf2[:, 0:1]
            self.lnA_sb = sbuf("lnA_sb", [128, 16])
            self.misc = sbuf("misc", [128, 256])
            self.rows = sbuf("rows", [1, 1536])
            self.scr1 = sbuf("scr1", [128, 512])
            self.scr2 = sbuf("scr2", [128, 512])
            self.tq1 = sbuf("tq1", [1, 512])
            self.dsl = sbuf("dsl", [128, 1024])

            @blk.sync
            def _(e):
                S["dH"].inc(e.dma_start(out=self.lnA_sb[:], in_=g["lnA"]), 16)
                S["dH"].inc(e.dma_start(out=self.misc[0:1, 0:1], in_=g["theta"]), 16)
                S["dH"].inc(e.dma_start(out=self.misc[0:1, 32:64], in_=g["fb"]), 16)


            @blk.gpsimd
            def _(e):
                S["dW"].inc(e.dma_start(out=self.R_sb, in_=g["Rm"]), 16)
                S["dW"].inc(e.dma_start(out=self.IDE, in_=g["IdeM"]), 16)
                e.memset(self.ones_col, 1.0)
                e.memset(self.onesrow[:], 1.0)
                e.memset(self.misc[:, 200:201], float(HALFPI))
                e.memset(self.misc[:, 201:202], float(EPS))
                e.memset(self.misc[:, 202:203], -0.5)
                e.memset(self.misc[:, 203:204], -1.0)
                e.memset(self.misc[:, 204:205], 0.125)
                e.memset(self.misc[:, 205:206], 1.0)
                e.memset(self.misc[:, 206:207], 8.0)
                e.memset(self.misc[:, 208:209], 1e-24)
                e.memset(self.misc[:, 209:210], -40.0)
                S["g0"].inc(e.memset(self.onesf2[:], 1.0))

            @blk.vector
            def _(e):
                S["dH"].wait(e)
                S["g0"].wait(e)
                for r in range(4):
                    e.tensor_scalar(self.misc[0:1, 64 + 32 * r:96 + 32 * r],
                                    self.misc[0:1, 32:64],
                                    self.misc[0:1, 0:1],
                                    float(np.float32(1.0 / 220.0)),
                                    ALU.mult, ALU.mult)
                S["v0"].inc(e.engine_nop() if hasattr(e, "engine_nop") else e.nop())
            self.fdup = self.misc[0:1, 64:192]

            stageA_stack = ExitStack()
            self.khat = stageA_stack.enter_context(nc.sbuf_tensor("khat", [128, NCH * T], BF16))
            self.vext = stageA_stack.enter_context(nc.sbuf_tensor("vext", [128, 32 * 1040], BF16))
            self.vsum = stageA_stack.enter_context(nc.sbuf_tensor("vsum", [1, 1056], BF16))
            with nc.sbuf_tensor("tabCk", [128, T], BF16) as tabCk, \
                 nc.sbuf_tensor("tabSk", [128, T], BF16) as tabSk:
                with nc.sbuf_tensor("tkrow", [1, T], F32) as tkrow:
                    dHb0 = S["dH"].n

                    @blk.sync
                    def _(e):
                        S["dH"].inc(e.dma_start(out=tkrow[0:1, :], in_=g["t_k"]), 16)

                    self.angle_tables(tkrow[0:1, :], T, tabCk, tabSk,
                                      pre=lambda e: S["dH"].wait(e, dHb0 + 16))
                if self.debug == "deep":
                    @blk.sync
                    def _(e):
                        S["s0"].wait(e)
                        S["dO"].inc(e.dma_start(out=g["d_tabCk"], in_=tabCk[:]), 16)
                        S["dO"].inc(e.dma_start(out=g["d_tabSk"], in_=tabSk[:]), 16)
                self.p1_hbuild()
                self.p2_khat(tabCk, tabSk)
            self.p3_vext()
            if self.debug == "deep":
                @blk.sync
                def _(e):
                    S["v0"].wait(e)
                    S["dO"].inc(e.dma_start(out=g["d_khat"], in_=self.khat[:]), 16)
                    S["dO"].inc(e.dma_start(out=g["d_vext"], in_=self.vext[:, 0:32 * 1040]), 16)
                    S["dO"].inc(e.dma_start(out=g["d_vsum"], in_=self.vsum[:]), 16)
            self.p4_attention()
            stageA_stack.close()

            if self.debug != "x1":
                self.inner_layers()
                self.p9_final()

            @blk.sync
            def _(e):
                S["dO"].wait(e)
        return nc, g

    def p1_hbuild(self):
        S, blk, g = self.S, self.blk, self.g
        with self.nc.sbuf_tensor("xb2", [128, 2 * NCH * 512], BF16) as xb2, \
             self.nc.sbuf_tensor("hloc", [128, 2 * NCH * 512], BF16) as hloc:
            dWb = S["dW"].n
            applies = []

            @blk.gpsimd
            def _(e):
                S["t0"].wait(e)
                for tt in range(8):
                    S["dW"].inc(e.dma_start(
                        out=xb2[:, (tt % 2) * NCH * 512:(tt % 2) * NCH * 512 + NCH * 512].rearrange("p (c t) -> p c t", t=512),
                        in_=g["x_fm"].rearrange("(c p) t -> p c t", p=128)[:, :, tt * 512:(tt + 1) * 512]), 16)


            for tt in range(8):
                base = (tt % 2) * NCH * 512

                @blk.gpsimd
                def _(e, tt=tt, base=base):
                    if tt == 0:
                        S["t0"].wait(e)
                    if tt >= 2:
                        S["v1"].wait(e, applies[tt - 2])
                    S["dW"].inc(e.dma_start(
                        out=xb2[:, base:base + NCH * 512].rearrange("p (c t) -> p c t", t=512),
                        in_=g["x_fm"].rearrange("(c p) t -> p c t", p=128)[:, :, tt * 512:(tt + 1) * 512]), 16)
                dma_mark = S["dW"].n

                def pre_wait(e, dma_mark=dma_mark, tt=tt):
                    S["dW"].wait(e, dma_mark)

                va = self.emit_ln(
                    512,
                    src_f32=None,
                    src_bf=lambda cc, base=base: xb2[:, base + cc * 512:base + cc * 512 + 512],
                    dst=lambda cc, base=base: hloc[:, base + cc * 512:base + cc * 512 + 512],
                    gamma=lambda cc: self.lnA_sb[:, cc:cc + 1],
                    beta=lambda cc: self.lnA_sb[:, 8 + cc:9 + cc],
                    pre_wait=pre_wait)
                applies.append(va)

                @blk.sync
                def _(e, tt=tt, base=base, va=va):
                    S["v1"].wait(e, va)
                    S["dH"].inc(e.dma_start(
                        out=g["h_dram"].rearrange("p (c t) -> p c t", t=T)[:, :, tt * 512:(tt + 1) * 512],
                        in_=hloc[:, base:base + NCH * 512].rearrange("p (c t) -> p c t", t=512)), 16)

            @blk.sync
            def _(e):
                S["dH"].wait(e)

    def p2_khat(self, tabCk, tabSk):
        S, blk, bk, g = self.S, self.blk, self.bk, self.g
        with self.nc.sbuf_tensor("wk_sb", [128, NCH * 1024], BF16) as wk_sb, \
             self.nc.sbuf_tensor("hti", [128, 2 * NCH * 512], BF16) as hti, \
             self.nc.sbuf_tensor("kraw", [128, 2 * 512], BF16) as kraw, \
             self.nc.sbuf_tensor("t1sb", [128, 2 * 512], F32) as t1sb:
            dWb, dHb = S["dW"].n, S["dH"].n
            t0b, t1b, s0b, v0b = S["t0"].n, S["t1"].n, S["s0"].n, S["v0"].n

            @blk.gpsimd
            def _(e):
                S["v1"].wait(e)
                S["dH"].wait(e)
                for oc in range(NCH):
                    S["dW"].inc(e.dma_start(out=wk_sb[:, oc * 1024:(oc + 1) * 1024],
                                            in_=g["wkA"][oc]), 16)

            @blk.sync
            def _(e):
                S["v1"].wait(e)
                for tt in range(8):
                    if tt >= 2:
                        S["t0"].wait(e, t0b + (tt - 1) * 8)
                    S["dH"].inc(e.dma_start(
                        out=hti[:, (tt % 2) * NCH * 512:(tt % 2) * NCH * 512 + NCH * 512].rearrange("p (c t) -> p c t", t=512),
                        in_=g["h_dram"].rearrange("p (c t) -> p c t", t=T)[:, :, tt * 512:(tt + 1) * 512]), 16)

            @blk.tensor
            def _(e):
                S["dW"].wait(e, dWb + 16 * NCH)
                S["s0"].wait(e)
                i = 0
                for tt in range(8):
                    S["dH"].wait(e, dHb + 16 * (tt + 1))
                    hb = (tt % 2) * NCH * 512
                    for oc in range(NCH):
                        if i >= 2:
                            S["s0"].wait(e, s0b + i - 1)
                        for kc in range(NCH):
                            ins = e.matmul(bk(i % 2),
                                           wk_sb[:, oc * 1024 + kc * 128:oc * 1024 + kc * 128 + 128],
                                           hti[:, hb + kc * 512:hb + kc * 512 + 512],
                                           start=(kc == 0), stop=(kc == NCH - 1))
                        S["t0"].inc(ins)
                        if i >= 1:
                            S["s0"].wait(e, s0b + i)
                            S["v0"].wait(e, v0b + max(0, i - 1))
                            S["t1"].inc(e.matmul(bk(2 + (i - 1) % 2), self.R_sb,
                                                 kraw[:, ((i - 1) % 2) * 512:((i - 1) % 2) * 512 + 512],
                                                 start=True, stop=True))
                        i += 1
                S["s0"].wait(e, s0b + 64)
                S["v0"].wait(e, v0b + 62)
                S["t1"].inc(e.matmul(bk(3), self.R_sb, kraw[:, 512:1024],
                                     start=True, stop=True))

            @blk.scalar
            def _(e):
                for i in range(64):
                    S["t0"].wait(e, t0b + i + 1)
                    if i >= 2:
                        S["v0"].wait(e, v0b + i - 1)
                    S["s0"].inc(e.activation(kraw[:, (i % 2) * 512:(i % 2) * 512 + 512],
                                             bk(i % 2), AF.Copy))

            @blk.vector
            def _(e):
                for i in range(64):
                    tt, oc = i // 8, i % 8
                    S["t1"].wait(e, t1b + i + 1)
                    kb = kraw[:, (i % 2) * 512:(i % 2) * 512 + 512]
                    dst = self.khat[:, oc * T + tt * 512:oc * T + tt * 512 + 512]
                    tb = t1sb[:, (i % 2) * 512:(i % 2) * 512 + 512]
                    e.tensor_tensor(tb, kb, tabCk[:, tt * 512:tt * 512 + 512], ALU.mult)
                    e.tensor_tensor(self.scr2[:], bk(2 + i % 2), tabSk[:, tt * 512:tt * 512 + 512], ALU.mult)
                    S["v0"].inc(e.tensor_tensor(dst, tb, self.scr2[:], ALU.add))

    def p3_vext(self):
        S, blk, bk, g = self.S, self.blk, self.bk, self.g
        with self.nc.sbuf_tensor("wv_sb", [128, NCH * 1024], BF16) as wv_sb, \
             self.nc.sbuf_tensor("hti3", [128, 2 * NCH * 512], BF16) as hti:
            dWb, dHb = S["dW"].n, S["dH"].n
            t0b, v0b, g0b = S["t0"].n, S["v0"].n, S["g0"].n

            @blk.gpsimd
            def _(e):
                S["v0"].wait(e)
                S["t0"].wait(e)
                ap = self.vext.rearrange("p (a c) -> p a c", c=65)
                S["g0"].inc(e.memset(ap[:, :, 64:65], 1.0))
                for kc in range(NCH):
                    S["dW"].inc(e.dma_start(out=wv_sb[:, kc * 1024:(kc + 1) * 1024],
                                            in_=g["wvA"][kc * 128:(kc + 1) * 128, :]), 16)

            @blk.sync
            def _(e):
                S["t0"].wait(e)
                S["v0"].wait(e)
                for tt in range(8):
                    if tt >= 2:
                        S["t0"].wait(e, t0b + (tt - 1) * 8)
                    S["dH"].inc(e.dma_start(
                        out=hti[:, (tt % 2) * NCH * 512:(tt % 2) * NCH * 512 + NCH * 512].rearrange("p (c t) -> p c t", t=512),
                        in_=g["h_dram"].rearrange("p (c t) -> p c t", t=T)[:, :, tt * 512:(tt + 1) * 512]), 16)

            @blk.tensor
            def _(e):
                S["dW"].wait(e, dWb + 16 * NCH)
                S["g0"].wait(e, g0b + 1)
                i = 0
                for tt in range(8):
                    S["dH"].wait(e, dHb + 16 * (tt + 1))
                    hb = (tt % 2) * NCH * 512
                    for ts in range(4):
                        for et in range(2):
                            if i >= 3:
                                S["v0"].wait(e, v0b + i - 2)
                            for kc in range(NCH):
                                ins = e.matmul(
                                    bk(4 + i % 3),
                                    hti[:, hb + kc * 512 + ts * 128:hb + kc * 512 + ts * 128 + 128],
                                    wv_sb[:, kc * 1024 + et * 512:kc * 1024 + et * 512 + 512],
                                    start=(kc == 0), stop=(kc == NCH - 1))
                            S["t0"].inc(ins)
                            i += 1

            @blk.vector
            def _(e):
                i = 0
                for tt in range(8):
                    for ts in range(4):
                        for et in range(2):
                            S["t0"].wait(e, t0b + i + 1)
                            gts = tt * 4 + ts
                            dst = self.vext.rearrange("p (a c) -> p a c", c=65)[
                                :, gts * 16 + et * 8:gts * 16 + et * 8 + 8, 0:64]
                            S["v0"].inc(e.tensor_copy(
                                dst, bk(4 + i % 3).rearrange("p (h c) -> p h c", c=64)))
                            i += 1

            t0b2 = S["t0"].n

            @blk.tensor
            def _(e):
                S["v0"].wait(e)
                for ts in range(32):
                    for pi, (c0, w) in enumerate([(0, 512), (512, 512), (1024, 16)]):
                        ins = e.matmul(bk(pi, 0, 1, 0, w), self.ones_col,
                                       self.vext[:, ts * 1040 + c0:ts * 1040 + c0 + w],
                                       start=(ts == 0), stop=(ts == 31))
                S["t0"].inc(ins)

            @blk.vector
            def _(e):
                S["t0"].wait(e, t0b2 + 1)
                e.tensor_copy(self.vsum[0:1, 0:512], bk(0, 0, 1, 0, 512))
                e.tensor_copy(self.vsum[0:1, 512:1024], bk(1, 0, 1, 0, 512))
                S["v0"].inc(e.tensor_copy(self.vsum[0:1, 1024:1040], bk(2, 0, 1, 0, 16)))

    def p4_attention(self):
        nc = self.nc
        with nc.sbuf_tensor("xt", [128, NCH * 512], F32) as xt, \
             nc.sbuf_tensor("xtb", [128, NCH * 512], BF16) as xtb, \
             nc.sbuf_tensor("hq", [128, NCH * 512], BF16) as hq, \
             nc.sbuf_tensor("qhat", [128, NCH * 512], BF16) as qhat, \
             nc.sbuf_tensor("qraw", [128, 2 * 512], BF16) as qraw, \
             nc.sbuf_tensor("wq2", [128, 2 * 1024], BF16) as wq2, \
             nc.sbuf_tensor("wo2", [128, 2 * 1024], BF16) as wo2, \
             nc.sbuf_tensor("u2", [128, 4 * 512], BF16) as u2, \
             nc.sbuf_tensor("qtC", [128, 512], BF16) as qtC, \
             nc.sbuf_tensor("qtS", [128, 512], BF16) as qtS:
            osb = hq
            dsl = self.dsl
            for qi, (q0, qw) in enumerate(QT):
                self.p4_qtile(qi, q0, qw, xt, xtb, hq, qhat, qraw, wq2, wo2,
                              u2, osb, dsl, qtC, qtS)

    def p4_qtile(self, qi, q0, qw, xt, xtb, hq, qhat, qraw, wq2, wo2,
                 u2, osb, dsl, qtC, qtS):
        S, blk, bk, g = self.S, self.blk, self.bk, self.g
        dHb = S["dH"].n

        @blk.sync
        def _(e):
            if qi == 0:
                S["t0"].wait(e)
                S["v0"].wait(e)
            if qi > 0:
                S["dO"].wait(e)
            S["dH"].inc(e.dma_start(out=self.tq1[0:1, 0:qw],
                                    in_=g["t_q"][0:1, q0:q0 + qw]), 16)
            S["dH"].inc(e.dma_start(
                out=xt[:, 0:NCH * qw].rearrange("p (c t) -> p c t", t=qw),
                in_=g["x_loc"].rearrange("(c p) t -> p c t", p=128)[:, :, q0:q0 + qw]), 16)

        self.angle_tables(self.tq1[0:1, 0:qw], qw, qtC, qtS,
                          pre=lambda e: S["dH"].wait(e, dHb + 16))

        def pre_wait(e):
            S["dH"].wait(e, dHb + 32)

        hva = self.emit_ln(
            qw,
            src_f32=lambda cc: xt[:, cc * qw:cc * qw + qw],
            src_bf=lambda cc: xtb[:, cc * 512:cc * 512 + qw],
            dst=lambda cc: hq[:, cc * 512:cc * 512 + qw],
            gamma=lambda cc: self.lnA_sb[:, cc:cc + 1],
            beta=lambda cc: self.lnA_sb[:, 8 + cc:9 + cc],
            pre_wait=pre_wait)

        dWb = S["dW"].n
        t0b, t1b2, s0b, v0b = S["t0"].n, S["t1"].n, S["s0"].n, S["v0"].n

        @blk.gpsimd
        def _(e):
            S["t0"].wait(e)
            for oc in range(NCH):
                if oc >= 2:
                    S["t0"].wait(e, t0b + oc - 1)
                S["dW"].inc(e.dma_start(out=wq2[:, (oc % 2) * 1024:(oc % 2) * 1024 + 1024],
                                        in_=g["wqA"][oc]), 16)

        @blk.tensor
        def _(e):
            S["v1"].wait(e, hva)
            for oc in range(NCH):
                S["dW"].wait(e, dWb + 16 * (oc + 1))
                if oc >= 2:
                    S["s0"].wait(e, s0b + oc - 1)
                for kc in range(NCH):
                    ins = e.matmul(bk(oc % 2, 0, 128, 0, qw),
                                   wq2[:, (oc % 2) * 1024 + kc * 128:(oc % 2) * 1024 + kc * 128 + 128],
                                   hq[:, kc * 512:kc * 512 + qw],
                                   start=(kc == 0), stop=(kc == NCH - 1))
                S["t0"].inc(ins)
                if oc >= 1:
                    S["s0"].wait(e, s0b + oc)
                    S["v0"].wait(e, v0b + max(0, oc - 2))
                    S["t1"].inc(e.matmul(bk(2 + (oc - 1) % 2, 0, 128, 0, qw), self.R_sb,
                                         qraw[:, ((oc - 1) % 2) * 512:((oc - 1) % 2) * 512 + qw],
                                         start=True, stop=True))
            S["s0"].wait(e, s0b + 8)
            S["v0"].wait(e, v0b + 6)
            S["t1"].inc(e.matmul(bk(3, 0, 128, 0, qw), self.R_sb,
                                 qraw[:, 512:512 + qw], start=True, stop=True))

        @blk.scalar
        def _(e):
            for oc in range(NCH):
                S["t0"].wait(e, t0b + oc + 1)
                if oc >= 2:
                    S["v0"].wait(e, v0b + oc - 1)
                S["s0"].inc(e.activation(qraw[:, (oc % 2) * 512:(oc % 2) * 512 + qw],
                                         bk(oc % 2, 0, 128, 0, qw), AF.Copy))

        @blk.vector
        def _(e):
            for oc in range(NCH):
                S["t1"].wait(e, t1b2 + oc + 1)
                qb = qraw[:, (oc % 2) * 512:(oc % 2) * 512 + qw]
                dst = qhat[:, oc * 512:oc * 512 + qw]
                e.tensor_tensor(self.scr1[:, 0:qw], qb, qtC[:, 0:qw], ALU.mult)
                e.tensor_tensor(self.scr2[:, 0:qw], bk(2 + (oc % 2), 0, 128, 0, qw),
                                qtS[:, 0:qw], ALU.mult)
                S["v0"].inc(e.tensor_tensor(dst, self.scr1[:, 0:qw], self.scr2[:, 0:qw], ALU.add))

        if self.debug == "deep" and qi == 0:
            @blk.sync
            def _(e):
                S["v0"].wait(e)
                S["dO"].inc(e.dma_start(out=g["d_qhat"], in_=qhat[:]), 16)

        def u2s0(j, qw=None, xtb=xtb, u2=u2):
            r = (u2[:, 0:512], u2[:, 512:1024], xtb[:, 0:512])[j % 3]
            return r[:, 0:qw] if qw else r

        def u2s1(j, qw=None, xtb=xtb, u2=u2):
            r = (u2[:, 1024:1536], u2[:, 1536:2048], xtb[:, 512:1024])[j % 3]
            return r[:, 0:qw] if qw else r

        for hp in range(NCH):
            h0, h1 = 2 * hp, 2 * hp + 1
            t0b3 = S["t0"].n
            s0b3 = S["s0"].n
            t2b3 = S["t2"].n

            @blk.tensor
            def _(e, hp=hp, t0b3=t0b3, s0b3=s0b3):
                S["v0"].wait(e)
                e.matmul(bk(4, 0, 65, 0, qw), self.vsum[0:1, h0 * 65:h0 * 65 + 65],
                         self.onesrow[0:1, 0:qw], start=True, stop=False)
                e.matmul(bk(5, 0, 65, 0, qw), self.vsum[0:1, h1 * 65:h1 * 65 + 65],
                         self.onesrow[0:1, 0:qw], start=True, stop=False)
                for ks in range(32):
                    if ks >= 2:
                        S["s0"].wait(e, s0b3 + 2 * (ks - 1))
                    S["t0"].inc(e.matmul(bk(ks % 2, 0, 128, 0, qw),
                                         self.khat[0:64, hp * T + ks * 128:hp * T + ks * 128 + 128],
                                         qhat[0:64, hp * 512:hp * 512 + qw],
                                         start=True, stop=True))
                    S["t0"].inc(e.matmul(bk(2 + ks % 2, 0, 128, 0, qw),
                                         self.khat[64:128, hp * T + ks * 128:hp * T + ks * 128 + 128],
                                         qhat[64:128, hp * 512:hp * 512 + qw],
                                         start=True, stop=True))
                    if ks >= 2:
                        j = ks - 2
                        e.matmul(bk(4, 0, 65, 0, qw),
                                 self.vext[:, j * 1040 + h0 * 65:j * 1040 + h0 * 65 + 65],
                                 u2s0(j, qw),
                                 start=False, stop=False)
                        S["t2"].inc(e.matmul(bk(5, 0, 65, 0, qw),
                                             self.vext[:, j * 1040 + h1 * 65:j * 1040 + h1 * 65 + 65],
                                             u2s1(j, qw),
                                             start=False, stop=False))
                for j in (30, 31):
                    S["s0"].wait(e, s0b3 + 2 * (j + 1))
                    e.matmul(bk(4, 0, 65, 0, qw),
                             self.vext[:, j * 1040 + h0 * 65:j * 1040 + h0 * 65 + 65],
                             u2s0(j, qw),
                             start=False, stop=(j == 31))
                    S["t2"].inc(e.matmul(bk(5, 0, 65, 0, qw),
                                         self.vext[:, j * 1040 + h1 * 65:j * 1040 + h1 * 65 + 65],
                                         u2s1(j, qw),
                                         start=False, stop=(j == 31)))

            @blk.scalar
            def _(e, hp=hp, t0b3=t0b3, t2b3=t2b3):
                for ks in range(32):
                    S["t0"].wait(e, t0b3 + 2 * ks + 2)
                    if ks >= 3:
                        S["t2"].wait(e, t2b3 + ks - 2)
                    S["s0"].inc(e.activation(u2s0(ks, qw),
                                             bk(ks % 2, 0, 128, 0, qw), AF.Square,
                                             scale=self.misc[:, 204:205],
                                             bias=self.misc[:, 205:206]))
                    S["s0"].inc(e.activation(u2s1(ks, qw),
                                             bk(2 + ks % 2, 0, 128, 0, qw), AF.Square,
                                             scale=self.misc[:, 204:205],
                                             bias=self.misc[:, 205:206]))
                S["t2"].wait(e, t2b3 + 32)
                e.activation(dsl[64:65, 0:qw], bk(4, 64, 65, 0, qw), AF.Ln)
                e.activation(dsl[64:65, 512:512 + qw], bk(5, 64, 65, 0, qw), AF.Ln)
                e.activation(dsl[64:65, 0:qw], dsl[64:65, 0:qw], AF.Exp,
                             scale=self.misc[64:65, 203:204])
                S["s1"].inc(e.activation(dsl[64:65, 512:512 + qw], dsl[64:65, 512:512 + qw],
                                         AF.Exp, scale=self.misc[64:65, 203:204]))

            s1b3 = S["s1"].n
            t1b3 = S["t1"].n

            @blk.tensor
            def _(e, s1b3=s1b3):
                S["s1"].wait(e, s1b3)
                e.matmul(bk(6, 0, 64, 0, qw), self.onesf2[64:65, 0:64], dsl[64:65, 0:qw],
                         start=True, stop=True)
                S["t1"].inc(e.matmul(bk(7, 0, 64, 0, qw), self.onesf2[64:65, 0:64],
                                     dsl[64:65, 512:512 + qw], start=True, stop=True))

            @blk.vector
            def _(e, hp=hp, t1b3=t1b3):
                S["t1"].wait(e, t1b3 + 1)
                e.tensor_copy(self.scr1[0:64, 0:qw], bk(6, 0, 64, 0, qw))
                e.tensor_copy(self.scr2[0:64, 0:qw], bk(7, 0, 64, 0, qw))
                e.tensor_tensor(osb[0:64, hp * 512:hp * 512 + qw], bk(4, 0, 64, 0, qw),
                                self.scr1[0:64, 0:qw], ALU.mult)
                S["v0"].inc(e.tensor_tensor(qraw[0:64, 0:qw],
                                            bk(5, 0, 64, 0, qw), self.scr2[0:64, 0:qw],
                                            ALU.mult))

            v0b4 = S["v0"].n
            t1b4 = S["t1"].n

            @blk.tensor
            def _(e, v0b4=v0b4):
                S["v0"].wait(e, v0b4)
                S["t1"].inc(e.matmul(bk(6, 64, 128, 0, qw), self.IDE,
                                     qraw[0:64, 0:qw], start=True, stop=True,
                                     tile_position=(0, 64)))

            @blk.vector
            def _(e, hp=hp, t1b4=t1b4):
                S["t1"].wait(e, t1b4 + 1)
                S["v0"].inc(e.tensor_copy(osb[64:128, hp * 512:hp * 512 + qw],
                                          bk(6, 64, 128, 0, qw)))

        if self.debug == "deep" and qi == 0:
            @blk.sync
            def _(e):
                S["v0"].wait(e)
                S["dO"].inc(e.dma_start(out=g["d_osb"], in_=osb[:]), 16)

        t0b4 = S["t0"].n
        dWb2 = S["dW"].n
        v1b4 = S["v1"].n

        @blk.gpsimd
        def _(e):
            S["t0"].wait(e)
            for dc in range(NCH):
                if dc >= 2:
                    S["t0"].wait(e, t0b4 + dc - 1)
                S["dW"].inc(e.dma_start(out=wo2[:, (dc % 2) * 1024:(dc % 2) * 1024 + 1024],
                                        in_=g["woA"][dc]), 16)

        @blk.tensor
        def _(e):
            S["v0"].wait(e)
            for dc in range(NCH):
                S["dW"].wait(e, dWb2 + 16 * (dc + 1))
                if dc >= 2:
                    S["v1"].wait(e, v1b4 + dc - 1)
                for ec in range(NCH):
                    ins = e.matmul(bk(dc % 2, 0, 128, 0, qw),
                                   wo2[:, (dc % 2) * 1024 + ec * 128:(dc % 2) * 1024 + ec * 128 + 128],
                                   osb[:, ec * 512:ec * 512 + qw],
                                   start=(ec == 0), stop=(ec == NCH - 1))
                S["t0"].inc(ins)

        @blk.vector
        def _(e):
            if qi > 0:
                S["dO"].wait(e)
            for dc in range(NCH):
                S["t0"].wait(e, t0b4 + dc + 1)
                S["v1"].inc(e.tensor_tensor(xt[:, dc * qw:dc * qw + qw],
                                            xt[:, dc * qw:dc * qw + qw],
                                            bk(dc % 2, 0, 128, 0, qw), ALU.add))

        self.emit_ln(
            qw,
            src_f32=lambda cc: xt[:, cc * qw:cc * qw + qw],
            src_bf=lambda cc: xtb[:, cc * 512:cc * 512 + qw],
            dst=lambda cc: xt[:, cc * qw:cc * qw + qw],
            gamma=None, beta=None,
            pre_wait=lambda e: S["v1"].wait(e))

        @blk.sync
        def _(e):
            S["v1"].wait(e)
            dsts = ["y_dram", "x1_dram"] + (["dbg"] if self.debug in ("x1", "inner") else [])
            for name in dsts:
                S["dO"].inc(e.dma_start(
                    out=g[name].rearrange("(c p) t -> p c t", p=128)[:, :, q0:q0 + qw],
                    in_=xt[:, 0:NCH * qw].rearrange("p (c t) -> p c t", t=qw)), 16)

    def inner_layers(self):
        S, blk, g = self.S, self.blk, self.g
        nc = self.nc
        with ExitStack() as inner:
            def isb(name, shape, dt=F32):
                return inner.enter_context(nc.sbuf_tensor(name, shape, dt))
            self.pkAq = isb("pkAq", [128, LOC], BF16)
            self.pkBq = isb("pkBq", [128, LOC], BF16)
            self.pkAk = isb("pkAk", [128, LOC], BF16)
            self.pkBk = isb("pkBk", [128, LOC], BF16)
            self.yt = isb("yt", [128, 4 * 512])
            self.yt2 = isb("yt2", [128, 4 * 513])
            self.xsb = isb("xsb", [128, NCH * 512], BF16)
            self.osbi = isb("osbi", [128, NCH * 512], BF16)
            self.kroll = isb("kroll", [128, NCH * 768], BF16)
            self.vroll = isb("vroll", [128, 6 * 1040], BF16)
            self.u2i = isb("u2i", [128, 2 * 2048], BF16)
            self.rnb = isb("rnb", [16, 1024], BF16)
            self.lc = isb("lc", [128, 96])
            self.expd = isb("expd", [16, NCH * 128], BF16)
            self.IND_i = isb("IND_i", [128, 128], BF16)

            @blk.gpsimd
            def _(e):
                S["t0"].wait(e)
                S["v1"].wait(e)
                S["dW"].inc(e.dma_start(out=self.pkAq[:], in_=g["rpAq"]), 16)
                S["dW"].inc(e.dma_start(out=self.pkBq[:], in_=g["rpBq"]), 16)
                S["dW"].inc(e.dma_start(out=self.pkAk[:], in_=g["rpAk"]), 16)
                S["dW"].inc(e.dma_start(out=self.pkBk[:], in_=g["rpBk"]), 16)
                S["dW"].inc(e.dma_start(out=self.expd[:], in_=g["EXPD"]), 16)
                S["dW"].inc(e.dma_start(out=self.IND_i[:], in_=g["IND"]), 16)
                ap = self.vroll.rearrange("p (a c) -> p a c", c=65)
                S["g0"].inc(e.memset(ap[:, :, 64:65], 1.0))

            for li in range(NL):
                self._layer_consts(li)
                with nc.sbuf_tensor(f"wqk_i{li}", [128, 16 * 1024], BF16) as wqk_i, \
                     nc.sbuf_tensor(f"wv_i{li}", [128, NCH * 1024], BF16) as wv_i, \
                     nc.sbuf_tensor(f"wo_i{li}", [128, NCH * 1024], BF16) as wo_i, \
                     nc.sbuf_tensor(f"qkbf{li}", [128, 16 * 512], BF16) as qkbf, \
                     nc.sbuf_tensor(f"q2b{li}", [128, 2 * 512], BF16) as q2b:
                    self._inner_attn(li, wqk_i, wv_i, wo_i, qkbf, q2b)
                with nc.sbuf_tensor(f"xn_i{li}", [128, NCH * LOC], BF16) as xn_i, \
                     nc.sbuf_tensor(f"w1p{li}", [128, 4 * 1024], BF16) as w1p, \
                     nc.sbuf_tensor(f"glb{li}", [128, 512], BF16) as glb, \
                     nc.sbuf_tensor(f"hmb{li}", [128, 2 * 512], BF16) as hmb:
                    self._inner_ffn_b01(li, xn_i, w1p, glb, hmb)
                with nc.sbuf_tensor(f"w2r{li}", [128, NCH * NJ * 128], BF16) as w2r, \
                     nc.sbuf_tensor(f"hmt{li}", [128, NJ * 512], BF16) as hmt, \
                     nc.sbuf_tensor(f"yres{li}", [128, NCH * 512], F32) as yres:
                    self._inner_ffn_b2(li, w2r, hmt, yres)

    def _layer_consts(self, li):
        S, blk, g = self.S, self.blk, self.g

        @blk.sync
        def _(e):
            S["v0"].wait(e)
            S["v1"].wait(e)
            S["v3"].wait(e)
            S["t3"].wait(e)
            S["dH"].inc(e.dma_start(out=self.lc[:, 0:44], in_=g["b1c"][li]), 16)
            S["dH"].inc(e.dma_start(out=self.lc[:, 44:48], in_=g["qkv4"][li]), 16)
            S["dH"].inc(e.dma_start(out=self.lc[:, 48:56], in_=g["b2c"][li]), 16)
            S["dH"].inc(e.dma_start(out=self.lc[:, 56:64], in_=g["lng"][li]), 16)
            S["dH"].inc(e.dma_start(out=self.lc[:, 64:72], in_=g["lnb6"][li]), 16)
            S["dH"].inc(e.dma_start(out=self.misc[0:1, 0:128], in_=g["qkabs"][li]), 16)

        vmb = S["v3"].n

        @blk.vector
        def _(e):
            S["dH"].wait(e)
            S["v3"].inc(e.tensor_reduce(self.misc[0:1, 192:193], self.misc[0:1, 0:64],
                                        mybir.AxisListType.X, ALU.max))
            S["v3"].inc(e.tensor_reduce(self.misc[0:1, 193:194], self.misc[0:1, 64:128],
                                        mybir.AxisListType.X, ALU.max))

        t3b = S["t3"].n

        @blk.tensor
        def _(e):
            S["v3"].wait(e, vmb + 2)
            S["v0"].wait(e)
            S["v1"].wait(e)
            S["t3"].inc(e.matmul(self.bk(7, 0, 128, 0, 2), self.onesf2[0:1, :],
                                 self.misc[0:1, 192:194], start=True, stop=True))

        vcb = S["v3"].n

        @blk.vector
        def _(e):
            S["t3"].wait(e, t3b + 1)
            S["v3"].inc(e.tensor_tensor(self.misc[0:1, 194:195], self.misc[0:1, 192:193],
                                        self.bk(7, 0, 1, 1, 2), ALU.mult))

        s1b = S["s1"].n

        @blk.scalar
        def _(e):
            S["v3"].wait(e, vcb + 1)
            S["s1"].inc(e.activation(self.misc[0:1, 195:196], self.misc[0:1, 194:195],
                                     AF.Relu, scale=self.misc[0:1, 206:207],
                                     bias=self.misc[0:1, 209:210]))

        vdb = S["v3"].n

        @blk.vector
        def _(e):
            S["s1"].wait(e, s1b + 1)
            S["v3"].inc(e.tensor_single_scalar(self.misc[0:1, 196:197],
                                               self.misc[0:1, 195:196], -1.0, ALU.mult))

        t3c = S["t3"].n

        @blk.tensor
        def _(e):
            S["v3"].wait(e, vdb + 1)
            S["t3"].inc(e.matmul(self.bk(7, 0, 128, 0, 1), self.onesf2[0:1, :],
                                 self.misc[0:1, 196:197], start=True, stop=True))

        @blk.vector
        def _(e):
            S["t3"].wait(e, t3c + 1)
            S["v3"].inc(e.tensor_copy(self.misc[:, 207:208], self.bk(7, 0, 128, 0, 1)))

    def _inner_attn(self, li, wqk_i, wv_i, wo_i, qkbf, q2b):
        S, blk, bk, g = self.S, self.blk, self.bk, self.g
        itmp = q2b[:, 0:256]

        @blk.gpsimd
        def _(e):
            S["t0"].wait(e)
            S["v0"].wait(e)
            for oc in range(16):
                S["dW"].inc(e.dma_start(out=wqk_i[:, oc * 1024:(oc + 1) * 1024],
                                        in_=g["Wqk"][li][oc]), 16)
            for kc in range(NCH):
                S["dW"].inc(e.dma_start(out=wv_i[:, kc * 1024:(kc + 1) * 1024],
                                        in_=g["Wv6"][li][kc * 128:(kc + 1) * 128, :]), 16)
            for dc in range(NCH):
                S["dW"].inc(e.dma_start(out=wo_i[:, dc * 1024:(dc + 1) * 1024],
                                        in_=g["Wo6"][li][dc]), 16)
        wdma = S["dW"].n

        for qi, (t0_, qw) in enumerate(QT):
            self._ia_tile(li, qi, t0_, qw, wqk_i, wv_i, wo_i, qkbf, q2b, itmp, wdma)

    def _ia_tile(self, li, qi, t0_, qw, wqk_i, wv_i, wo_i, qkbf, q2b, itmp, wdma):
        S, blk, bk, g = self.S, self.blk, self.bk, self.g
        yt, yt2, xsb, osbi = self.yt, self.yt2, self.xsb, self.osbi
        kroll, vroll, u2i, rnb, lc = self.kroll, self.vroll, self.u2i, self.rnb, self.lc
        nw = qw // 256
        nts = qw // 128
        src_name = "y_dram"
        dst_name = "ya_dram"

        dHb = S["dH"].n

        @blk.sync
        def _(e):
            S["dO"].wait(e)
            S["v0"].wait(e)
            S["dH"].inc(e.dma_start(
                out=yt[:, 0:4 * qw].rearrange("p (c t) -> p c t", t=qw),
                in_=g[src_name].rearrange("(c p) t -> p c t", p=128)[:, 0:4, t0_:t0_ + qw]), 16)
            if qi == 0:
                S["dH"].inc(e.dma_start(
                    out=yt2.rearrange("p (c t) -> p c t", t=513)[:, :, 1:qw + 1],
                    in_=g[src_name].rearrange("(c p) t -> p c t", p=128)[:, 4:8, 0:qw]), 16)
            else:
                S["dH"].inc(e.dma_start(
                    out=yt2.rearrange("p (c t) -> p c t", t=513)[:, :, 0:qw + 1],
                    in_=g[src_name].rearrange("(c p) t -> p c t", p=128)[:, 4:8, t0_ - 1:t0_ + qw]), 16)

        vcb = S["v1"].n

        @blk.vector
        def _(e):
            S["dH"].wait(e, dHb + 32)
            if qi == 0:
                e.memset(yt2.rearrange("p (c t) -> p c t", t=513)[:, :, 0:1], 0.0)
            for cc in range(4):
                e.tensor_copy(xsb[:, cc * 512:cc * 512 + qw], yt[:, cc * qw:cc * qw + qw])
            for cc in range(4):
                ins = e.tensor_copy(xsb[:, (4 + cc) * 512:(4 + cc) * 512 + qw],
                                    yt2[:, cc * 513:cc * 513 + qw])
            S["v1"].inc(ins)

        t0b, t1b, t2b, s0b, v0b = S["t0"].n, S["t1"].n, S["t2"].n, S["s0"].n, S["v0"].n

        @blk.tensor
        def _(e):
            S["dW"].wait(e, wdma)
            S["v1"].wait(e, vcb + 1)
            S["s1"].wait(e)
            for oc in range(16):
                if oc >= 2:
                    S["s0"].wait(e, s0b + oc - 1)
                for kc in range(NCH):
                    ins = e.matmul(bk(oc % 2, 0, 128, 0, qw),
                                   wqk_i[:, oc * 1024 + kc * 128:oc * 1024 + kc * 128 + 128],
                                   xsb[:, kc * 512:kc * 512 + qw],
                                   start=(kc == 0), stop=(kc == NCH - 1))
                S["t0"].inc(ins)
                if oc >= 1:
                    S["s0"].wait(e, s0b + oc)
                    S["v0"].wait(e, v0b + max(0, oc - 2))
                    S["t1"].inc(e.matmul(bk(2 + (oc - 1) % 2, 0, 128, 0, qw), self.R_sb,
                                         qkbf[:, (oc - 1) * 512:(oc - 1) * 512 + qw],
                                         start=True, stop=True))
                    side = 4 if (oc - 1) < 8 else 5
                    S["t2"].inc(e.matmul(bk(side, 0, 16, 0, qw),
                                         self.IND_i[:, ((oc - 1) % 8) * 16:((oc - 1) % 8) * 16 + 16],
                                         q2b[:, ((oc - 1) % 2) * 512:((oc - 1) % 2) * 512 + qw],
                                         start=((oc - 1) % 8 == 0), stop=((oc - 1) % 8 == 7)))
            S["s0"].wait(e, s0b + 16)
            S["v0"].wait(e, v0b + 14)
            S["t1"].inc(e.matmul(bk(2 + 1, 0, 128, 0, qw), self.R_sb,
                                 qkbf[:, 15 * 512:15 * 512 + qw], start=True, stop=True))
            S["t2"].inc(e.matmul(bk(5, 0, 16, 0, qw),
                                 self.IND_i[:, 7 * 16:8 * 16],
                                 q2b[:, 1 * 512:1 * 512 + qw],
                                 start=False, stop=True))

        s1b = S["s1"].n

        @blk.scalar
        def _(e):
            for oc in range(16):
                S["t0"].wait(e, t0b + oc + 1)
                if oc >= 2:
                    S["t2"].wait(e, t2b + oc - 1)
                e.activation(qkbf[:, oc * 512:oc * 512 + qw], bk(oc % 2, 0, 128, 0, qw), AF.Copy)
                S["s0"].inc(e.activation(q2b[:, (oc % 2) * 512:(oc % 2) * 512 + qw],
                                         bk(oc % 2, 0, 128, 0, qw), AF.Square))
            S["t2"].wait(e, t2b + 16)
            e.activation(self.dsl[0:16, 0:qw], bk(4, 0, 16, 0, qw), AF.Ln,
                         bias=self.misc[0:16, 208:209])
            e.activation(self.rnb[0:16, 0:qw], self.dsl[0:16, 0:qw], AF.Exp,
                         scale=self.misc[0:16, 202:203])
            e.activation(self.dsl[0:16, 512:512 + qw], bk(5, 0, 16, 0, qw), AF.Ln,
                         bias=self.misc[0:16, 208:209])
            S["s1"].inc(e.activation(self.rnb[0:16, 512:512 + qw], self.dsl[0:16, 512:512 + qw],
                                     AF.Exp, scale=self.misc[0:16, 202:203]))

        t3b = S["t3"].n
        t0vb = S["t0"].n

        v3pre = S["v3"].n

        @blk.tensor
        def _(e):
            S["s1"].wait(e, s1b + 1)
            for oc in range(16):
                if oc >= 2:
                    S["v3"].wait(e, v3pre + oc - 1)
                side_off = 0 if oc < 8 else 512
                S["t3"].inc(e.matmul(bk(6 + oc % 2, 0, 128, 0, qw),
                                     self.expd[:, (oc % 8) * 128:(oc % 8) * 128 + 128],
                                     self.rnb[0:16, side_off:side_off + qw],
                                     start=True, stop=True))
            i = 0
            for ts in range(nts):
                for et in range(2):
                    if i >= 2:
                        S["v1"].wait(e, vcb + 1 + i - 1)
                    for kc in range(NCH):
                        ins = e.matmul(bk(i % 2, 0, 128, 0, 512),
                                       xsb[:, kc * 512 + ts * 128:kc * 512 + ts * 128 + 128],
                                       wv_i[:, kc * 1024 + et * 512:kc * 1024 + et * 512 + 512],
                                       start=(kc == 0), stop=(kc == NCH - 1))
                    S["t0"].inc(ins)
                    i += 1

        v0b2 = S["v0"].n
        v3b2 = S["v3"].n

        @blk.vector
        def _(e):
            for oc in range(16):
                S["t1"].wait(e, t1b + oc + 1)
                qsd = lc[:, 44:45] if oc < 8 else lc[:, 46:47]
                qsr = lc[:, 45:46] if oc < 8 else lc[:, 47:48]
                pkA = self.pkAq if oc < 8 else self.pkAk
                pkB = self.pkBq if oc < 8 else self.pkBk
                qb = qkbf[:, oc * 512:oc * 512 + qw]
                e.scalar_tensor_tensor(self.scr1[:, 0:qw], qb, qsd,
                                       pkA[:, t0_:t0_ + qw], ALU.mult, ALU.mult)
                e.scalar_tensor_tensor(self.scr2[:, 0:qw], bk(2 + (oc % 2), 0, 128, 0, qw),
                                       qsr, pkB[:, t0_:t0_ + qw], ALU.mult, ALU.mult)
                S["v0"].inc(e.tensor_tensor(qb, self.scr1[:, 0:qw], self.scr2[:, 0:qw], ALU.add))
            for oc in range(16):
                S["t3"].wait(e, t3b + oc + 1)
                qb = qkbf[:, oc * 512:oc * 512 + qw]
                if oc < 8:
                    dst = qb
                else:
                    dst = kroll[:, (oc - 8) * 768 + 256:(oc - 8) * 768 + 256 + qw]
                S["v3"].inc(e.tensor_tensor(dst, qb, bk(6 + oc % 2, 0, 128, 0, qw), ALU.mult))
            i = 0
            for ts in range(nts):
                for et in range(2):
                    S["t0"].wait(e, t0vb + i + 1)
                    dst = vroll.rearrange("p (a c) -> p a c", c=65)[
                        :, (2 + ts) * 16 + et * 8:(2 + ts) * 16 + et * 8 + 8, 0:64]
                    S["v1"].inc(e.tensor_copy(
                        dst, bk(i % 2, 0, 128, 0, 512).rearrange("p (h c) -> p h c", c=64)))
                    i += 1
        v1ev = S["v1"].n

        self._ia_windows(li, qi, qw, qkbf, q2b, v1ev)

        t0b4 = S["t0"].n
        v1b4 = S["v1"].n

        @blk.tensor
        def _(e):
            S["v0"].wait(e)
            for dc in range(NCH):
                if dc >= 2:
                    S["v1"].wait(e, v1b4 + dc - 1)
                for ec in range(NCH):
                    ins = e.matmul(bk(dc % 2, 0, 128, 0, qw),
                                   wo_i[:, dc * 1024 + ec * 128:dc * 1024 + ec * 128 + 128],
                                   osbi[:, ec * 512:ec * 512 + qw],
                                   start=(ec == 0), stop=(ec == NCH - 1))
                S["t0"].inc(ins)

        @blk.vector
        def _(e):
            for dc in range(NCH):
                S["t0"].wait(e, t0b4 + dc + 1)
                if dc < 4:
                    dst = yt[:, dc * qw:dc * qw + qw]
                else:
                    dst = yt2[:, (dc - 4) * 513 + 1:(dc - 4) * 513 + 1 + qw]
                S["v1"].inc(e.tensor_tensor(dst, dst, bk(dc % 2, 0, 128, 0, qw), ALU.add))
            if qi < len(QT) - 1:
                e.tensor_copy(
                    kroll.rearrange("p (c t) -> p c t", t=768)[:, :, 0:256],
                    kroll.rearrange("p (c t) -> p c t", t=768)[:, :, qw:qw + 256])
                S["v0"].inc(e.tensor_copy(
                    vroll[:, 0:2 * 1040],
                    vroll[:, (nts - 2 + 2) * 1040:(nts + 2) * 1040]))

        @blk.sync
        def _(e):
            S["v1"].wait(e)
            dns = [dst_name] + ([f"d_ya{li}"] if self.debug == "inner" else [])
            for dn in dns:
                S["dO"].inc(e.dma_start(
                    out=g[dn].rearrange("(c p) t -> p c t", p=128)[:, 0:4, t0_:t0_ + qw],
                    in_=yt[:, 0:4 * qw].rearrange("p (c t) -> p c t", t=qw)), 16)
                S["dO"].inc(e.dma_start(
                    out=g[dn].rearrange("(c p) t -> p c t", p=128)[:, 4:8, t0_:t0_ + qw],
                    in_=yt2.rearrange("p (c t) -> p c t", t=513)[:, :, 1:qw + 1]), 16)

    def _ia_windows(self, li, qi, qw, qkbf, q2b, v1ev):
        S, blk, bk = self.S, self.blk, self.bk
        kroll, vroll, u2i, osbi = self.kroll, self.vroll, self.u2i, self.osbi
        nw = qw // 256
        nch_t = nw * NCH
        AVB = (6, 0)
        BCB = (7, 1)
        itmp = (q2b[:, 0:256], q2b[:, 512:768])
        t1t0 = S["t1"].n
        v0b0 = S["v0"].n
        v3b0 = S["v3"].n
        m = {k: {} for k in ("t0", "s2e", "g0", "t1a", "s2r", "t3", "v0d", "t1i", "v0c")}

        def chain(c):
            w, hp = c // NCH, c % NCH
            ks_lo = 2 if (qi == 0 and w == 0) else 0
            return w, hp, ks_lo, c % 2

        def S1(c):
            w, hp, ks_lo, p = chain(c)

            @blk.tensor
            def _(e, c=c, w=w, hp=hp, ks_lo=ks_lo):
                if c == 0:
                    S["v0"].wait(e, v0b0)
                    S["v3"].wait(e, v3b0)
                else:
                    S["s2"].wait(e, m["s2e"][c - 1])
                qcol = hp * 512 + w * 256
                for ks in range(ks_lo, 4):
                    kc0 = hp * 768 + w * 256 + ks * 128
                    S["t0"].inc(e.matmul(
                        bk(2 + (ks // 2), 0, 128, (ks % 2) * 256, (ks % 2) * 256 + 256),
                        kroll[0:64, kc0:kc0 + 128],
                        qkbf[0:64, qcol:qcol + 256], start=True, stop=True))
                    S["t0"].inc(e.matmul(
                        bk(4 + (ks // 2), 0, 128, (ks % 2) * 256, (ks % 2) * 256 + 256),
                        kroll[64:128, kc0:kc0 + 128],
                        qkbf[64:128, qcol:qcol + 256], start=True, stop=True))
            m["t0"][c] = S["t0"].n

        def S2(c):
            w, hp, ks_lo, p = chain(c)
            bu = p * 2048

            @blk.scalar
            def _(e, c=c, ks_lo=ks_lo, bu=bu):
                S["t0"].wait(e, m["t0"][c])
                S["t1"].wait(e, m["t1a"][c - 2] if c >= 2 else t1t0)
                if ks_lo == 0:
                    e.activation(u2i[:, bu:bu + 1024], self.psA[:, 1024:2048], AF.Exp,
                                 scale=self.misc[:, 206:207], bias=self.misc[:, 207:208])
                    S["s2"].inc(e.activation(u2i[:, bu + 1024:bu + 2048], self.psB[:, 0:1024],
                                             AF.Exp, scale=self.misc[:, 206:207],
                                             bias=self.misc[:, 207:208]))
                else:
                    e.activation(u2i[:, bu + 512:bu + 1024], self.psA[:, 1536:2048], AF.Exp,
                                 scale=self.misc[:, 206:207], bias=self.misc[:, 207:208])
                    S["s2"].inc(e.activation(u2i[:, bu + 1536:bu + 2048], self.psB[:, 512:1024],
                                             AF.Exp, scale=self.misc[:, 206:207],
                                             bias=self.misc[:, 207:208]))
            m["s2e"][c] = S["s2"].n

        def S3(c):
            w, hp, ks_lo, p = chain(c)
            bu = p * 2048

            @blk.gpsimd
            def _(e, c=c, ks_lo=ks_lo, bu=bu):
                S["s2"].wait(e, m["s2e"][c])
                for hh in range(2):
                    base = bu + hh * 1024
                    if ks_lo == 0:
                        S["g0"].inc(e.affine_select(
                            u2i[:, base:base + 512], u2i[:, base:base + 512],
                            [[128, 2], [-1, 256]], ALU.is_ge, 0.0,
                            base=0, channel_multiplier=1))
                    S["g0"].inc(e.affine_select(
                        u2i[:, base + 512:base + 1024], u2i[:, base + 512:base + 1024],
                        [[-128, 2], [1, 256]], ALU.is_ge, 0.0,
                        base=0, channel_multiplier=-1))
            m["g0"][c] = S["g0"].n

        def S4(c):
            w, hp, ks_lo, p = chain(c)
            bu = p * 2048
            ab = AVB[p]

            @blk.tensor
            def _(e, c=c, w=w, hp=hp, ks_lo=ks_lo, bu=bu, ab=ab):
                S["g0"].wait(e, m["g0"][c])
                if c >= 2:
                    S["v0"].wait(e, m["v0d"][c - 2])
                else:
                    S["v1"].wait(e, v1ev)
                first = True
                for ks in range(ks_lo, 4):
                    tchunk = 2 * w + ks
                    e.matmul(bk(ab, 0, 65, 0, 256),
                             vroll[:, tchunk * 1040 + (2 * hp) * 65:tchunk * 1040 + (2 * hp) * 65 + 65],
                             u2i[:, bu + ks * 256:bu + ks * 256 + 256],
                             start=first, stop=False)
                    first = False
                    ins = e.matmul(bk(ab, 0, 65, 256, 512),
                                   vroll[:, tchunk * 1040 + (2 * hp + 1) * 65:tchunk * 1040 + (2 * hp + 1) * 65 + 65],
                                   u2i[:, bu + 1024 + ks * 256:bu + 1024 + ks * 256 + 256],
                                   start=False, stop=(ks == 3))
                S["t1"].inc(ins)
            m["t1a"][c] = S["t1"].n

        def S5(c):
            w, hp, ks_lo, p = chain(c)
            ab = AVB[p]
            ds = self.dsl[64:65, p * 512:p * 512 + 512]

            @blk.scalar
            def _(e, c=c, ab=ab, ds=ds):
                S["t1"].wait(e, m["t1a"][c])
                e.activation(ds, bk(ab, 64, 65, 0, 512), AF.Ln)
                S["s2"].inc(e.activation(ds, ds, AF.Exp, scale=self.misc[64:65, 203:204]))
            m["s2r"][c] = S["s2"].n

        def S6(c):
            w, hp, ks_lo, p = chain(c)
            bb = BCB[p]
            ds = self.dsl[64:65, p * 512:p * 512 + 512]

            @blk.tensor
            def _(e, c=c, bb=bb, ds=ds):
                S["s2"].wait(e, m["s2r"][c])
                if c >= 2:
                    S["v0"].wait(e, m["v0c"][c - 2])
                elif bb == 1:
                    S["v1"].wait(e, v1ev)
                S["t3"].inc(e.matmul(bk(bb, 0, 64, 0, 512), self.onesf2[64:65, 0:64],
                                     ds, start=True, stop=True))
            m["t3"][c] = S["t3"].n

        def S7(c):
            w, hp, ks_lo, p = chain(c)
            ab, bb = AVB[p], BCB[p]
            qcol = hp * 512 + w * 256

            @blk.vector
            def _(e, c=c, ab=ab, bb=bb, qcol=qcol, p=p):
                S["t3"].wait(e, m["t3"][c])
                e.tensor_copy(self.scr1[0:64, 0:512], bk(bb, 0, 64, 0, 512))
                e.tensor_tensor(osbi[0:64, qcol:qcol + 256], bk(ab, 0, 64, 0, 256),
                                self.scr1[0:64, 0:256], ALU.mult)
                S["v0"].inc(e.tensor_tensor(itmp[p][0:64, 0:256], bk(ab, 0, 64, 256, 512),
                                            self.scr1[0:64, 256:512], ALU.mult))
            m["v0d"][c] = S["v0"].n

        def S8(c):
            w, hp, ks_lo, p = chain(c)
            bb = BCB[p]

            @blk.tensor
            def _(e, c=c, bb=bb, p=p):
                S["v0"].wait(e, m["v0d"][c])
                S["t1"].inc(e.matmul(bk(bb, 64, 128, 0, 256), self.IDE, itmp[p][0:64, 0:256],
                                     start=True, stop=True, tile_position=(0, 64)))
            m["t1i"][c] = S["t1"].n

        def S9(c):
            w, hp, ks_lo, p = chain(c)
            bb = BCB[p]
            qcol = hp * 512 + w * 256

            @blk.vector
            def _(e, c=c, bb=bb, qcol=qcol):
                S["t1"].wait(e, m["t1i"][c])
                S["v0"].inc(e.tensor_copy(osbi[64:128, qcol:qcol + 256],
                                          bk(bb, 64, 128, 0, 256)))
            m["v0c"][c] = S["v0"].n

        for s in range(nch_t + 3):
            if s < nch_t:
                S1(s); S2(s); S3(s)
            if 1 <= s <= nch_t:
                S4(s - 1); S5(s - 1)
            if 2 <= s <= nch_t + 1:
                S6(s - 2); S7(s - 2)
            if 3 <= s <= nch_t + 2:
                S8(s - 3); S9(s - 3)

    def _inner_ffn_b01(self, li, xn_i, w1p, glb, hmb):
        S, blk, bk, g = self.S, self.blk, self.bk, self.g
        yt, yt2, xsb, lc = self.yt, self.yt2, self.xsb, self.lc

        v0pre = S["v0"].n
        v1pre = S["v1"].n
        xnm = []

        for qi, (t0_, qw) in enumerate(QT):
            dHb = S["dH"].n

            @blk.sync
            def _(e, qi=qi, t0_=t0_, qw=qw):
                if qi == 0:
                    S["t0"].wait(e)
                    S["v0"].wait(e)
                S["v1"].wait(e)
                S["dH"].inc(e.dma_start(
                    out=yt[:, 0:4 * qw].rearrange("p (c t) -> p c t", t=qw),
                    in_=g["ya_dram"].rearrange("(c p) t -> p c t", p=128)[:, 0:4, t0_:t0_ + qw]), 16)
                if qi == 0:
                    S["dH"].inc(e.dma_start(
                        out=yt2.rearrange("p (c t) -> p c t", t=513)[:, :, 1:qw + 1],
                        in_=g["ya_dram"].rearrange("(c p) t -> p c t", p=128)[:, 4:8, 0:qw]), 16)
                else:
                    S["dH"].inc(e.dma_start(
                        out=yt2.rearrange("p (c t) -> p c t", t=513)[:, :, 0:qw + 1],
                        in_=g["ya_dram"].rearrange("(c p) t -> p c t", p=128)[:, 4:8, t0_ - 1:t0_ + qw]), 16)

            if qi == 0:
                @blk.vector
                def _(e, dHb=dHb):
                    S["dH"].wait(e, dHb + 32)
                    S["v2"].inc(e.memset(yt2.rearrange("p (c t) -> p c t", t=513)[:, :, 0:1], 0.0))

            def src_f32(cc, qw=qw):
                if cc < 4:
                    return yt[:, cc * qw:cc * qw + qw]
                return yt2[:, (cc - 4) * 513:(cc - 4) * 513 + qw]

            xnm.append(self.emit_ln(
                qw,
                src_f32=src_f32,
                src_bf=lambda cc: xsb[:, cc * 512:cc * 512 + qw],
                dst=lambda cc, t0_=t0_, qw=qw: xn_i[:, cc * LOC + t0_:cc * LOC + t0_ + qw],
                gamma=lambda cc: lc[:, 56 + cc:57 + cc],
                beta=lambda cc: lc[:, 64 + cc:65 + cc],
                pre_wait=lambda e, dHb=dHb: S["dH"].wait(e, dHb + 32)))

        wdb = S["dW"].n
        t0pre = S["t0"].n
        ntile = len(QT)

        @blk.gpsimd
        def _(e):
            S["t0"].wait(e, t0pre)
            S["v0"].wait(e, v0pre)
            S["v1"].wait(e, v1pre)
            for j in range(NJ):
                if j >= 2:
                    S["t0"].wait(e, t0pre + (j - 1) * ntile)
                S["dW"].inc(e.dma_start(out=w1p[:, (j % 2) * 2048:(j % 2) * 2048 + 1024],
                                        in_=g["W1t"][li][j]), 16)
                S["dW"].inc(e.dma_start(out=w1p[:, (j % 2) * 2048 + 1024:(j % 2) * 2048 + 2048],
                                        in_=g["W1t"][li][NJ + j]), 16)

        t0b = S["t0"].n
        s2b = S["s2"].n
        v0b = S["v0"].n

        @blk.tensor
        def _(e):
            i = 0
            for j in range(NJ):
                S["dW"].wait(e, wdb + 32 * (j + 1))
                for qi, (t0_, qw) in enumerate(QT):
                    if j == 0:
                        S["v1"].wait(e, xnm[qi])
                    if i >= 2:
                        S["s2"].wait(e, s2b + i - 1)
                        S["v0"].wait(e, v0b + max(0, i - 1))
                    pb = 2 * (i % 2)
                    for kc in range(NCH):
                        e.matmul(bk(pb, 0, 128, 0, qw),
                                 w1p[:, (j % 2) * 2048 + kc * 128:(j % 2) * 2048 + kc * 128 + 128],
                                 xn_i[:, kc * LOC + t0_:kc * LOC + t0_ + qw],
                                 start=(kc == 0), stop=(kc == NCH - 1))
                        ins = e.matmul(bk(pb + 1, 0, 128, 0, qw),
                                       w1p[:, (j % 2) * 2048 + 1024 + kc * 128:(j % 2) * 2048 + 1024 + kc * 128 + 128],
                                       xn_i[:, kc * LOC + t0_:kc * LOC + t0_ + qw],
                                       start=(kc == 0), stop=(kc == NCH - 1))
                    S["t0"].inc(ins)
                    i += 1

        @blk.scalar
        def _(e):
            i = 0
            for j in range(NJ):
                for qi, (t0_, qw) in enumerate(QT):
                    S["t0"].wait(e, t0b + i + 1)
                    if i >= 1:
                        S["v0"].wait(e, v0b + i - 1)
                    S["s2"].inc(e.activation(glb[:, 0:qw], bk(2 * (i % 2) + 1, 0, 128, 0, qw),
                                             AF.Gelu, bias=lc[:, 22 + j:23 + j]))
                    i += 1

        dOb = S["dO"].n

        @blk.vector
        def _(e):
            i = 0
            for j in range(NJ):
                for qi, (t0_, qw) in enumerate(QT):
                    S["s2"].wait(e, s2b + i + 1)
                    if i >= 2:
                        S["dO"].wait(e, dOb + 16 * (i - 1))
                    S["v0"].inc(e.scalar_tensor_tensor(
                        hmb[:, (i % 2) * 512:(i % 2) * 512 + qw],
                        bk(2 * (i % 2), 0, 128, 0, qw), lc[:, j:j + 1], glb[:, 0:qw],
                        ALU.add, ALU.mult))
                    i += 1

        @blk.sync
        def _(e):
            i = 0
            for j in range(NJ):
                for qi, (t0_, qw) in enumerate(QT):
                    S["v0"].wait(e, v0b + i + 1)
                    S["dO"].inc(e.dma_start(
                        out=g["hm_dram"][:, j * LOC + t0_:j * LOC + t0_ + qw],
                        in_=hmb[:, (i % 2) * 512:(i % 2) * 512 + qw]), 16)
                    i += 1

    def _inner_ffn_b2(self, li, w2r, hmt, yres):
        S, blk, bk, g = self.S, self.blk, self.bk, self.g
        lc = self.lc

        wdb = S["dW"].n

        @blk.gpsimd
        def _(e):
            S["t0"].wait(e)
            S["v0"].wait(e)
            for oc in range(NCH):
                S["dW"].inc(e.dma_start(out=w2r[:, oc * NJ * 128:(oc + 1) * NJ * 128],
                                        in_=g["W2t"][li][oc]), 16)

        for qi, (t0_, qw) in enumerate(QT):
            dHb = S["dH"].n

            @blk.sync
            def _(e, qi=qi, t0_=t0_, qw=qw, dHb=dHb):
                S["v1"].wait(e)
                S["dH"].inc(e.dma_start(
                    out=hmt[:, 0:NJ * qw].rearrange("p (j t) -> p j t", t=qw),
                    in_=g["hm_dram"].rearrange("p (j t) -> p j t", t=LOC)[:, :, t0_:t0_ + qw]), 16)
                S["dH"].inc(e.dma_start(
                    out=yres[:, 0:NCH * qw].rearrange("p (c t) -> p c t", t=qw),
                    in_=g["ya_dram"].rearrange("(c p) t -> p c t", p=128)[:, :, t0_:t0_ + qw]), 16)

            t0b = S["t0"].n
            v1b = S["v1"].n

            @blk.tensor
            def _(e, qi=qi, t0_=t0_, qw=qw, dHb=dHb, t0b=t0b, v1b=v1b):
                S["dW"].wait(e, wdb + 16 * NCH)
                S["dH"].wait(e, dHb + 32)
                for oc in range(NCH):
                    if oc >= 2:
                        S["v1"].wait(e, v1b + oc - 1)
                    for j in range(NJ):
                        ins = e.matmul(bk(oc % 2, 0, 128, 0, qw),
                                       w2r[:, oc * NJ * 128 + j * 128:oc * NJ * 128 + j * 128 + 128],
                                       hmt[:, j * qw:j * qw + qw],
                                       start=(j == 0), stop=(j == NJ - 1))
                    S["t0"].inc(ins)

            @blk.vector
            def _(e, qi=qi, t0_=t0_, qw=qw, t0b=t0b):
                for oc in range(NCH):
                    S["t0"].wait(e, t0b + oc + 1)
                    S["v1"].inc(e.scalar_tensor_tensor(
                        yres[:, oc * qw:oc * qw + qw], bk(oc % 2, 0, 128, 0, qw),
                        lc[:, 48 + oc:49 + oc], yres[:, oc * qw:oc * qw + qw],
                        ALU.add, ALU.add))

            @blk.sync
            def _(e, qi=qi, t0_=t0_, qw=qw):
                S["v1"].wait(e)
                dns = ["y_dram"] + ([f"d_y{li}"] if self.debug == "inner" else [])
                for dn in dns:
                    S["dO"].inc(e.dma_start(
                        out=g[dn].rearrange("(c p) t -> p c t", p=128)[:, :, t0_:t0_ + qw],
                        in_=yres[:, 0:NCH * qw].rearrange("p (c t) -> p c t", t=qw)), 16)

    def p9_final(self):
        S, blk, g = self.S, self.blk, self.g
        nc = self.nc
        with nc.sbuf_tensor("f1", [128, NCH * 512], F32) as f1, \
             nc.sbuf_tensor("f2", [128, NCH * 512], F32) as f2, \
             nc.sbuf_tensor("fbc", [128, NCH * 512], BF16) as fb:
            for qi, (t0_, qw) in enumerate(QT):
                dHb = S["dH"].n

                @blk.sync
                def _(e, qi=qi, t0_=t0_, qw=qw):
                    if qi == 0:
                        S["t0"].wait(e)
                        S["v0"].wait(e)
                    S["v1"].wait(e)
                    S["dO"].wait(e)
                    S["dH"].inc(e.dma_start(
                        out=f1[:, 0:NCH * qw].rearrange("p (c t) -> p c t", t=qw),
                        in_=g["y_dram"].rearrange("(c p) t -> p c t", p=128)[:, :, t0_:t0_ + qw]), 16)
                    S["dH"].inc(e.dma_start(
                        out=f2[:, 0:NCH * qw].rearrange("p (c t) -> p c t", t=qw),
                        in_=g["x1_dram"].rearrange("(c p) t -> p c t", p=128)[:, :, t0_:t0_ + qw]), 16)

                vsb = S["v2"].n

                @blk.vector
                def _(e, dHb=dHb, qw=qw):
                    S["dH"].wait(e, dHb + 32)
                    for cc in range(NCH):
                        ins = e.tensor_tensor(f1[:, cc * qw:cc * qw + qw],
                                              f1[:, cc * qw:cc * qw + qw],
                                              f2[:, cc * qw:cc * qw + qw], ALU.add)
                    S["v2"].inc(ins)

                self.emit_ln(
                    qw,
                    src_f32=lambda cc, qw=qw: f1[:, cc * qw:cc * qw + qw],
                    src_bf=lambda cc: fb[:, cc * 512:cc * 512 + qw],
                    dst=lambda cc, qw=qw: f1[:, cc * qw:cc * qw + qw],
                    gamma=None, beta=None,
                    pre_wait=lambda e, vsb=vsb: S["v2"].wait(e, vsb + 1))

                @blk.sync
                def _(e, qi=qi, t0_=t0_, qw=qw):
                    S["v1"].wait(e)
                    S["dO"].inc(e.dma_start(
                        out=g["out"].rearrange("(c p) t -> p c t", p=128)[:, :, t0_:t0_ + qw],
                        in_=f1[:, 0:NCH * qw].rearrange("p (c t) -> p c t", t=qw)), 16)




def prepare_inputs(ins):
    x = ins["L0_attn_x"]
    w_q = ins["L0_attn_w_q"]; w_kv = ins["L0_attn_w_kv"]; w_o = ins["L0_attn_w_o"]
    base = {}
    base["wqA"] = tile_weight(w_q[:, PERM_A])
    base["wkA"] = tile_weight(w_kv[:, :D][:, PERM_A])
    base["wvA"] = np.ascontiguousarray(w_kv[:, D:])
    base["woA"] = tile_weight(w_o)
    lnA = np.zeros((128, 16), np.float32)
    lnA[:, 0:8] = ins["L0_attn_ln_w"].reshape(8, 128).T
    lnA[:, 8:16] = ins["L0_attn_ln_b"].reshape(8, 128).T
    base["lnA"] = lnA
    base["theta"] = np.asarray(ins["L0_attn_theta"], np.float32).reshape(1, 1)
    base["fb"] = FB.reshape(1, 32)
    base["t_k"] = np.arange(T, dtype=np.float32).reshape(1, T)
    base["Rm"] = rot_lhsT()
    base["IdeM"] = np.eye(64, dtype=np.float32)
    base["IND"] = head_indicator()
    base["EXPD"] = head_expander()
    Wqkv = ins["L0_ffn_Wqkv"]
    base["Wqk"] = np.stack([tile_weight(Wqkv[li][:, :2 * D]) for li in range(NL)])
    base["Wv6"] = np.ascontiguousarray(Wqkv[:, :, 2 * D:])
    base["Wo6"] = np.stack([tile_weight(ins["L0_ffn_Wo"][li]) for li in range(NL)])
    W1 = ins["L0_ffn_W1"]; W2 = ins["L0_ffn_W2"]
    w1t = []
    for li in range(NL):
        a = tile_weight(W1[li][:, :2730], mc=NJ)
        gt = tile_weight(W1[li][:, 2730:], mc=NJ)
        w1t.append(np.concatenate([a, gt], 0))
    base["W1t"] = np.stack(w1t)
    base["W2t"] = np.stack([tile_weight(W2[li], kc=NJ) for li in range(NL)])
    b1 = ins["L0_ffn_b1"]
    b1c = np.zeros((NL, 128, 44), np.float32)
    for li in range(NL):
        a = np.zeros(NJ * 128, np.float32); a[:2730] = b1[li][:2730]
        gt = np.zeros(NJ * 128, np.float32); gt[:2730] = b1[li][2730:]
        b1c[li, :, 0:22] = a.reshape(NJ, 128).T
        b1c[li, :, 22:44] = gt.reshape(NJ, 128).T
    base["b1c"] = b1c
    base["b2c"] = np.stack([ins["L0_ffn_b2"][li].reshape(8, 128).T for li in range(NL)])
    base["lng"] = np.stack([ins["L0_ffn_ln_g"][li].reshape(8, 128).T for li in range(NL)])
    base["lnb6"] = np.stack([ins["L0_ffn_ln_b"][li].reshape(8, 128).T for li in range(NL)])
    qs, ks = ins["L0_ffn_q_scale"], ins["L0_ffn_k_scale"]
    qkv4 = np.zeros((NL, 128, 4), np.float32)
    qkabs = np.zeros((NL, 1, 128), np.float32)
    for li in range(NL):
        qkv4[li, :, 0] = np.tile(qs[li], 2)
        qkv4[li, :, 1] = np.tile(np.concatenate([qs[li][32:], qs[li][:32]]), 2)
        qkv4[li, :, 2] = np.tile(ks[li], 2)
        qkv4[li, :, 3] = np.tile(np.concatenate([ks[li][32:], ks[li][:32]]), 2)
        qkabs[li, 0, 0:64] = np.abs(qs[li])
        qkabs[li, 0, 64:128] = np.abs(ks[li])
    base["qkv4"] = qkv4
    base["qkabs"] = qkabs

    maps = []
    for c in range(8):
        b, par = c // 2, c % 2
        toff = (0, 256)[par]
        m = dict(base)
        xb = np.ascontiguousarray(x[b].T.astype(np.float32))
        m["x_fm"] = xb
        m["x_loc"] = np.ascontiguousarray(xb[:, toff:toff + LOC])
        m["t_q"] = np.arange(toff, toff + LOC, dtype=np.float32).reshape(1, LOC)
        pos = np.arange(toff, toff + LOC)
        m["rpAq"] = np.ascontiguousarray(np.tile((ICOS[pos] * ISCALE[pos]).T, (2, 1)).astype(ml_dtypes.bfloat16))
        m["rpBq"] = np.ascontiguousarray(np.tile((ISIN[pos] * ISCALE[pos]).T, (2, 1)).astype(ml_dtypes.bfloat16))
        m["rpAk"] = np.ascontiguousarray(np.tile((ICOS[pos] / ISCALE[pos]).T, (2, 1)).astype(ml_dtypes.bfloat16))
        m["rpBk"] = np.ascontiguousarray(np.tile((ISIN[pos] / ISCALE[pos]).T, (2, 1)).astype(ml_dtypes.bfloat16))
        maps.append(m)
    return maps


def kernel(**inputs):
    nc, g = Builder(debug=None).build()
    maps = prepare_inputs(inputs)
    res = run_bass_kernel_spmd(nc, maps, core_ids=list(range(8)))
    out = np.empty((4, T, D), np.float32)
    for c in range(8):
        b, par = c // 2, c % 2
        o = res.results[c]["out"]
        if par == 0:
            out[b, 0:2048] = o[:, 0:2048].T
        else:
            out[b, 2048:4096] = o[:, 2048 - 256:4096 - 256].T
    return out


# version 07
# speedup vs baseline: 1.0125x; 1.0125x over previous
import numpy as np
import ml_dtypes
from contextlib import ExitStack

import concourse.bass as bass
import concourse.mybir as mybir
from concourse.bass_utils import run_bass_kernel_spmd

F32 = mybir.dt.float32
BF16 = mybir.dt.bfloat16
AF = mybir.ActivationFunctionType
ALU = mybir.AluOpType

D, T, H, HD = 1024, 4096, 16, 64
NCH = 8
EPS = np.float32(1e-5)
LOC = 3840
NL = 6
NJ = 22
QT = [(i * 512, 512) for i in range(7)] + [(3584, 256)]
NW = LOC // 256

_2PI = 6.283185307179586


def _split12(x):
    f = np.float32(x)
    m, e = np.frexp(np.float64(f))
    return np.float32(np.ldexp(np.round(np.ldexp(m, 12)), int(e) - 12))


D1c = _split12(_2PI)
D2c = _split12(_2PI - np.float64(D1c))
D3c = np.float32(_2PI - np.float64(D1c) - np.float64(D2c))
INV2PI = np.float32(1.0 / _2PI)
MAGIC = np.float32(3 << 22)
PI = np.float32(np.pi)
HALFPI = np.float32(np.pi / 2)


def freqs_base_f32():
    hd2 = HD // 2
    top = np.float32(2595.0) * np.log10(np.float32(1.0 + 4000.0 / 200.0))
    lin = np.linspace(np.float32(0.0), np.float32(top), hd2, dtype=np.float32)
    mel = np.power(np.float32(10.0), lin / np.float32(2595.0), dtype=np.float32) - np.float32(1.0)
    return (np.float32(200.0) * mel / np.float32(1000.0)).astype(np.float32)


def inner_rotary_tables():
    WINDOW = 256
    half = np.arange(0, HD, 2, dtype=np.float32)
    inv_freq = (np.float32(1.0) / np.power(np.float32(10000.0), half / np.float32(HD), dtype=np.float32)).astype(np.float32)
    t = np.arange(T, dtype=np.float32)
    freqs = t[:, None] * inv_freq[None, :]
    freqs = np.concatenate([freqs, freqs], -1).astype(np.float32)
    base_scale = ((half + np.float32(0.4 * HD)) / np.float32(1.4 * HD)).astype(np.float32)
    power = ((t - np.float32(T // 2)) / np.float32(WINDOW // 2)).astype(np.float32)
    scale = np.power(base_scale[None, :], power[:, None], dtype=np.float32)
    scale = np.concatenate([scale, scale], -1).astype(np.float32)
    return freqs, scale


def rot_lhsT():
    M = np.zeros((128, 128), np.float32)
    for m in range(128):
        if m % 64 < 32:
            M[m, m + 32] = -1.0
        else:
            M[m, m - 32] = 1.0
    return np.ascontiguousarray(M.T)


def head_indicator():
    ind = np.zeros((128, 128), np.float32)
    for cc in range(NCH):
        for p in range(128):
            ind[p, cc * 16 + 2 * cc + (p // 64)] = 1.0
    return ind


def head_expander():
    e = np.zeros((16, NCH * 128), np.float32)
    for cc in range(NCH):
        for c in range(128):
            e[2 * cc + (c // 64), cc * 128 + c] = 1.0
    return e


def tile_weight(w, kc=None, mc=None):
    K, M = w.shape
    kc = (K + 127) // 128 if kc is None else kc
    mc = (M + 127) // 128 if mc is None else mc
    wp = np.zeros((kc * 128, mc * 128), np.float32)
    wp[:K, :M] = w
    return np.ascontiguousarray(
        wp.reshape(kc, 128, mc, 128).transpose(2, 1, 0, 3).reshape(mc, 128, kc * 128))


FB = freqs_base_f32()
IFREQS, ISCALE = inner_rotary_tables()
ICOS = np.cos(IFREQS).astype(np.float32)
ISIN = np.sin(IFREQS).astype(np.float32)
permA = np.concatenate([np.arange(0, HD, 2), np.arange(1, HD, 2)])
PERM_A = np.concatenate([h * HD + permA for h in range(H)])


class Sem:
    def __init__(self, nc, ctx, name):
        self.h = ctx.enter_context(nc.semaphore(name))
        self.n = 0

    def inc(self, instr, k=1):
        instr.then_inc(self.h, k)
        self.n += k
        return instr

    def wait(self, eng, at=None):
        eng.wait_ge(self.h, self.n if at is None else at)


class Builder:
    def __init__(self, debug=None):
        self.debug = debug
        self.nc = bass.Bass("TRN2", target_bir_lowering=False, debug=False,
                            detect_race_conditions=False)
        self.g = {}

    def declare(self):
        nc, g = self.nc, self.g

        def din(name, shape, dt=F32):
            g[name] = nc.dram_tensor(name, shape, dt, kind="ExternalInput").ap()

        def dint(name, shape, dt=F32):
            g[name] = nc.dram_tensor(name, shape, dt, kind="Internal").ap()

        din("x_fm", [D, T])
        din("x_loc", [D, LOC])
        din("wqA", [NCH, 128, D])
        din("wkA", [NCH, 128, D])
        din("wvA", [D, D])
        din("woA", [NCH, 128, D])
        din("lnA", [128, 16])
        din("theta", [1, 1])
        din("fb", [1, 32])
        din("t_q", [1, LOC])
        din("t_k", [1, T])
        din("Rm", [128, 128])
        din("IdeM", [64, 64])
        din("IND", [128, 128])
        din("EXPD", [16, NCH * 128])
        din("Wqk", [NL, 16, 128, D])
        din("Wv6", [NL, D, D])
        din("Wo6", [NL, NCH, 128, D])
        din("W1t", [NL, 44, 128, D])
        din("W2t", [NL, NCH, 128, NJ * 128])
        din("b1c", [NL, 128, 44])
        din("b2c", [NL, 128, 8])
        din("lng", [NL, 128, 8])
        din("lnb6", [NL, 128, 8])
        din("qkv4", [NL, 128, 4])
        din("qkabs", [NL, 1, 128])
        din("rpAq", [128, LOC], BF16)
        din("rpBq", [128, LOC], BF16)
        din("rpAk", [128, LOC], BF16)
        din("rpBk", [128, LOC], BF16)

        g["out"] = nc.dram_tensor("out", [D, LOC], F32, kind="ExternalOutput").ap()
        if self.debug in ("x1", "deep", "inner"):
            g["dbg"] = nc.dram_tensor("dbg", [D, LOC], F32, kind="ExternalOutput").ap()
        if self.debug == "inner":
            for li in range(NL):
                g[f"d_ya{li}"] = nc.dram_tensor(f"d_ya{li}", [D, LOC], F32, kind="ExternalOutput").ap()
                g[f"d_y{li}"] = nc.dram_tensor(f"d_y{li}", [D, LOC], F32, kind="ExternalOutput").ap()
        if self.debug == "deep":
            g["d_tabCk"] = nc.dram_tensor("d_tabCk", [128, T], BF16, kind="ExternalOutput").ap()
            g["d_tabSk"] = nc.dram_tensor("d_tabSk", [128, T], BF16, kind="ExternalOutput").ap()
            g["d_khat"] = nc.dram_tensor("d_khat", [128, NCH * T], BF16, kind="ExternalOutput").ap()
            g["d_vext"] = nc.dram_tensor("d_vext", [128, 32 * 1040], BF16, kind="ExternalOutput").ap()
            g["d_vsum"] = nc.dram_tensor("d_vsum", [1, 1056], BF16, kind="ExternalOutput").ap()
            g["d_qhat"] = nc.dram_tensor("d_qhat", [128, NCH * 512], BF16, kind="ExternalOutput").ap()
            g["d_osb"] = nc.dram_tensor("d_osb", [128, NCH * 512], BF16, kind="ExternalOutput").ap()

        dint("h_dram", [128, NCH * T], BF16)
        dint("y_dram", [D, LOC])
        dint("ya_dram", [D, LOC])
        dint("x1_dram", [D, LOC])
        dint("hm_dram", [128, NJ * LOC], BF16)

    def bk(self, i, p0=0, p1=128, c0=0, c1=512):
        t = self.psA if i < 4 else self.psB
        return t[p0:p1, (i % 4) * 512 + c0:(i % 4) * 512 + c1]

    def emit_ln(self, qw, src_f32, src_bf, dst, gamma=None, beta=None, pre_wait=None):
        S, blk, bk = self.S, self.blk, self.bk
        rows = self.rows
        t1b, v1b, s1b, v2b = S["t1"].n, S["v1"].n, S["s1"].n, S["v2"].n

        @blk.vector
        def _(e):
            if pre_wait is not None:
                pre_wait(e)
            for cc in range(NCH):
                if cc >= 2:
                    S["t1"].wait(e, t1b + cc - 1)
                if src_f32 is not None:
                    e.tensor_copy(src_bf(cc), src_f32(cc))
                sq = self.scr1 if cc % 2 == 0 else self.scr2
                S["v1"].inc(e.tensor_tensor(sq[:, 0:qw], src_bf(cc), src_bf(cc), ALU.mult))
            S["t1"].wait(e, t1b + 8)
            e.tensor_single_scalar(rows[0:1, 0:qw], bk(6, 0, 1, 0, qw), float(np.float32(1.0 / D)), ALU.mult)
            e.tensor_single_scalar(rows[0:1, 512:512 + qw], bk(5, 0, 1, 0, qw), float(np.float32(1.0 / D)), ALU.mult)
            e.tensor_tensor(rows[0:1, 1024:1024 + qw], rows[0:1, 0:qw], rows[0:1, 0:qw], ALU.mult)
            S["v1"].inc(e.tensor_tensor(rows[0:1, 512:512 + qw], rows[0:1, 512:512 + qw],
                                        rows[0:1, 1024:1024 + qw], ALU.subtract))

        @blk.scalar
        def _(e):
            S["v1"].wait(e, v1b + 9)
            e.activation(rows[0:1, 512:512 + qw], rows[0:1, 512:512 + qw], AF.Ln,
                         bias=self.misc[0:1, 201:202])
            S["s1"].inc(e.activation(rows[0:1, 512:512 + qw], rows[0:1, 512:512 + qw],
                                     AF.Exp, scale=self.misc[0:1, 202:203]))

        @blk.vector
        def _(e):
            S["s1"].wait(e, s1b + 1)
            S["v2"].inc(e.tensor_tensor(rows[0:1, 1024:1024 + qw], rows[0:1, 0:qw],
                                        rows[0:1, 512:512 + qw], ALU.mult))

        @blk.tensor
        def _(e):
            S["s0"].wait(e)
            S["s2"].wait(e)
            S["v0"].wait(e)
            for cc in range(NCH):
                S["v1"].wait(e, v1b + cc + 1)
                sq = self.scr1 if cc % 2 == 0 else self.scr2
                e.matmul(bk(6, 0, 1, 0, qw), self.ones_col, src_bf(cc),
                         start=(cc == 0), stop=(cc == NCH - 1))
                S["t1"].inc(e.matmul(bk(5, 0, 1, 0, qw), self.onesc_f, sq[:, 0:qw],
                                     start=(cc == 0), stop=(cc == NCH - 1)))
            S["v1"].wait(e, v1b + 9)
            S["v2"].wait(e, v2b + 1)
            e.matmul(bk(7, 0, 128, 0, qw), self.onesf2[0:1, :], rows[0:1, 512:512 + qw],
                     start=True, stop=True)
            S["t1"].inc(e.matmul(bk(5, 0, 128, 0, qw), self.onesf2[0:1, :],
                                 rows[0:1, 1024:1024 + qw], start=True, stop=True))

        va = S["v1"].n

        @blk.vector
        def _(e):
            S["t1"].wait(e, t1b + 9)
            for cc in range(NCH):
                src = src_f32(cc) if src_f32 is not None else src_bf(cc)
                e.tensor_tensor(self.scr1[:, 0:qw], src, bk(7, 0, 128, 0, qw), ALU.mult)
                if gamma is None:
                    ins = e.tensor_tensor(dst(cc), self.scr1[:, 0:qw],
                                          bk(5, 0, 128, 0, qw), ALU.subtract)
                else:
                    e.tensor_tensor(self.scr1[:, 0:qw], self.scr1[:, 0:qw],
                                    bk(5, 0, 128, 0, qw), ALU.subtract)
                    ins = e.tensor_scalar(dst(cc), self.scr1[:, 0:qw],
                                          gamma(cc), beta(cc), ALU.mult, ALU.add)
            S["v1"].inc(ins)

        return va + 1

    def angle_tables(self, trow, ncols, dstC, dstS, pre=None):
        S, blk, bk = self.S, self.blk, self.bk
        nt = (ncols + 511) // 512
        tb, vb, sb_ = S["t0"].n, S["v0"].n, S["s0"].n
        v1b = S["v1"].n

        @blk.tensor
        def _(e):
            if pre is not None:
                pre(e)
            S["v0"].wait(e, vb)
            S["v1"].wait(e, v1b)
            for i in range(nt):
                t0, w = i * 512, min(512, ncols - i * 512)
                if i > 0:
                    S["v0"].wait(e, vb + i)
                S["t0"].inc(e.matmul(bk(7, 0, 128, 0, w), self.fdup,
                                     trow[0:1, t0:t0 + w], start=True, stop=True))

        @blk.vector
        def _(e):
            for i in range(nt):
                t0, w = i * 512, min(512, ncols - i * 512)
                S["t0"].wait(e, tb + i + 1)
                if i > 0:
                    S["s0"].wait(e, sb_ + 2 * i)
                a = bk(7, 0, 128, 0, w)
                r = bk(6, 0, 128, 0, w)
                ab = bk(5, 0, 128, 0, w)
                nn = self.scr1[:, 0:w]
                nh = self.scr2[:, 0:w]
                e.tensor_scalar(nn, a, float(INV2PI), float(MAGIC), ALU.mult, ALU.add)
                e.tensor_single_scalar(nn, nn, float(MAGIC), ALU.subtract)
                e.tensor_scalar(nh, nn, float(np.float32(1.0 / 64.0)), float(MAGIC),
                                ALU.mult, ALU.add)
                e.tensor_scalar(nh, nh, float(MAGIC), 64.0, ALU.subtract, ALU.mult)
                e.scalar_tensor_tensor(r, nh, float(-D1c), a, ALU.mult, ALU.add)
                e.scalar_tensor_tensor(r, nh, float(-D2c), r, ALU.mult, ALU.add)
                e.tensor_tensor(nh, nn, nh, ALU.subtract)
                e.scalar_tensor_tensor(r, nh, float(-D1c), r, ALU.mult, ALU.add)
                e.scalar_tensor_tensor(r, nh, float(-D2c), r, ALU.mult, ALU.add)
                e.scalar_tensor_tensor(r, nn, float(-D3c), r, ALU.mult, ALU.add)
                e.tensor_single_scalar(r, r, float(PI), ALU.min)
                e.tensor_single_scalar(r, r, float(-PI), ALU.max)
                e.tensor_single_scalar(nn, r, -1.0, ALU.mult)
                S["v0"].inc(e.tensor_max(ab, nn, r))

        @blk.scalar
        def _(e):
            for i in range(nt):
                t0, w = i * 512, min(512, ncols - i * 512)
                S["v0"].wait(e, vb + i + 1)
                S["s0"].inc(e.activation(dstS[:, t0:t0 + w], bk(6, 0, 128, 0, w), AF.Sin))
                S["s0"].inc(e.activation(dstC[:, t0:t0 + w], bk(5, 0, 128, 0, w),
                                         AF.Sin, scale=self.misc[:, 203:204],
                                         bias=self.misc[:, 200:201]))

    def build(self):
        self.declare()
        nc, g = self.nc, self.g
        with ExitStack() as ctx:
            self.ctx = ctx
            S = self.S = {n: Sem(nc, ctx, n) for n in
                          ["dW", "dH", "dO", "t0", "t1", "t2", "t3", "s0", "s1",
                           "s2", "v0", "v1", "v2", "v3", "g0"]}
            self.psA = ctx.enter_context(nc.psum_tensor("psA", [128, 2048], F32))
            self.psB = ctx.enter_context(nc.psum_tensor("psB", [128, 2048], F32))

            def sbuf(name, shape, dt=F32):
                return ctx.enter_context(nc.sbuf_tensor(name, shape, dt))
            self.sbuf = sbuf
            self.blk = ctx.enter_context(nc.Block())
            blk = self.blk

            cst = sbuf("cst", [128, 200], BF16)
            self.R_sb = cst[:, 0:128]
            self.ones_col = cst[:, 128:129]
            self.IDE = cst[0:64, 132:196]
            self.onesrow = sbuf("onesrow", [1, 512], BF16)
            self.onesf2 = sbuf("onesf2", [128, 128], F32)
            self.onesc_f = self.onesf2[:, 0:1]
            self.lnA_sb = sbuf("lnA_sb", [128, 16])
            self.misc = sbuf("misc", [128, 256])
            self.rows = sbuf("rows", [1, 1536])
            self.scr1 = sbuf("scr1", [128, 512])
            self.scr2 = sbuf("scr2", [128, 512])
            self.tq1 = sbuf("tq1", [1, 512])
            self.dsl = sbuf("dsl", [128, 1024])

            @blk.sync
            def _(e):
                S["dH"].inc(e.dma_start(out=self.lnA_sb[:], in_=g["lnA"]), 16)
                S["dH"].inc(e.dma_start(out=self.misc[0:1, 0:1], in_=g["theta"]), 16)
                S["dH"].inc(e.dma_start(out=self.misc[0:1, 32:64], in_=g["fb"]), 16)


            @blk.gpsimd
            def _(e):
                S["dW"].inc(e.dma_start(out=self.R_sb, in_=g["Rm"]), 16)
                S["dW"].inc(e.dma_start(out=self.IDE, in_=g["IdeM"]), 16)
                e.memset(self.ones_col, 1.0)
                e.memset(self.onesrow[:], 1.0)
                e.memset(self.misc[:, 200:201], float(HALFPI))
                e.memset(self.misc[:, 201:202], float(EPS))
                e.memset(self.misc[:, 202:203], -0.5)
                e.memset(self.misc[:, 203:204], -1.0)
                e.memset(self.misc[:, 204:205], 0.125)
                e.memset(self.misc[:, 205:206], 1.0)
                e.memset(self.misc[:, 206:207], 8.0)
                e.memset(self.misc[:, 208:209], 1e-24)
                e.memset(self.misc[:, 209:210], -40.0)
                S["g0"].inc(e.memset(self.onesf2[:], 1.0))

            @blk.vector
            def _(e):
                S["dH"].wait(e)
                S["g0"].wait(e)
                for r in range(4):
                    e.tensor_scalar(self.misc[0:1, 64 + 32 * r:96 + 32 * r],
                                    self.misc[0:1, 32:64],
                                    self.misc[0:1, 0:1],
                                    float(np.float32(1.0 / 220.0)),
                                    ALU.mult, ALU.mult)
                S["v0"].inc(e.engine_nop() if hasattr(e, "engine_nop") else e.nop())
            self.fdup = self.misc[0:1, 64:192]

            stageA_stack = ExitStack()
            self.khat = stageA_stack.enter_context(nc.sbuf_tensor("khat", [128, NCH * T], BF16))
            self.vext = stageA_stack.enter_context(nc.sbuf_tensor("vext", [128, 32 * 1040], BF16))
            self.vsum = stageA_stack.enter_context(nc.sbuf_tensor("vsum", [1, 1056], BF16))
            with nc.sbuf_tensor("tabCk", [128, T], BF16) as tabCk, \
                 nc.sbuf_tensor("tabSk", [128, T], BF16) as tabSk:
                with nc.sbuf_tensor("tkrow", [1, T], F32) as tkrow:
                    dHb0 = S["dH"].n

                    @blk.sync
                    def _(e):
                        S["dH"].inc(e.dma_start(out=tkrow[0:1, :], in_=g["t_k"]), 16)

                    self.angle_tables(tkrow[0:1, :], T, tabCk, tabSk,
                                      pre=lambda e: S["dH"].wait(e, dHb0 + 16))
                if self.debug == "deep":
                    @blk.sync
                    def _(e):
                        S["s0"].wait(e)
                        S["dO"].inc(e.dma_start(out=g["d_tabCk"], in_=tabCk[:]), 16)
                        S["dO"].inc(e.dma_start(out=g["d_tabSk"], in_=tabSk[:]), 16)
                self.p1_hbuild()
                self.p2_khat(tabCk, tabSk)
            self.p3_vext()
            if self.debug == "deep":
                @blk.sync
                def _(e):
                    S["v0"].wait(e)
                    S["dO"].inc(e.dma_start(out=g["d_khat"], in_=self.khat[:]), 16)
                    S["dO"].inc(e.dma_start(out=g["d_vext"], in_=self.vext[:, 0:32 * 1040]), 16)
                    S["dO"].inc(e.dma_start(out=g["d_vsum"], in_=self.vsum[:]), 16)
            self.p4_attention()
            stageA_stack.close()

            if self.debug != "x1":
                self.inner_layers()
                self.p9_final()

            @blk.sync
            def _(e):
                S["dO"].wait(e)
        return nc, g

    def p1_hbuild(self):
        S, blk, g = self.S, self.blk, self.g
        with self.nc.sbuf_tensor("xb2", [128, 2 * NCH * 512], BF16) as xb2, \
             self.nc.sbuf_tensor("hloc", [128, 2 * NCH * 512], BF16) as hloc:
            dWb = S["dW"].n
            applies = []

            @blk.gpsimd
            def _(e):
                S["t0"].wait(e)
                for tt in range(8):
                    S["dW"].inc(e.dma_start(
                        out=xb2[:, (tt % 2) * NCH * 512:(tt % 2) * NCH * 512 + NCH * 512].rearrange("p (c t) -> p c t", t=512),
                        in_=g["x_fm"].rearrange("(c p) t -> p c t", p=128)[:, :, tt * 512:(tt + 1) * 512]), 16)


            for tt in range(8):
                base = (tt % 2) * NCH * 512

                @blk.gpsimd
                def _(e, tt=tt, base=base):
                    if tt == 0:
                        S["t0"].wait(e)
                    if tt >= 2:
                        S["v1"].wait(e, applies[tt - 2])
                    S["dW"].inc(e.dma_start(
                        out=xb2[:, base:base + NCH * 512].rearrange("p (c t) -> p c t", t=512),
                        in_=g["x_fm"].rearrange("(c p) t -> p c t", p=128)[:, :, tt * 512:(tt + 1) * 512]), 16)
                dma_mark = S["dW"].n

                def pre_wait(e, dma_mark=dma_mark, tt=tt):
                    S["dW"].wait(e, dma_mark)

                va = self.emit_ln(
                    512,
                    src_f32=None,
                    src_bf=lambda cc, base=base: xb2[:, base + cc * 512:base + cc * 512 + 512],
                    dst=lambda cc, base=base: hloc[:, base + cc * 512:base + cc * 512 + 512],
                    gamma=lambda cc: self.lnA_sb[:, cc:cc + 1],
                    beta=lambda cc: self.lnA_sb[:, 8 + cc:9 + cc],
                    pre_wait=pre_wait)
                applies.append(va)

                @blk.sync
                def _(e, tt=tt, base=base, va=va):
                    S["v1"].wait(e, va)
                    S["dH"].inc(e.dma_start(
                        out=g["h_dram"].rearrange("p (c t) -> p c t", t=T)[:, :, tt * 512:(tt + 1) * 512],
                        in_=hloc[:, base:base + NCH * 512].rearrange("p (c t) -> p c t", t=512)), 16)

            @blk.sync
            def _(e):
                S["dH"].wait(e)

    def p2_khat(self, tabCk, tabSk):
        S, blk, bk, g = self.S, self.blk, self.bk, self.g
        with self.nc.sbuf_tensor("wk_sb", [128, NCH * 1024], BF16) as wk_sb, \
             self.nc.sbuf_tensor("hti", [128, 2 * NCH * 512], BF16) as hti, \
             self.nc.sbuf_tensor("kraw", [128, 2 * 512], BF16) as kraw, \
             self.nc.sbuf_tensor("t1sb", [128, 2 * 512], F32) as t1sb:
            dWb, dHb = S["dW"].n, S["dH"].n
            t0b, t1b, s0b, v0b = S["t0"].n, S["t1"].n, S["s0"].n, S["v0"].n

            @blk.gpsimd
            def _(e):
                S["v1"].wait(e)
                S["dH"].wait(e)
                for oc in range(NCH):
                    S["dW"].inc(e.dma_start(out=wk_sb[:, oc * 1024:(oc + 1) * 1024],
                                            in_=g["wkA"][oc]), 16)

            @blk.sync
            def _(e):
                S["v1"].wait(e)
                for tt in range(8):
                    if tt >= 2:
                        S["t0"].wait(e, t0b + (tt - 1) * 8)
                    S["dH"].inc(e.dma_start(
                        out=hti[:, (tt % 2) * NCH * 512:(tt % 2) * NCH * 512 + NCH * 512].rearrange("p (c t) -> p c t", t=512),
                        in_=g["h_dram"].rearrange("p (c t) -> p c t", t=T)[:, :, tt * 512:(tt + 1) * 512]), 16)

            @blk.tensor
            def _(e):
                S["dW"].wait(e, dWb + 16 * NCH)
                S["s0"].wait(e)
                i = 0
                for tt in range(8):
                    S["dH"].wait(e, dHb + 16 * (tt + 1))
                    hb = (tt % 2) * NCH * 512
                    for oc in range(NCH):
                        if i >= 2:
                            S["s0"].wait(e, s0b + i - 1)
                        for kc in range(NCH):
                            ins = e.matmul(bk(i % 2),
                                           wk_sb[:, oc * 1024 + kc * 128:oc * 1024 + kc * 128 + 128],
                                           hti[:, hb + kc * 512:hb + kc * 512 + 512],
                                           start=(kc == 0), stop=(kc == NCH - 1))
                        S["t0"].inc(ins)
                        if i >= 1:
                            S["s0"].wait(e, s0b + i)
                            S["v0"].wait(e, v0b + max(0, i - 1))
                            S["t1"].inc(e.matmul(bk(2 + (i - 1) % 2), self.R_sb,
                                                 kraw[:, ((i - 1) % 2) * 512:((i - 1) % 2) * 512 + 512],
                                                 start=True, stop=True))
                        i += 1
                S["s0"].wait(e, s0b + 64)
                S["v0"].wait(e, v0b + 62)
                S["t1"].inc(e.matmul(bk(3), self.R_sb, kraw[:, 512:1024],
                                     start=True, stop=True))

            @blk.scalar
            def _(e):
                for i in range(64):
                    S["t0"].wait(e, t0b + i + 1)
                    if i >= 2:
                        S["v0"].wait(e, v0b + i - 1)
                    S["s0"].inc(e.activation(kraw[:, (i % 2) * 512:(i % 2) * 512 + 512],
                                             bk(i % 2), AF.Copy))

            @blk.vector
            def _(e):
                for i in range(64):
                    tt, oc = i // 8, i % 8
                    S["t1"].wait(e, t1b + i + 1)
                    kb = kraw[:, (i % 2) * 512:(i % 2) * 512 + 512]
                    dst = self.khat[:, oc * T + tt * 512:oc * T + tt * 512 + 512]
                    tb = t1sb[:, (i % 2) * 512:(i % 2) * 512 + 512]
                    e.tensor_tensor(tb, kb, tabCk[:, tt * 512:tt * 512 + 512], ALU.mult)
                    e.tensor_tensor(self.scr2[:], bk(2 + i % 2), tabSk[:, tt * 512:tt * 512 + 512], ALU.mult)
                    S["v0"].inc(e.tensor_tensor(dst, tb, self.scr2[:], ALU.add))

    def p3_vext(self):
        S, blk, bk, g = self.S, self.blk, self.bk, self.g
        with self.nc.sbuf_tensor("wv_sb", [128, NCH * 1024], BF16) as wv_sb, \
             self.nc.sbuf_tensor("hti3", [128, 2 * NCH * 512], BF16) as hti:
            dWb, dHb = S["dW"].n, S["dH"].n
            t0b, v0b, g0b = S["t0"].n, S["v0"].n, S["g0"].n

            @blk.gpsimd
            def _(e):
                S["v0"].wait(e)
                S["t0"].wait(e)
                ap = self.vext.rearrange("p (a c) -> p a c", c=65)
                S["g0"].inc(e.memset(ap[:, :, 64:65], 1.0))
                for kc in range(NCH):
                    S["dW"].inc(e.dma_start(out=wv_sb[:, kc * 1024:(kc + 1) * 1024],
                                            in_=g["wvA"][kc * 128:(kc + 1) * 128, :]), 16)

            @blk.sync
            def _(e):
                S["t0"].wait(e)
                S["v0"].wait(e)
                for tt in range(8):
                    if tt >= 2:
                        S["t0"].wait(e, t0b + (tt - 1) * 8)
                    S["dH"].inc(e.dma_start(
                        out=hti[:, (tt % 2) * NCH * 512:(tt % 2) * NCH * 512 + NCH * 512].rearrange("p (c t) -> p c t", t=512),
                        in_=g["h_dram"].rearrange("p (c t) -> p c t", t=T)[:, :, tt * 512:(tt + 1) * 512]), 16)

            @blk.tensor
            def _(e):
                S["dW"].wait(e, dWb + 16 * NCH)
                S["g0"].wait(e, g0b + 1)
                i = 0
                for tt in range(8):
                    S["dH"].wait(e, dHb + 16 * (tt + 1))
                    hb = (tt % 2) * NCH * 512
                    for ts in range(4):
                        for et in range(2):
                            if i >= 3:
                                S["v0"].wait(e, v0b + i - 2)
                            for kc in range(NCH):
                                ins = e.matmul(
                                    bk(4 + i % 3),
                                    hti[:, hb + kc * 512 + ts * 128:hb + kc * 512 + ts * 128 + 128],
                                    wv_sb[:, kc * 1024 + et * 512:kc * 1024 + et * 512 + 512],
                                    start=(kc == 0), stop=(kc == NCH - 1))
                            S["t0"].inc(ins)
                            i += 1

            @blk.vector
            def _(e):
                i = 0
                for tt in range(8):
                    for ts in range(4):
                        for et in range(2):
                            S["t0"].wait(e, t0b + i + 1)
                            gts = tt * 4 + ts
                            dst = self.vext.rearrange("p (a c) -> p a c", c=65)[
                                :, gts * 16 + et * 8:gts * 16 + et * 8 + 8, 0:64]
                            S["v0"].inc(e.tensor_copy(
                                dst, bk(4 + i % 3).rearrange("p (h c) -> p h c", c=64)))
                            i += 1

            t0b2 = S["t0"].n

            @blk.tensor
            def _(e):
                S["v0"].wait(e)
                for ts in range(32):
                    for pi, (c0, w) in enumerate([(0, 512), (512, 512), (1024, 16)]):
                        ins = e.matmul(bk(pi, 0, 1, 0, w), self.ones_col,
                                       self.vext[:, ts * 1040 + c0:ts * 1040 + c0 + w],
                                       start=(ts == 0), stop=(ts == 31))
                S["t0"].inc(ins)

            @blk.vector
            def _(e):
                S["t0"].wait(e, t0b2 + 1)
                e.tensor_copy(self.vsum[0:1, 0:512], bk(0, 0, 1, 0, 512))
                e.tensor_copy(self.vsum[0:1, 512:1024], bk(1, 0, 1, 0, 512))
                S["v0"].inc(e.tensor_copy(self.vsum[0:1, 1024:1040], bk(2, 0, 1, 0, 16)))

    def p4_attention(self):
        nc = self.nc
        with nc.sbuf_tensor("xt", [128, NCH * 512], F32) as xt, \
             nc.sbuf_tensor("xtb", [128, NCH * 512], BF16) as xtb, \
             nc.sbuf_tensor("hq", [128, NCH * 512], BF16) as hq, \
             nc.sbuf_tensor("qhat", [128, NCH * 512], BF16) as qhat, \
             nc.sbuf_tensor("qraw", [128, 2 * 512], BF16) as qraw, \
             nc.sbuf_tensor("wq2", [128, 2 * 1024], BF16) as wq2, \
             nc.sbuf_tensor("wo2", [128, 2 * 1024], BF16) as wo2, \
             nc.sbuf_tensor("u2", [128, 4 * 512], BF16) as u2, \
             nc.sbuf_tensor("qtC", [128, 512], BF16) as qtC, \
             nc.sbuf_tensor("qtS", [128, 512], BF16) as qtS:
            osb = hq
            dsl = self.dsl
            for qi, (q0, qw) in enumerate(QT):
                self.p4_qtile(qi, q0, qw, xt, xtb, hq, qhat, qraw, wq2, wo2,
                              u2, osb, dsl, qtC, qtS)

    def p4_qtile(self, qi, q0, qw, xt, xtb, hq, qhat, qraw, wq2, wo2,
                 u2, osb, dsl, qtC, qtS):
        S, blk, bk, g = self.S, self.blk, self.bk, self.g
        dHb = S["dH"].n

        @blk.sync
        def _(e):
            if qi == 0:
                S["t0"].wait(e)
                S["v0"].wait(e)
            if qi > 0:
                S["dO"].wait(e)
            S["dH"].inc(e.dma_start(out=self.tq1[0:1, 0:qw],
                                    in_=g["t_q"][0:1, q0:q0 + qw]), 16)
            S["dH"].inc(e.dma_start(
                out=xt[:, 0:NCH * qw].rearrange("p (c t) -> p c t", t=qw),
                in_=g["x_loc"].rearrange("(c p) t -> p c t", p=128)[:, :, q0:q0 + qw]), 16)

        self.angle_tables(self.tq1[0:1, 0:qw], qw, qtC, qtS,
                          pre=lambda e: S["dH"].wait(e, dHb + 16))

        def pre_wait(e):
            S["dH"].wait(e, dHb + 32)

        hva = self.emit_ln(
            qw,
            src_f32=lambda cc: xt[:, cc * qw:cc * qw + qw],
            src_bf=lambda cc: xtb[:, cc * 512:cc * 512 + qw],
            dst=lambda cc: hq[:, cc * 512:cc * 512 + qw],
            gamma=lambda cc: self.lnA_sb[:, cc:cc + 1],
            beta=lambda cc: self.lnA_sb[:, 8 + cc:9 + cc],
            pre_wait=pre_wait)

        dWb = S["dW"].n
        t0b, t1b2, s0b, v0b = S["t0"].n, S["t1"].n, S["s0"].n, S["v0"].n

        @blk.gpsimd
        def _(e):
            S["t0"].wait(e)
            for oc in range(NCH):
                if oc >= 2:
                    S["t0"].wait(e, t0b + oc - 1)
                S["dW"].inc(e.dma_start(out=wq2[:, (oc % 2) * 1024:(oc % 2) * 1024 + 1024],
                                        in_=g["wqA"][oc]), 16)

        @blk.tensor
        def _(e):
            S["v1"].wait(e, hva)
            for oc in range(NCH):
                S["dW"].wait(e, dWb + 16 * (oc + 1))
                if oc >= 2:
                    S["s0"].wait(e, s0b + oc - 1)
                for kc in range(NCH):
                    ins = e.matmul(bk(oc % 2, 0, 128, 0, qw),
                                   wq2[:, (oc % 2) * 1024 + kc * 128:(oc % 2) * 1024 + kc * 128 + 128],
                                   hq[:, kc * 512:kc * 512 + qw],
                                   start=(kc == 0), stop=(kc == NCH - 1))
                S["t0"].inc(ins)
                if oc >= 1:
                    S["s0"].wait(e, s0b + oc)
                    S["v0"].wait(e, v0b + max(0, oc - 2))
                    S["t1"].inc(e.matmul(bk(2 + (oc - 1) % 2, 0, 128, 0, qw), self.R_sb,
                                         qraw[:, ((oc - 1) % 2) * 512:((oc - 1) % 2) * 512 + qw],
                                         start=True, stop=True))
            S["s0"].wait(e, s0b + 8)
            S["v0"].wait(e, v0b + 6)
            S["t1"].inc(e.matmul(bk(3, 0, 128, 0, qw), self.R_sb,
                                 qraw[:, 512:512 + qw], start=True, stop=True))

        @blk.scalar
        def _(e):
            for oc in range(NCH):
                S["t0"].wait(e, t0b + oc + 1)
                if oc >= 2:
                    S["v0"].wait(e, v0b + oc - 1)
                S["s0"].inc(e.activation(qraw[:, (oc % 2) * 512:(oc % 2) * 512 + qw],
                                         bk(oc % 2, 0, 128, 0, qw), AF.Copy))

        @blk.vector
        def _(e):
            for oc in range(NCH):
                S["t1"].wait(e, t1b2 + oc + 1)
                qb = qraw[:, (oc % 2) * 512:(oc % 2) * 512 + qw]
                dst = qhat[:, oc * 512:oc * 512 + qw]
                e.tensor_tensor(self.scr1[:, 0:qw], qb, qtC[:, 0:qw], ALU.mult)
                e.tensor_tensor(self.scr2[:, 0:qw], bk(2 + (oc % 2), 0, 128, 0, qw),
                                qtS[:, 0:qw], ALU.mult)
                S["v0"].inc(e.tensor_tensor(dst, self.scr1[:, 0:qw], self.scr2[:, 0:qw], ALU.add))

        if self.debug == "deep" and qi == 0:
            @blk.sync
            def _(e):
                S["v0"].wait(e)
                S["dO"].inc(e.dma_start(out=g["d_qhat"], in_=qhat[:]), 16)

        def u2s0(j, qw=None, xtb=xtb, u2=u2):
            r = (u2[:, 0:512], u2[:, 512:1024], xtb[:, 0:512])[j % 3]
            return r[:, 0:qw] if qw else r

        def u2s1(j, qw=None, xtb=xtb, u2=u2):
            r = (u2[:, 1024:1536], u2[:, 1536:2048], xtb[:, 512:1024])[j % 3]
            return r[:, 0:qw] if qw else r

        for hp in range(NCH):
            h0, h1 = 2 * hp, 2 * hp + 1
            t0b3 = S["t0"].n
            s0b3 = S["s0"].n
            t2b3 = S["t2"].n

            @blk.tensor
            def _(e, hp=hp, t0b3=t0b3, s0b3=s0b3):
                S["v0"].wait(e)
                e.matmul(bk(4, 0, 65, 0, qw), self.vsum[0:1, h0 * 65:h0 * 65 + 65],
                         self.onesrow[0:1, 0:qw], start=True, stop=False)
                e.matmul(bk(5, 0, 65, 0, qw), self.vsum[0:1, h1 * 65:h1 * 65 + 65],
                         self.onesrow[0:1, 0:qw], start=True, stop=False)
                for ks in range(32):
                    if ks >= 2:
                        S["s0"].wait(e, s0b3 + 2 * (ks - 1))
                    S["t0"].inc(e.matmul(bk(ks % 2, 0, 128, 0, qw),
                                         self.khat[0:64, hp * T + ks * 128:hp * T + ks * 128 + 128],
                                         qhat[0:64, hp * 512:hp * 512 + qw],
                                         start=True, stop=True))
                    S["t0"].inc(e.matmul(bk(2 + ks % 2, 0, 128, 0, qw),
                                         self.khat[64:128, hp * T + ks * 128:hp * T + ks * 128 + 128],
                                         qhat[64:128, hp * 512:hp * 512 + qw],
                                         start=True, stop=True))
                    if ks >= 2:
                        j = ks - 2
                        e.matmul(bk(4, 0, 65, 0, qw),
                                 self.vext[:, j * 1040 + h0 * 65:j * 1040 + h0 * 65 + 65],
                                 u2s0(j, qw),
                                 start=False, stop=False)
                        S["t2"].inc(e.matmul(bk(5, 0, 65, 0, qw),
                                             self.vext[:, j * 1040 + h1 * 65:j * 1040 + h1 * 65 + 65],
                                             u2s1(j, qw),
                                             start=False, stop=False))
                for j in (30, 31):
                    S["s0"].wait(e, s0b3 + 2 * (j + 1))
                    e.matmul(bk(4, 0, 65, 0, qw),
                             self.vext[:, j * 1040 + h0 * 65:j * 1040 + h0 * 65 + 65],
                             u2s0(j, qw),
                             start=False, stop=(j == 31))
                    S["t2"].inc(e.matmul(bk(5, 0, 65, 0, qw),
                                         self.vext[:, j * 1040 + h1 * 65:j * 1040 + h1 * 65 + 65],
                                         u2s1(j, qw),
                                         start=False, stop=(j == 31)))

            @blk.scalar
            def _(e, hp=hp, t0b3=t0b3, t2b3=t2b3):
                for ks in range(32):
                    S["t0"].wait(e, t0b3 + 2 * ks + 2)
                    if ks >= 3:
                        S["t2"].wait(e, t2b3 + ks - 2)
                    S["s0"].inc(e.activation(u2s0(ks, qw),
                                             bk(ks % 2, 0, 128, 0, qw), AF.Square,
                                             scale=self.misc[:, 204:205],
                                             bias=self.misc[:, 205:206]))
                    S["s0"].inc(e.activation(u2s1(ks, qw),
                                             bk(2 + ks % 2, 0, 128, 0, qw), AF.Square,
                                             scale=self.misc[:, 204:205],
                                             bias=self.misc[:, 205:206]))
                S["t2"].wait(e, t2b3 + 32)
                e.activation(dsl[64:65, 0:qw], bk(4, 64, 65, 0, qw), AF.Ln)
                e.activation(dsl[64:65, 512:512 + qw], bk(5, 64, 65, 0, qw), AF.Ln)
                e.activation(dsl[64:65, 0:qw], dsl[64:65, 0:qw], AF.Exp,
                             scale=self.misc[64:65, 203:204])
                S["s1"].inc(e.activation(dsl[64:65, 512:512 + qw], dsl[64:65, 512:512 + qw],
                                         AF.Exp, scale=self.misc[64:65, 203:204]))

            s1b3 = S["s1"].n
            t1b3 = S["t1"].n

            @blk.tensor
            def _(e, s1b3=s1b3):
                S["s1"].wait(e, s1b3)
                e.matmul(bk(6, 0, 64, 0, qw), self.onesf2[64:65, 0:64], dsl[64:65, 0:qw],
                         start=True, stop=True)
                S["t1"].inc(e.matmul(bk(7, 0, 64, 0, qw), self.onesf2[64:65, 0:64],
                                     dsl[64:65, 512:512 + qw], start=True, stop=True))

            @blk.vector
            def _(e, hp=hp, t1b3=t1b3):
                S["t1"].wait(e, t1b3 + 1)
                e.tensor_copy(self.scr1[0:64, 0:qw], bk(6, 0, 64, 0, qw))
                e.tensor_copy(self.scr2[0:64, 0:qw], bk(7, 0, 64, 0, qw))
                e.tensor_tensor(osb[0:64, hp * 512:hp * 512 + qw], bk(4, 0, 64, 0, qw),
                                self.scr1[0:64, 0:qw], ALU.mult)
                S["v0"].inc(e.tensor_tensor(qraw[0:64, 0:qw],
                                            bk(5, 0, 64, 0, qw), self.scr2[0:64, 0:qw],
                                            ALU.mult))

            v0b4 = S["v0"].n
            t1b4 = S["t1"].n

            @blk.tensor
            def _(e, v0b4=v0b4):
                S["v0"].wait(e, v0b4)
                S["t1"].inc(e.matmul(bk(6, 64, 128, 0, qw), self.IDE,
                                     qraw[0:64, 0:qw], start=True, stop=True,
                                     tile_position=(0, 64)))

            @blk.vector
            def _(e, hp=hp, t1b4=t1b4):
                S["t1"].wait(e, t1b4 + 1)
                S["v0"].inc(e.tensor_copy(osb[64:128, hp * 512:hp * 512 + qw],
                                          bk(6, 64, 128, 0, qw)))

        if self.debug == "deep" and qi == 0:
            @blk.sync
            def _(e):
                S["v0"].wait(e)
                S["dO"].inc(e.dma_start(out=g["d_osb"], in_=osb[:]), 16)

        t0b4 = S["t0"].n
        dWb2 = S["dW"].n
        v1b4 = S["v1"].n

        @blk.gpsimd
        def _(e):
            S["t0"].wait(e)
            for dc in range(NCH):
                if dc >= 2:
                    S["t0"].wait(e, t0b4 + dc - 1)
                S["dW"].inc(e.dma_start(out=wo2[:, (dc % 2) * 1024:(dc % 2) * 1024 + 1024],
                                        in_=g["woA"][dc]), 16)

        @blk.tensor
        def _(e):
            S["v0"].wait(e)
            for dc in range(NCH):
                S["dW"].wait(e, dWb2 + 16 * (dc + 1))
                if dc >= 2:
                    S["v1"].wait(e, v1b4 + dc - 1)
                for ec in range(NCH):
                    ins = e.matmul(bk(dc % 2, 0, 128, 0, qw),
                                   wo2[:, (dc % 2) * 1024 + ec * 128:(dc % 2) * 1024 + ec * 128 + 128],
                                   osb[:, ec * 512:ec * 512 + qw],
                                   start=(ec == 0), stop=(ec == NCH - 1))
                S["t0"].inc(ins)

        @blk.vector
        def _(e):
            if qi > 0:
                S["dO"].wait(e)
            for dc in range(NCH):
                S["t0"].wait(e, t0b4 + dc + 1)
                S["v1"].inc(e.tensor_tensor(xt[:, dc * qw:dc * qw + qw],
                                            xt[:, dc * qw:dc * qw + qw],
                                            bk(dc % 2, 0, 128, 0, qw), ALU.add))

        self.emit_ln(
            qw,
            src_f32=lambda cc: xt[:, cc * qw:cc * qw + qw],
            src_bf=lambda cc: xtb[:, cc * 512:cc * 512 + qw],
            dst=lambda cc: xt[:, cc * qw:cc * qw + qw],
            gamma=None, beta=None,
            pre_wait=lambda e: S["v1"].wait(e))

        @blk.sync
        def _(e):
            S["v1"].wait(e)
            dsts = ["y_dram", "x1_dram"] + (["dbg"] if self.debug in ("x1", "inner") else [])
            for name in dsts:
                S["dO"].inc(e.dma_start(
                    out=g[name].rearrange("(c p) t -> p c t", p=128)[:, :, q0:q0 + qw],
                    in_=xt[:, 0:NCH * qw].rearrange("p (c t) -> p c t", t=qw)), 16)

    def inner_layers(self):
        S, blk, g = self.S, self.blk, self.g
        nc = self.nc
        with ExitStack() as inner:
            def isb(name, shape, dt=F32):
                return inner.enter_context(nc.sbuf_tensor(name, shape, dt))
            self.pkAq = isb("pkAq", [128, LOC], BF16)
            self.pkBq = isb("pkBq", [128, LOC], BF16)
            self.pkAk = isb("pkAk", [128, LOC], BF16)
            self.pkBk = isb("pkBk", [128, LOC], BF16)
            self.yt = isb("yt", [128, 4 * 512])
            self.yt2 = isb("yt2", [128, 4 * 513])
            self.xsb = isb("xsb", [128, NCH * 512], BF16)
            self.osbi = isb("osbi", [128, NCH * 512], BF16)
            self.kroll = isb("kroll", [128, NCH * 768], BF16)
            self.vroll = isb("vroll", [128, 6 * 1040], BF16)
            self.u2i = isb("u2i", [128, 2 * 2048], BF16)
            self.rnb = isb("rnb", [16, 1024], BF16)
            self.lc = isb("lc", [128, 96])
            self.expd = isb("expd", [16, NCH * 128], BF16)
            self.IND_i = isb("IND_i", [128, 128], BF16)

            @blk.gpsimd
            def _(e):
                S["t0"].wait(e)
                S["v1"].wait(e)
                S["dW"].inc(e.dma_start(out=self.pkAq[:], in_=g["rpAq"]), 16)
                S["dW"].inc(e.dma_start(out=self.pkBq[:], in_=g["rpBq"]), 16)
                S["dW"].inc(e.dma_start(out=self.pkAk[:], in_=g["rpAk"]), 16)
                S["dW"].inc(e.dma_start(out=self.pkBk[:], in_=g["rpBk"]), 16)
                S["dW"].inc(e.dma_start(out=self.expd[:], in_=g["EXPD"]), 16)
                S["dW"].inc(e.dma_start(out=self.IND_i[:], in_=g["IND"]), 16)
                ap = self.vroll.rearrange("p (a c) -> p a c", c=65)
                S["g0"].inc(e.memset(ap[:, :, 64:65], 1.0))

            for li in range(NL):
                self._layer_consts(li)
                with nc.sbuf_tensor(f"wqk_i{li}", [128, 16 * 1024], BF16) as wqk_i, \
                     nc.sbuf_tensor(f"wv_i{li}", [128, NCH * 1024], BF16) as wv_i, \
                     nc.sbuf_tensor(f"wo_i{li}", [128, NCH * 1024], BF16) as wo_i, \
                     nc.sbuf_tensor(f"qkbf{li}", [128, 16 * 512], BF16) as qkbf, \
                     nc.sbuf_tensor(f"q2b{li}", [128, 2 * 512], BF16) as q2b:
                    self._inner_attn(li, wqk_i, wv_i, wo_i, qkbf, q2b)
                with nc.sbuf_tensor(f"xn_i{li}", [128, NCH * LOC], BF16) as xn_i, \
                     nc.sbuf_tensor(f"w1p{li}", [128, 4 * 1024], BF16) as w1p, \
                     nc.sbuf_tensor(f"glb{li}", [128, 512], BF16) as glb, \
                     nc.sbuf_tensor(f"hmb{li}", [128, 2 * 512], BF16) as hmb:
                    self._inner_ffn_b01(li, xn_i, w1p, glb, hmb)
                with nc.sbuf_tensor(f"w2r{li}", [128, NCH * NJ * 128], BF16) as w2r, \
                     nc.sbuf_tensor(f"hmt{li}", [128, NJ * 512], BF16) as hmt, \
                     nc.sbuf_tensor(f"yres{li}", [128, NCH * 512], F32) as yres:
                    self._inner_ffn_b2(li, w2r, hmt, yres)

    def _layer_consts(self, li):
        S, blk, g = self.S, self.blk, self.g

        @blk.sync
        def _(e):
            S["v0"].wait(e)
            S["v1"].wait(e)
            S["v3"].wait(e)
            S["t3"].wait(e)
            S["dH"].inc(e.dma_start(out=self.lc[:, 0:44], in_=g["b1c"][li]), 16)
            S["dH"].inc(e.dma_start(out=self.lc[:, 44:48], in_=g["qkv4"][li]), 16)
            S["dH"].inc(e.dma_start(out=self.lc[:, 48:56], in_=g["b2c"][li]), 16)
            S["dH"].inc(e.dma_start(out=self.lc[:, 56:64], in_=g["lng"][li]), 16)
            S["dH"].inc(e.dma_start(out=self.lc[:, 64:72], in_=g["lnb6"][li]), 16)
            S["dH"].inc(e.dma_start(out=self.misc[0:1, 0:128], in_=g["qkabs"][li]), 16)

        vmb = S["v3"].n

        @blk.vector
        def _(e):
            S["dH"].wait(e)
            S["v3"].inc(e.tensor_reduce(self.misc[0:1, 192:193], self.misc[0:1, 0:64],
                                        mybir.AxisListType.X, ALU.max))
            S["v3"].inc(e.tensor_reduce(self.misc[0:1, 193:194], self.misc[0:1, 64:128],
                                        mybir.AxisListType.X, ALU.max))

        t3b = S["t3"].n

        @blk.tensor
        def _(e):
            S["v3"].wait(e, vmb + 2)
            S["v0"].wait(e)
            S["v1"].wait(e)
            S["t3"].inc(e.matmul(self.bk(7, 0, 128, 0, 2), self.onesf2[0:1, :],
                                 self.misc[0:1, 192:194], start=True, stop=True))

        vcb = S["v3"].n

        @blk.vector
        def _(e):
            S["t3"].wait(e, t3b + 1)
            S["v3"].inc(e.tensor_tensor(self.misc[0:1, 194:195], self.misc[0:1, 192:193],
                                        self.bk(7, 0, 1, 1, 2), ALU.mult))

        s1b = S["s1"].n

        @blk.scalar
        def _(e):
            S["v3"].wait(e, vcb + 1)
            S["s1"].inc(e.activation(self.misc[0:1, 195:196], self.misc[0:1, 194:195],
                                     AF.Relu, scale=self.misc[0:1, 206:207],
                                     bias=self.misc[0:1, 209:210]))

        vdb = S["v3"].n

        @blk.vector
        def _(e):
            S["s1"].wait(e, s1b + 1)
            S["v3"].inc(e.tensor_single_scalar(self.misc[0:1, 196:197],
                                               self.misc[0:1, 195:196], -1.0, ALU.mult))

        t3c = S["t3"].n

        @blk.tensor
        def _(e):
            S["v3"].wait(e, vdb + 1)
            S["t3"].inc(e.matmul(self.bk(7, 0, 128, 0, 1), self.onesf2[0:1, :],
                                 self.misc[0:1, 196:197], start=True, stop=True))

        @blk.vector
        def _(e):
            S["t3"].wait(e, t3c + 1)
            S["v3"].inc(e.tensor_copy(self.misc[:, 207:208], self.bk(7, 0, 128, 0, 1)))

    def _inner_attn(self, li, wqk_i, wv_i, wo_i, qkbf, q2b):
        S, blk, bk, g = self.S, self.blk, self.bk, self.g
        itmp = q2b[:, 0:256]

        @blk.gpsimd
        def _(e):
            S["t0"].wait(e)
            S["v0"].wait(e)
            for oc in range(16):
                S["dW"].inc(e.dma_start(out=wqk_i[:, oc * 1024:(oc + 1) * 1024],
                                        in_=g["Wqk"][li][oc]), 16)
            for kc in range(NCH):
                S["dW"].inc(e.dma_start(out=wv_i[:, kc * 1024:(kc + 1) * 1024],
                                        in_=g["Wv6"][li][kc * 128:(kc + 1) * 128, :]), 16)
            for dc in range(NCH):
                S["dW"].inc(e.dma_start(out=wo_i[:, dc * 1024:(dc + 1) * 1024],
                                        in_=g["Wo6"][li][dc]), 16)
        wdma = S["dW"].n

        for qi, (t0_, qw) in enumerate(QT):
            self._ia_tile(li, qi, t0_, qw, wqk_i, wv_i, wo_i, qkbf, q2b, itmp, wdma)

    def _ia_tile(self, li, qi, t0_, qw, wqk_i, wv_i, wo_i, qkbf, q2b, itmp, wdma):
        S, blk, bk, g = self.S, self.blk, self.bk, self.g
        yt, yt2, xsb, osbi = self.yt, self.yt2, self.xsb, self.osbi
        kroll, vroll, u2i, rnb, lc = self.kroll, self.vroll, self.u2i, self.rnb, self.lc
        nw = qw // 256
        nts = qw // 128
        src_name = "y_dram"
        dst_name = "ya_dram"

        dHb = S["dH"].n

        @blk.sync
        def _(e):
            S["dO"].wait(e)
            S["v0"].wait(e)
            S["dH"].inc(e.dma_start(
                out=yt[:, 0:4 * qw].rearrange("p (c t) -> p c t", t=qw),
                in_=g[src_name].rearrange("(c p) t -> p c t", p=128)[:, 0:4, t0_:t0_ + qw]), 16)
            if qi == 0:
                S["dH"].inc(e.dma_start(
                    out=yt2.rearrange("p (c t) -> p c t", t=513)[:, :, 1:qw + 1],
                    in_=g[src_name].rearrange("(c p) t -> p c t", p=128)[:, 4:8, 0:qw]), 16)
            else:
                S["dH"].inc(e.dma_start(
                    out=yt2.rearrange("p (c t) -> p c t", t=513)[:, :, 0:qw + 1],
                    in_=g[src_name].rearrange("(c p) t -> p c t", p=128)[:, 4:8, t0_ - 1:t0_ + qw]), 16)

        vcb = S["v1"].n

        @blk.vector
        def _(e):
            S["dH"].wait(e, dHb + 32)
            if qi == 0:
                e.memset(yt2.rearrange("p (c t) -> p c t", t=513)[:, :, 0:1], 0.0)
            for cc in range(4):
                e.tensor_copy(xsb[:, cc * 512:cc * 512 + qw], yt[:, cc * qw:cc * qw + qw])
            for cc in range(4):
                ins = e.tensor_copy(xsb[:, (4 + cc) * 512:(4 + cc) * 512 + qw],
                                    yt2[:, cc * 513:cc * 513 + qw])
            S["v1"].inc(ins)

        t0b, t1b, t2b, s0b, v0b = S["t0"].n, S["t1"].n, S["t2"].n, S["s0"].n, S["v0"].n

        @blk.tensor
        def _(e):
            S["dW"].wait(e, wdma)
            S["v1"].wait(e, vcb + 1)
            S["s1"].wait(e)
            for oc in range(16):
                if oc >= 2:
                    S["s0"].wait(e, s0b + oc - 1)
                for kc in range(NCH):
                    ins = e.matmul(bk(oc % 2, 0, 128, 0, qw),
                                   wqk_i[:, oc * 1024 + kc * 128:oc * 1024 + kc * 128 + 128],
                                   xsb[:, kc * 512:kc * 512 + qw],
                                   start=(kc == 0), stop=(kc == NCH - 1))
                S["t0"].inc(ins)
                if oc >= 1:
                    S["s0"].wait(e, s0b + oc)
                    S["v0"].wait(e, v0b + max(0, oc - 2))
                    S["t1"].inc(e.matmul(bk(2 + (oc - 1) % 2, 0, 128, 0, qw), self.R_sb,
                                         qkbf[:, (oc - 1) * 512:(oc - 1) * 512 + qw],
                                         start=True, stop=True))
                    side = 4 if (oc - 1) < 8 else 5
                    S["t2"].inc(e.matmul(bk(side, 0, 16, 0, qw),
                                         self.IND_i[:, ((oc - 1) % 8) * 16:((oc - 1) % 8) * 16 + 16],
                                         q2b[:, ((oc - 1) % 2) * 512:((oc - 1) % 2) * 512 + qw],
                                         start=((oc - 1) % 8 == 0), stop=((oc - 1) % 8 == 7)))
            S["s0"].wait(e, s0b + 16)
            S["v0"].wait(e, v0b + 14)
            S["t1"].inc(e.matmul(bk(2 + 1, 0, 128, 0, qw), self.R_sb,
                                 qkbf[:, 15 * 512:15 * 512 + qw], start=True, stop=True))
            S["t2"].inc(e.matmul(bk(5, 0, 16, 0, qw),
                                 self.IND_i[:, 7 * 16:8 * 16],
                                 q2b[:, 1 * 512:1 * 512 + qw],
                                 start=False, stop=True))

        s1b = S["s1"].n

        @blk.scalar
        def _(e):
            for oc in range(16):
                S["t0"].wait(e, t0b + oc + 1)
                if oc >= 2:
                    S["t2"].wait(e, t2b + oc - 1)
                e.activation(qkbf[:, oc * 512:oc * 512 + qw], bk(oc % 2, 0, 128, 0, qw), AF.Copy)
                S["s0"].inc(e.activation(q2b[:, (oc % 2) * 512:(oc % 2) * 512 + qw],
                                         bk(oc % 2, 0, 128, 0, qw), AF.Square))
            S["t2"].wait(e, t2b + 16)
            e.activation(self.dsl[0:16, 0:qw], bk(4, 0, 16, 0, qw), AF.Ln,
                         bias=self.misc[0:16, 208:209])
            e.activation(self.rnb[0:16, 0:qw], self.dsl[0:16, 0:qw], AF.Exp,
                         scale=self.misc[0:16, 202:203])
            e.activation(self.dsl[0:16, 512:512 + qw], bk(5, 0, 16, 0, qw), AF.Ln,
                         bias=self.misc[0:16, 208:209])
            S["s1"].inc(e.activation(self.rnb[0:16, 512:512 + qw], self.dsl[0:16, 512:512 + qw],
                                     AF.Exp, scale=self.misc[0:16, 202:203]))

        t3b = S["t3"].n
        t0vb = S["t0"].n

        v3pre = S["v3"].n

        @blk.tensor
        def _(e):
            S["s1"].wait(e, s1b + 1)
            for oc in range(16):
                if oc >= 2:
                    S["v3"].wait(e, v3pre + oc - 1)
                side_off = 0 if oc < 8 else 512
                S["t3"].inc(e.matmul(bk(6 + oc % 2, 0, 128, 0, qw),
                                     self.expd[:, (oc % 8) * 128:(oc % 8) * 128 + 128],
                                     self.rnb[0:16, side_off:side_off + qw],
                                     start=True, stop=True))
            i = 0
            for ts in range(nts):
                for et in range(2):
                    if i >= 2:
                        S["v1"].wait(e, vcb + 1 + i - 1)
                    for kc in range(NCH):
                        ins = e.matmul(bk(i % 2, 0, 128, 0, 512),
                                       xsb[:, kc * 512 + ts * 128:kc * 512 + ts * 128 + 128],
                                       wv_i[:, kc * 1024 + et * 512:kc * 1024 + et * 512 + 512],
                                       start=(kc == 0), stop=(kc == NCH - 1))
                    S["t0"].inc(ins)
                    i += 1

        v0b2 = S["v0"].n
        v3b2 = S["v3"].n

        @blk.vector
        def _(e):
            for oc in range(16):
                S["t1"].wait(e, t1b + oc + 1)
                qsd = lc[:, 44:45] if oc < 8 else lc[:, 46:47]
                qsr = lc[:, 45:46] if oc < 8 else lc[:, 47:48]
                pkA = self.pkAq if oc < 8 else self.pkAk
                pkB = self.pkBq if oc < 8 else self.pkBk
                qb = qkbf[:, oc * 512:oc * 512 + qw]
                e.scalar_tensor_tensor(self.scr1[:, 0:qw], qb, qsd,
                                       pkA[:, t0_:t0_ + qw], ALU.mult, ALU.mult)
                e.scalar_tensor_tensor(self.scr2[:, 0:qw], bk(2 + (oc % 2), 0, 128, 0, qw),
                                       qsr, pkB[:, t0_:t0_ + qw], ALU.mult, ALU.mult)
                S["v0"].inc(e.tensor_tensor(qb, self.scr1[:, 0:qw], self.scr2[:, 0:qw], ALU.add))
            for oc in range(16):
                S["t3"].wait(e, t3b + oc + 1)
                qb = qkbf[:, oc * 512:oc * 512 + qw]
                if oc < 8:
                    dst = qb
                else:
                    dst = kroll[:, (oc - 8) * 768 + 256:(oc - 8) * 768 + 256 + qw]
                S["v3"].inc(e.tensor_tensor(dst, qb, bk(6 + oc % 2, 0, 128, 0, qw), ALU.mult))
            i = 0
            for ts in range(nts):
                for et in range(2):
                    S["t0"].wait(e, t0vb + i + 1)
                    dst = vroll.rearrange("p (a c) -> p a c", c=65)[
                        :, (2 + ts) * 16 + et * 8:(2 + ts) * 16 + et * 8 + 8, 0:64]
                    S["v1"].inc(e.tensor_copy(
                        dst, bk(i % 2, 0, 128, 0, 512).rearrange("p (h c) -> p h c", c=64)))
                    i += 1
        v1ev = S["v1"].n

        self._ia_windows(li, qi, qw, qkbf, q2b, v1ev)

        t0b4 = S["t0"].n
        v1b4 = S["v1"].n

        @blk.tensor
        def _(e):
            S["v0"].wait(e)
            for dc in range(NCH):
                if dc >= 2:
                    S["v1"].wait(e, v1b4 + dc - 1)
                for ec in range(NCH):
                    ins = e.matmul(bk(dc % 2, 0, 128, 0, qw),
                                   wo_i[:, dc * 1024 + ec * 128:dc * 1024 + ec * 128 + 128],
                                   osbi[:, ec * 512:ec * 512 + qw],
                                   start=(ec == 0), stop=(ec == NCH - 1))
                S["t0"].inc(ins)

        @blk.vector
        def _(e):
            for dc in range(NCH):
                S["t0"].wait(e, t0b4 + dc + 1)
                if dc < 4:
                    dst = yt[:, dc * qw:dc * qw + qw]
                else:
                    dst = yt2[:, (dc - 4) * 513 + 1:(dc - 4) * 513 + 1 + qw]
                S["v1"].inc(e.tensor_tensor(dst, dst, bk(dc % 2, 0, 128, 0, qw), ALU.add))
            if qi < len(QT) - 1:
                e.tensor_copy(
                    kroll.rearrange("p (c t) -> p c t", t=768)[:, :, 0:256],
                    kroll.rearrange("p (c t) -> p c t", t=768)[:, :, qw:qw + 256])
                S["v0"].inc(e.tensor_copy(
                    vroll[:, 0:2 * 1040],
                    vroll[:, (nts - 2 + 2) * 1040:(nts + 2) * 1040]))

        @blk.sync
        def _(e):
            S["v1"].wait(e)
            dns = [dst_name] + ([f"d_ya{li}"] if self.debug == "inner" else [])
            for dn in dns:
                S["dO"].inc(e.dma_start(
                    out=g[dn].rearrange("(c p) t -> p c t", p=128)[:, 0:4, t0_:t0_ + qw],
                    in_=yt[:, 0:4 * qw].rearrange("p (c t) -> p c t", t=qw)), 16)
                S["dO"].inc(e.dma_start(
                    out=g[dn].rearrange("(c p) t -> p c t", p=128)[:, 4:8, t0_:t0_ + qw],
                    in_=yt2.rearrange("p (c t) -> p c t", t=513)[:, :, 1:qw + 1]), 16)

    def _ia_windows(self, li, qi, qw, qkbf, q2b, v1ev):
        S, blk, bk = self.S, self.blk, self.bk
        kroll, vroll, u2i, osbi = self.kroll, self.vroll, self.u2i, self.osbi
        nw = qw // 256
        nch_t = nw * NCH
        AVB = (6, 0)
        BCB = (7, 1)
        itmp = (q2b[:, 0:256], q2b[:, 512:768])
        t1t0 = S["t1"].n
        v0b0 = S["v0"].n
        v3b0 = S["v3"].n
        m = {k: {} for k in ("t0", "s2e", "g0", "t1a", "s2r", "t3", "v0d", "t1i", "v0c")}

        def chain(c):
            w, hp = c // NCH, c % NCH
            ks_lo = 2 if (qi == 0 and w == 0) else 0
            return w, hp, ks_lo, c % 2

        def S1(c):
            w, hp, ks_lo, p = chain(c)

            @blk.tensor
            def _(e, c=c, w=w, hp=hp, ks_lo=ks_lo):
                if c == 0:
                    S["v0"].wait(e, v0b0)
                    S["v3"].wait(e, v3b0)
                else:
                    S["s2"].wait(e, m["s2e"][c - 1])
                qcol = hp * 512 + w * 256
                for ks in range(ks_lo, 4):
                    kc0 = hp * 768 + w * 256 + ks * 128
                    S["t0"].inc(e.matmul(
                        bk(2 + (ks // 2), 0, 128, (ks % 2) * 256, (ks % 2) * 256 + 256),
                        kroll[0:64, kc0:kc0 + 128],
                        qkbf[0:64, qcol:qcol + 256], start=True, stop=True))
                    S["t0"].inc(e.matmul(
                        bk(4 + (ks // 2), 0, 128, (ks % 2) * 256, (ks % 2) * 256 + 256),
                        kroll[64:128, kc0:kc0 + 128],
                        qkbf[64:128, qcol:qcol + 256], start=True, stop=True))
            m["t0"][c] = S["t0"].n

        def S2(c):
            w, hp, ks_lo, p = chain(c)
            bu = p * 2048

            @blk.scalar
            def _(e, c=c, ks_lo=ks_lo, bu=bu):
                S["t0"].wait(e, m["t0"][c])
                S["t1"].wait(e, m["t1a"][c - 2] if c >= 2 else t1t0)
                if ks_lo == 0:
                    e.activation(u2i[:, bu:bu + 1024], self.psA[:, 1024:2048], AF.Exp,
                                 scale=self.misc[:, 206:207], bias=self.misc[:, 207:208])
                    S["s2"].inc(e.activation(u2i[:, bu + 1024:bu + 2048], self.psB[:, 0:1024],
                                             AF.Exp, scale=self.misc[:, 206:207],
                                             bias=self.misc[:, 207:208]))
                else:
                    e.activation(u2i[:, bu + 512:bu + 1024], self.psA[:, 1536:2048], AF.Exp,
                                 scale=self.misc[:, 206:207], bias=self.misc[:, 207:208])
                    S["s2"].inc(e.activation(u2i[:, bu + 1536:bu + 2048], self.psB[:, 512:1024],
                                             AF.Exp, scale=self.misc[:, 206:207],
                                             bias=self.misc[:, 207:208]))
            m["s2e"][c] = S["s2"].n

        def S3(c):
            w, hp, ks_lo, p = chain(c)
            bu = p * 2048

            @blk.gpsimd
            def _(e, c=c, ks_lo=ks_lo, bu=bu):
                S["s2"].wait(e, m["s2e"][c])
                for hh in range(2):
                    base = bu + hh * 1024
                    if ks_lo == 0:
                        S["g0"].inc(e.affine_select(
                            u2i[:, base:base + 512], u2i[:, base:base + 512],
                            [[128, 2], [-1, 256]], ALU.is_ge, 0.0,
                            base=0, channel_multiplier=1))
                    S["g0"].inc(e.affine_select(
                        u2i[:, base + 512:base + 1024], u2i[:, base + 512:base + 1024],
                        [[-128, 2], [1, 256]], ALU.is_ge, 0.0,
                        base=0, channel_multiplier=-1))
            m["g0"][c] = S["g0"].n

        def S4(c):
            w, hp, ks_lo, p = chain(c)
            bu = p * 2048
            ab = AVB[p]

            @blk.tensor
            def _(e, c=c, w=w, hp=hp, ks_lo=ks_lo, bu=bu, ab=ab):
                S["g0"].wait(e, m["g0"][c])
                if c >= 2:
                    S["v0"].wait(e, m["v0d"][c - 2])
                else:
                    S["v1"].wait(e, v1ev)
                first = True
                for ks in range(ks_lo, 4):
                    tchunk = 2 * w + ks
                    e.matmul(bk(ab, 0, 65, 0, 256),
                             vroll[:, tchunk * 1040 + (2 * hp) * 65:tchunk * 1040 + (2 * hp) * 65 + 65],
                             u2i[:, bu + ks * 256:bu + ks * 256 + 256],
                             start=first, stop=False)
                    first = False
                for ks in range(ks_lo, 4):
                    tchunk = 2 * w + ks
                    ins = e.matmul(bk(ab, 0, 65, 256, 512),
                                   vroll[:, tchunk * 1040 + (2 * hp + 1) * 65:tchunk * 1040 + (2 * hp + 1) * 65 + 65],
                                   u2i[:, bu + 1024 + ks * 256:bu + 1024 + ks * 256 + 256],
                                   start=False, stop=(ks == 3))
                S["t1"].inc(ins)
            m["t1a"][c] = S["t1"].n

        def S5(c):
            w, hp, ks_lo, p = chain(c)
            ab = AVB[p]
            ds = self.dsl[64:65, p * 512:p * 512 + 512]

            @blk.scalar
            def _(e, c=c, ab=ab, ds=ds):
                S["t1"].wait(e, m["t1a"][c])
                e.activation(ds, bk(ab, 64, 65, 0, 512), AF.Ln)
                S["s2"].inc(e.activation(ds, ds, AF.Exp, scale=self.misc[64:65, 203:204]))
            m["s2r"][c] = S["s2"].n

        def S6(c):
            w, hp, ks_lo, p = chain(c)
            bb = BCB[p]
            ds = self.dsl[64:65, p * 512:p * 512 + 512]

            @blk.tensor
            def _(e, c=c, bb=bb, ds=ds):
                S["s2"].wait(e, m["s2r"][c])
                if c >= 2:
                    S["v0"].wait(e, m["v0c"][c - 2])
                elif bb == 1:
                    S["v1"].wait(e, v1ev)
                S["t3"].inc(e.matmul(bk(bb, 0, 64, 0, 512), self.onesf2[64:65, 0:64],
                                     ds, start=True, stop=True))
            m["t3"][c] = S["t3"].n

        def S7(c):
            w, hp, ks_lo, p = chain(c)
            ab, bb = AVB[p], BCB[p]
            qcol = hp * 512 + w * 256

            @blk.vector
            def _(e, c=c, ab=ab, bb=bb, qcol=qcol, p=p):
                S["t3"].wait(e, m["t3"][c])
                e.tensor_copy(self.scr1[0:64, 0:512], bk(bb, 0, 64, 0, 512))
                e.tensor_tensor(osbi[0:64, qcol:qcol + 256], bk(ab, 0, 64, 0, 256),
                                self.scr1[0:64, 0:256], ALU.mult)
                S["v0"].inc(e.tensor_tensor(itmp[p][0:64, 0:256], bk(ab, 0, 64, 256, 512),
                                            self.scr1[0:64, 256:512], ALU.mult))
            m["v0d"][c] = S["v0"].n

        def S8(c):
            w, hp, ks_lo, p = chain(c)
            bb = BCB[p]

            @blk.tensor
            def _(e, c=c, bb=bb, p=p):
                S["v0"].wait(e, m["v0d"][c])
                S["t1"].inc(e.matmul(bk(bb, 64, 128, 0, 256), self.IDE, itmp[p][0:64, 0:256],
                                     start=True, stop=True, tile_position=(0, 64)))
            m["t1i"][c] = S["t1"].n

        def S9(c):
            w, hp, ks_lo, p = chain(c)
            bb = BCB[p]
            qcol = hp * 512 + w * 256

            @blk.vector
            def _(e, c=c, bb=bb, qcol=qcol):
                S["t1"].wait(e, m["t1i"][c])
                S["v0"].inc(e.tensor_copy(osbi[64:128, qcol:qcol + 256],
                                          bk(bb, 64, 128, 0, 256)))
            m["v0c"][c] = S["v0"].n

        for s in range(nch_t + 3):
            if s < nch_t:
                S1(s); S2(s); S3(s)
            if 1 <= s <= nch_t:
                S4(s - 1); S5(s - 1)
            if 2 <= s <= nch_t + 1:
                S6(s - 2); S7(s - 2)
            if 3 <= s <= nch_t + 2:
                S8(s - 3); S9(s - 3)

    def _inner_ffn_b01(self, li, xn_i, w1p, glb, hmb):
        S, blk, bk, g = self.S, self.blk, self.bk, self.g
        yt, yt2, xsb, lc = self.yt, self.yt2, self.xsb, self.lc

        v0pre = S["v0"].n
        v1pre = S["v1"].n
        xnm = []

        for qi, (t0_, qw) in enumerate(QT):
            dHb = S["dH"].n

            @blk.sync
            def _(e, qi=qi, t0_=t0_, qw=qw):
                if qi == 0:
                    S["t0"].wait(e)
                    S["v0"].wait(e)
                S["v1"].wait(e)
                S["dH"].inc(e.dma_start(
                    out=yt[:, 0:4 * qw].rearrange("p (c t) -> p c t", t=qw),
                    in_=g["ya_dram"].rearrange("(c p) t -> p c t", p=128)[:, 0:4, t0_:t0_ + qw]), 16)
                if qi == 0:
                    S["dH"].inc(e.dma_start(
                        out=yt2.rearrange("p (c t) -> p c t", t=513)[:, :, 1:qw + 1],
                        in_=g["ya_dram"].rearrange("(c p) t -> p c t", p=128)[:, 4:8, 0:qw]), 16)
                else:
                    S["dH"].inc(e.dma_start(
                        out=yt2.rearrange("p (c t) -> p c t", t=513)[:, :, 0:qw + 1],
                        in_=g["ya_dram"].rearrange("(c p) t -> p c t", p=128)[:, 4:8, t0_ - 1:t0_ + qw]), 16)

            if qi == 0:
                @blk.vector
                def _(e, dHb=dHb):
                    S["dH"].wait(e, dHb + 32)
                    S["v2"].inc(e.memset(yt2.rearrange("p (c t) -> p c t", t=513)[:, :, 0:1], 0.0))

            def src_f32(cc, qw=qw):
                if cc < 4:
                    return yt[:, cc * qw:cc * qw + qw]
                return yt2[:, (cc - 4) * 513:(cc - 4) * 513 + qw]

            xnm.append(self.emit_ln(
                qw,
                src_f32=src_f32,
                src_bf=lambda cc: xsb[:, cc * 512:cc * 512 + qw],
                dst=lambda cc, t0_=t0_, qw=qw: xn_i[:, cc * LOC + t0_:cc * LOC + t0_ + qw],
                gamma=lambda cc: lc[:, 56 + cc:57 + cc],
                beta=lambda cc: lc[:, 64 + cc:65 + cc],
                pre_wait=lambda e, dHb=dHb: S["dH"].wait(e, dHb + 32)))

        wdb = S["dW"].n
        t0pre = S["t0"].n
        ntile = len(QT)

        @blk.gpsimd
        def _(e):
            S["t0"].wait(e, t0pre)
            S["v0"].wait(e, v0pre)
            S["v1"].wait(e, v1pre)
            for j in range(NJ):
                if j >= 2:
                    S["t0"].wait(e, t0pre + (j - 1) * ntile)
                S["dW"].inc(e.dma_start(out=w1p[:, (j % 2) * 2048:(j % 2) * 2048 + 1024],
                                        in_=g["W1t"][li][j]), 16)
                S["dW"].inc(e.dma_start(out=w1p[:, (j % 2) * 2048 + 1024:(j % 2) * 2048 + 2048],
                                        in_=g["W1t"][li][NJ + j]), 16)

        t0b = S["t0"].n
        s2b = S["s2"].n
        v0b = S["v0"].n

        @blk.tensor
        def _(e):
            i = 0
            for j in range(NJ):
                S["dW"].wait(e, wdb + 32 * (j + 1))
                for qi, (t0_, qw) in enumerate(QT):
                    if j == 0:
                        S["v1"].wait(e, xnm[qi])
                    if i >= 2:
                        S["s2"].wait(e, s2b + i - 1)
                        S["v0"].wait(e, v0b + max(0, i - 1))
                    pb = 2 * (i % 2)
                    for kc in range(NCH):
                        e.matmul(bk(pb, 0, 128, 0, qw),
                                 w1p[:, (j % 2) * 2048 + kc * 128:(j % 2) * 2048 + kc * 128 + 128],
                                 xn_i[:, kc * LOC + t0_:kc * LOC + t0_ + qw],
                                 start=(kc == 0), stop=(kc == NCH - 1))
                        ins = e.matmul(bk(pb + 1, 0, 128, 0, qw),
                                       w1p[:, (j % 2) * 2048 + 1024 + kc * 128:(j % 2) * 2048 + 1024 + kc * 128 + 128],
                                       xn_i[:, kc * LOC + t0_:kc * LOC + t0_ + qw],
                                       start=(kc == 0), stop=(kc == NCH - 1))
                    S["t0"].inc(ins)
                    i += 1

        @blk.scalar
        def _(e):
            i = 0
            for j in range(NJ):
                for qi, (t0_, qw) in enumerate(QT):
                    S["t0"].wait(e, t0b + i + 1)
                    if i >= 1:
                        S["v0"].wait(e, v0b + i - 1)
                    S["s2"].inc(e.activation(glb[:, 0:qw], bk(2 * (i % 2) + 1, 0, 128, 0, qw),
                                             AF.Gelu, bias=lc[:, 22 + j:23 + j]))
                    i += 1

        dOb = S["dO"].n

        @blk.vector
        def _(e):
            i = 0
            for j in range(NJ):
                for qi, (t0_, qw) in enumerate(QT):
                    S["s2"].wait(e, s2b + i + 1)
                    if i >= 2:
                        S["dO"].wait(e, dOb + 16 * (i - 1))
                    S["v0"].inc(e.scalar_tensor_tensor(
                        hmb[:, (i % 2) * 512:(i % 2) * 512 + qw],
                        bk(2 * (i % 2), 0, 128, 0, qw), lc[:, j:j + 1], glb[:, 0:qw],
                        ALU.add, ALU.mult))
                    i += 1

        @blk.sync
        def _(e):
            i = 0
            for j in range(NJ):
                for qi, (t0_, qw) in enumerate(QT):
                    S["v0"].wait(e, v0b + i + 1)
                    S["dO"].inc(e.dma_start(
                        out=g["hm_dram"][:, j * LOC + t0_:j * LOC + t0_ + qw],
                        in_=hmb[:, (i % 2) * 512:(i % 2) * 512 + qw]), 16)
                    i += 1

    def _inner_ffn_b2(self, li, w2r, hmt, yres):
        S, blk, bk, g = self.S, self.blk, self.bk, self.g
        lc = self.lc

        wdb = S["dW"].n

        @blk.gpsimd
        def _(e):
            S["t0"].wait(e)
            S["v0"].wait(e)
            for oc in range(NCH):
                S["dW"].inc(e.dma_start(out=w2r[:, oc * NJ * 128:(oc + 1) * NJ * 128],
                                        in_=g["W2t"][li][oc]), 16)

        for qi, (t0_, qw) in enumerate(QT):
            dHb = S["dH"].n

            @blk.sync
            def _(e, qi=qi, t0_=t0_, qw=qw, dHb=dHb):
                S["v1"].wait(e)
                S["dH"].inc(e.dma_start(
                    out=hmt[:, 0:NJ * qw].rearrange("p (j t) -> p j t", t=qw),
                    in_=g["hm_dram"].rearrange("p (j t) -> p j t", t=LOC)[:, :, t0_:t0_ + qw]), 16)
                S["dH"].inc(e.dma_start(
                    out=yres[:, 0:NCH * qw].rearrange("p (c t) -> p c t", t=qw),
                    in_=g["ya_dram"].rearrange("(c p) t -> p c t", p=128)[:, :, t0_:t0_ + qw]), 16)

            t0b = S["t0"].n
            v1b = S["v1"].n

            @blk.tensor
            def _(e, qi=qi, t0_=t0_, qw=qw, dHb=dHb, t0b=t0b, v1b=v1b):
                S["dW"].wait(e, wdb + 16 * NCH)
                S["dH"].wait(e, dHb + 32)
                for oc in range(NCH):
                    if oc >= 2:
                        S["v1"].wait(e, v1b + oc - 1)
                    for j in range(NJ):
                        ins = e.matmul(bk(oc % 2, 0, 128, 0, qw),
                                       w2r[:, oc * NJ * 128 + j * 128:oc * NJ * 128 + j * 128 + 128],
                                       hmt[:, j * qw:j * qw + qw],
                                       start=(j == 0), stop=(j == NJ - 1))
                    S["t0"].inc(ins)

            @blk.vector
            def _(e, qi=qi, t0_=t0_, qw=qw, t0b=t0b):
                for oc in range(NCH):
                    S["t0"].wait(e, t0b + oc + 1)
                    S["v1"].inc(e.scalar_tensor_tensor(
                        yres[:, oc * qw:oc * qw + qw], bk(oc % 2, 0, 128, 0, qw),
                        lc[:, 48 + oc:49 + oc], yres[:, oc * qw:oc * qw + qw],
                        ALU.add, ALU.add))

            @blk.sync
            def _(e, qi=qi, t0_=t0_, qw=qw):
                S["v1"].wait(e)
                dns = ["y_dram"] + ([f"d_y{li}"] if self.debug == "inner" else [])
                for dn in dns:
                    S["dO"].inc(e.dma_start(
                        out=g[dn].rearrange("(c p) t -> p c t", p=128)[:, :, t0_:t0_ + qw],
                        in_=yres[:, 0:NCH * qw].rearrange("p (c t) -> p c t", t=qw)), 16)

    def p9_final(self):
        S, blk, g = self.S, self.blk, self.g
        nc = self.nc
        with nc.sbuf_tensor("f1", [128, NCH * 512], F32) as f1, \
             nc.sbuf_tensor("f2", [128, NCH * 512], F32) as f2, \
             nc.sbuf_tensor("fbc", [128, NCH * 512], BF16) as fb:
            for qi, (t0_, qw) in enumerate(QT):
                dHb = S["dH"].n

                @blk.sync
                def _(e, qi=qi, t0_=t0_, qw=qw):
                    if qi == 0:
                        S["t0"].wait(e)
                        S["v0"].wait(e)
                    S["v1"].wait(e)
                    S["dO"].wait(e)
                    S["dH"].inc(e.dma_start(
                        out=f1[:, 0:NCH * qw].rearrange("p (c t) -> p c t", t=qw),
                        in_=g["y_dram"].rearrange("(c p) t -> p c t", p=128)[:, :, t0_:t0_ + qw]), 16)
                    S["dH"].inc(e.dma_start(
                        out=f2[:, 0:NCH * qw].rearrange("p (c t) -> p c t", t=qw),
                        in_=g["x1_dram"].rearrange("(c p) t -> p c t", p=128)[:, :, t0_:t0_ + qw]), 16)

                vsb = S["v2"].n

                @blk.vector
                def _(e, dHb=dHb, qw=qw):
                    S["dH"].wait(e, dHb + 32)
                    for cc in range(NCH):
                        ins = e.tensor_tensor(f1[:, cc * qw:cc * qw + qw],
                                              f1[:, cc * qw:cc * qw + qw],
                                              f2[:, cc * qw:cc * qw + qw], ALU.add)
                    S["v2"].inc(ins)

                self.emit_ln(
                    qw,
                    src_f32=lambda cc, qw=qw: f1[:, cc * qw:cc * qw + qw],
                    src_bf=lambda cc: fb[:, cc * 512:cc * 512 + qw],
                    dst=lambda cc, qw=qw: f1[:, cc * qw:cc * qw + qw],
                    gamma=None, beta=None,
                    pre_wait=lambda e, vsb=vsb: S["v2"].wait(e, vsb + 1))

                @blk.sync
                def _(e, qi=qi, t0_=t0_, qw=qw):
                    S["v1"].wait(e)
                    S["dO"].inc(e.dma_start(
                        out=g["out"].rearrange("(c p) t -> p c t", p=128)[:, :, t0_:t0_ + qw],
                        in_=f1[:, 0:NCH * qw].rearrange("p (c t) -> p c t", t=qw)), 16)




def prepare_inputs(ins):
    x = ins["L0_attn_x"]
    w_q = ins["L0_attn_w_q"]; w_kv = ins["L0_attn_w_kv"]; w_o = ins["L0_attn_w_o"]
    base = {}
    base["wqA"] = tile_weight(w_q[:, PERM_A])
    base["wkA"] = tile_weight(w_kv[:, :D][:, PERM_A])
    base["wvA"] = np.ascontiguousarray(w_kv[:, D:])
    base["woA"] = tile_weight(w_o)
    lnA = np.zeros((128, 16), np.float32)
    lnA[:, 0:8] = ins["L0_attn_ln_w"].reshape(8, 128).T
    lnA[:, 8:16] = ins["L0_attn_ln_b"].reshape(8, 128).T
    base["lnA"] = lnA
    base["theta"] = np.asarray(ins["L0_attn_theta"], np.float32).reshape(1, 1)
    base["fb"] = FB.reshape(1, 32)
    base["t_k"] = np.arange(T, dtype=np.float32).reshape(1, T)
    base["Rm"] = rot_lhsT()
    base["IdeM"] = np.eye(64, dtype=np.float32)
    base["IND"] = head_indicator()
    base["EXPD"] = head_expander()
    Wqkv = ins["L0_ffn_Wqkv"]
    base["Wqk"] = np.stack([tile_weight(Wqkv[li][:, :2 * D]) for li in range(NL)])
    base["Wv6"] = np.ascontiguousarray(Wqkv[:, :, 2 * D:])
    base["Wo6"] = np.stack([tile_weight(ins["L0_ffn_Wo"][li]) for li in range(NL)])
    W1 = ins["L0_ffn_W1"]; W2 = ins["L0_ffn_W2"]
    w1t = []
    for li in range(NL):
        a = tile_weight(W1[li][:, :2730], mc=NJ)
        gt = tile_weight(W1[li][:, 2730:], mc=NJ)
        w1t.append(np.concatenate([a, gt], 0))
    base["W1t"] = np.stack(w1t)
    base["W2t"] = np.stack([tile_weight(W2[li], kc=NJ) for li in range(NL)])
    b1 = ins["L0_ffn_b1"]
    b1c = np.zeros((NL, 128, 44), np.float32)
    for li in range(NL):
        a = np.zeros(NJ * 128, np.float32); a[:2730] = b1[li][:2730]
        gt = np.zeros(NJ * 128, np.float32); gt[:2730] = b1[li][2730:]
        b1c[li, :, 0:22] = a.reshape(NJ, 128).T
        b1c[li, :, 22:44] = gt.reshape(NJ, 128).T
    base["b1c"] = b1c
    base["b2c"] = np.stack([ins["L0_ffn_b2"][li].reshape(8, 128).T for li in range(NL)])
    base["lng"] = np.stack([ins["L0_ffn_ln_g"][li].reshape(8, 128).T for li in range(NL)])
    base["lnb6"] = np.stack([ins["L0_ffn_ln_b"][li].reshape(8, 128).T for li in range(NL)])
    qs, ks = ins["L0_ffn_q_scale"], ins["L0_ffn_k_scale"]
    qkv4 = np.zeros((NL, 128, 4), np.float32)
    qkabs = np.zeros((NL, 1, 128), np.float32)
    for li in range(NL):
        qkv4[li, :, 0] = np.tile(qs[li], 2)
        qkv4[li, :, 1] = np.tile(np.concatenate([qs[li][32:], qs[li][:32]]), 2)
        qkv4[li, :, 2] = np.tile(ks[li], 2)
        qkv4[li, :, 3] = np.tile(np.concatenate([ks[li][32:], ks[li][:32]]), 2)
        qkabs[li, 0, 0:64] = np.abs(qs[li])
        qkabs[li, 0, 64:128] = np.abs(ks[li])
    base["qkv4"] = qkv4
    base["qkabs"] = qkabs

    maps = []
    for c in range(8):
        b, par = c // 2, c % 2
        toff = (0, 256)[par]
        m = dict(base)
        xb = np.ascontiguousarray(x[b].T.astype(np.float32))
        m["x_fm"] = xb
        m["x_loc"] = np.ascontiguousarray(xb[:, toff:toff + LOC])
        m["t_q"] = np.arange(toff, toff + LOC, dtype=np.float32).reshape(1, LOC)
        pos = np.arange(toff, toff + LOC)
        m["rpAq"] = np.ascontiguousarray(np.tile((ICOS[pos] * ISCALE[pos]).T, (2, 1)).astype(ml_dtypes.bfloat16))
        m["rpBq"] = np.ascontiguousarray(np.tile((ISIN[pos] * ISCALE[pos]).T, (2, 1)).astype(ml_dtypes.bfloat16))
        m["rpAk"] = np.ascontiguousarray(np.tile((ICOS[pos] / ISCALE[pos]).T, (2, 1)).astype(ml_dtypes.bfloat16))
        m["rpBk"] = np.ascontiguousarray(np.tile((ISIN[pos] / ISCALE[pos]).T, (2, 1)).astype(ml_dtypes.bfloat16))
        maps.append(m)
    return maps


def kernel(**inputs):
    nc, g = Builder(debug=None).build()
    maps = prepare_inputs(inputs)
    res = run_bass_kernel_spmd(nc, maps, core_ids=list(range(8)))
    out = np.empty((4, T, D), np.float32)
    for c in range(8):
        b, par = c // 2, c % 2
        o = res.results[c]["out"]
        if par == 0:
            out[b, 0:2048] = o[:, 0:2048].T
        else:
            out[b, 2048:4096] = o[:, 2048 - 256:4096 - 256].T
    return out


# version 08
# speedup vs baseline: 1.0223x; 1.0097x over previous
import numpy as np
import ml_dtypes
from contextlib import ExitStack

import concourse.bass as bass
import concourse.mybir as mybir
from concourse.bass_utils import run_bass_kernel_spmd

F32 = mybir.dt.float32
BF16 = mybir.dt.bfloat16
AF = mybir.ActivationFunctionType
ALU = mybir.AluOpType

D, T, H, HD = 1024, 4096, 16, 64
NCH = 8
EPS = np.float32(1e-5)
LOC = 3840
NL = 6
NJ = 22
QT = [(i * 512, 512) for i in range(7)] + [(3584, 256)]
NW = LOC // 256

_2PI = 6.283185307179586


def _split12(x):
    f = np.float32(x)
    m, e = np.frexp(np.float64(f))
    return np.float32(np.ldexp(np.round(np.ldexp(m, 12)), int(e) - 12))


D1c = _split12(_2PI)
D2c = _split12(_2PI - np.float64(D1c))
D3c = np.float32(_2PI - np.float64(D1c) - np.float64(D2c))
INV2PI = np.float32(1.0 / _2PI)
MAGIC = np.float32(3 << 22)
PI = np.float32(np.pi)
HALFPI = np.float32(np.pi / 2)


def freqs_base_f32():
    hd2 = HD // 2
    top = np.float32(2595.0) * np.log10(np.float32(1.0 + 4000.0 / 200.0))
    lin = np.linspace(np.float32(0.0), np.float32(top), hd2, dtype=np.float32)
    mel = np.power(np.float32(10.0), lin / np.float32(2595.0), dtype=np.float32) - np.float32(1.0)
    return (np.float32(200.0) * mel / np.float32(1000.0)).astype(np.float32)


def inner_rotary_tables():
    WINDOW = 256
    half = np.arange(0, HD, 2, dtype=np.float32)
    inv_freq = (np.float32(1.0) / np.power(np.float32(10000.0), half / np.float32(HD), dtype=np.float32)).astype(np.float32)
    t = np.arange(T, dtype=np.float32)
    freqs = t[:, None] * inv_freq[None, :]
    freqs = np.concatenate([freqs, freqs], -1).astype(np.float32)
    base_scale = ((half + np.float32(0.4 * HD)) / np.float32(1.4 * HD)).astype(np.float32)
    power = ((t - np.float32(T // 2)) / np.float32(WINDOW // 2)).astype(np.float32)
    scale = np.power(base_scale[None, :], power[:, None], dtype=np.float32)
    scale = np.concatenate([scale, scale], -1).astype(np.float32)
    return freqs, scale


def rot_lhsT():
    M = np.zeros((128, 128), np.float32)
    for m in range(128):
        if m % 64 < 32:
            M[m, m + 32] = -1.0
        else:
            M[m, m - 32] = 1.0
    return np.ascontiguousarray(M.T)


def head_indicator():
    ind = np.zeros((128, 128), np.float32)
    for cc in range(NCH):
        for p in range(128):
            ind[p, cc * 16 + 2 * cc + (p // 64)] = 1.0
    return ind


def head_expander():
    e = np.zeros((16, NCH * 128), np.float32)
    for cc in range(NCH):
        for c in range(128):
            e[2 * cc + (c // 64), cc * 128 + c] = 1.0
    return e


def tile_weight(w, kc=None, mc=None):
    K, M = w.shape
    kc = (K + 127) // 128 if kc is None else kc
    mc = (M + 127) // 128 if mc is None else mc
    wp = np.zeros((kc * 128, mc * 128), np.float32)
    wp[:K, :M] = w
    return np.ascontiguousarray(
        wp.reshape(kc, 128, mc, 128).transpose(2, 1, 0, 3).reshape(mc, 128, kc * 128))


FB = freqs_base_f32()
IFREQS, ISCALE = inner_rotary_tables()
ICOS = np.cos(IFREQS).astype(np.float32)
ISIN = np.sin(IFREQS).astype(np.float32)
permA = np.concatenate([np.arange(0, HD, 2), np.arange(1, HD, 2)])
PERM_A = np.concatenate([h * HD + permA for h in range(H)])


class Sem:
    def __init__(self, nc, ctx, name):
        self.h = ctx.enter_context(nc.semaphore(name))
        self.n = 0

    def inc(self, instr, k=1):
        instr.then_inc(self.h, k)
        self.n += k
        return instr

    def wait(self, eng, at=None):
        eng.wait_ge(self.h, self.n if at is None else at)


class Builder:
    def __init__(self, debug=None):
        self.debug = debug
        self.nc = bass.Bass("TRN2", target_bir_lowering=False, debug=False,
                            detect_race_conditions=False)
        self.g = {}

    def declare(self):
        nc, g = self.nc, self.g

        def din(name, shape, dt=F32):
            g[name] = nc.dram_tensor(name, shape, dt, kind="ExternalInput").ap()

        def dint(name, shape, dt=F32):
            g[name] = nc.dram_tensor(name, shape, dt, kind="Internal").ap()

        din("x_fm", [D, T])
        din("x_loc", [D, LOC])
        din("wqA", [NCH, 128, D])
        din("wkA", [NCH, 128, D])
        din("wvA", [D, D])
        din("woA", [NCH, 128, D])
        din("lnA", [128, 16])
        din("theta", [1, 1])
        din("fb", [1, 32])
        din("t_q", [1, LOC])
        din("t_k", [1, T])
        din("Rm", [128, 128])
        din("IdeM", [64, 64])
        din("IND", [128, 128])
        din("EXPD", [16, NCH * 128])
        din("Wqk", [NL, 16, 128, D])
        din("Wv6", [NL, D, D])
        din("Wo6", [NL, NCH, 128, D])
        din("W1t", [NL, 44, 128, D])
        din("W2t", [NL, NCH, 128, NJ * 128])
        din("b1c", [NL, 128, 44])
        din("b2c", [NL, 128, 8])
        din("lng", [NL, 128, 8])
        din("lnb6", [NL, 128, 8])
        din("qkv4", [NL, 128, 4])
        din("qkabs", [NL, 1, 128])
        din("rpAq", [128, LOC], BF16)
        din("rpBq", [128, LOC], BF16)
        din("rpAk", [128, LOC], BF16)
        din("rpBk", [128, LOC], BF16)

        g["out"] = nc.dram_tensor("out", [D, LOC], F32, kind="ExternalOutput").ap()
        if self.debug in ("x1", "deep", "inner"):
            g["dbg"] = nc.dram_tensor("dbg", [D, LOC], F32, kind="ExternalOutput").ap()
        if self.debug == "inner":
            for li in range(NL):
                g[f"d_ya{li}"] = nc.dram_tensor(f"d_ya{li}", [D, LOC], F32, kind="ExternalOutput").ap()
                g[f"d_y{li}"] = nc.dram_tensor(f"d_y{li}", [D, LOC], F32, kind="ExternalOutput").ap()
        if self.debug == "deep":
            g["d_tabCk"] = nc.dram_tensor("d_tabCk", [128, T], BF16, kind="ExternalOutput").ap()
            g["d_tabSk"] = nc.dram_tensor("d_tabSk", [128, T], BF16, kind="ExternalOutput").ap()
            g["d_khat"] = nc.dram_tensor("d_khat", [128, NCH * T], BF16, kind="ExternalOutput").ap()
            g["d_vext"] = nc.dram_tensor("d_vext", [128, 32 * 1040], BF16, kind="ExternalOutput").ap()
            g["d_vsum"] = nc.dram_tensor("d_vsum", [1, 1056], BF16, kind="ExternalOutput").ap()
            g["d_qhat"] = nc.dram_tensor("d_qhat", [128, NCH * 512], BF16, kind="ExternalOutput").ap()
            g["d_osb"] = nc.dram_tensor("d_osb", [128, NCH * 512], BF16, kind="ExternalOutput").ap()

        dint("h_dram", [128, NCH * T], BF16)
        dint("y_dram", [D, LOC])
        dint("ya_dram", [D, LOC])
        dint("x1_dram", [D, LOC])
        dint("hm_dram", [128, NJ * LOC], BF16)

    def bk(self, i, p0=0, p1=128, c0=0, c1=512):
        t = self.psA if i < 4 else self.psB
        return t[p0:p1, (i % 4) * 512 + c0:(i % 4) * 512 + c1]

    def emit_ln(self, qw, src_f32, src_bf, dst, gamma=None, beta=None, pre_wait=None):
        S, blk, bk = self.S, self.blk, self.bk
        rows = self.rows
        t1b, v1b, s1b, v2b = S["t1"].n, S["v1"].n, S["s1"].n, S["v2"].n

        @blk.vector
        def _(e):
            if pre_wait is not None:
                pre_wait(e)
            for cc in range(NCH):
                if cc >= 2:
                    S["t1"].wait(e, t1b + cc - 1)
                if src_f32 is not None:
                    e.tensor_copy(src_bf(cc), src_f32(cc))
                sq = self.scr1 if cc % 2 == 0 else self.scr2
                S["v1"].inc(e.tensor_tensor(sq[:, 0:qw], src_bf(cc), src_bf(cc), ALU.mult))
            S["t1"].wait(e, t1b + 8)
            e.tensor_single_scalar(rows[0:1, 0:qw], bk(6, 0, 1, 0, qw), float(np.float32(1.0 / D)), ALU.mult)
            e.tensor_single_scalar(rows[0:1, 512:512 + qw], bk(5, 0, 1, 0, qw), float(np.float32(1.0 / D)), ALU.mult)
            e.tensor_tensor(rows[0:1, 1024:1024 + qw], rows[0:1, 0:qw], rows[0:1, 0:qw], ALU.mult)
            S["v1"].inc(e.tensor_tensor(rows[0:1, 512:512 + qw], rows[0:1, 512:512 + qw],
                                        rows[0:1, 1024:1024 + qw], ALU.subtract))

        @blk.scalar
        def _(e):
            S["v1"].wait(e, v1b + 9)
            e.activation(rows[0:1, 512:512 + qw], rows[0:1, 512:512 + qw], AF.Ln,
                         bias=self.misc[0:1, 201:202])
            S["s1"].inc(e.activation(rows[0:1, 512:512 + qw], rows[0:1, 512:512 + qw],
                                     AF.Exp, scale=self.misc[0:1, 202:203]))

        @blk.vector
        def _(e):
            S["s1"].wait(e, s1b + 1)
            S["v2"].inc(e.tensor_tensor(rows[0:1, 1024:1024 + qw], rows[0:1, 0:qw],
                                        rows[0:1, 512:512 + qw], ALU.mult))

        @blk.tensor
        def _(e):
            S["s0"].wait(e)
            S["s2"].wait(e)
            S["v0"].wait(e)
            for cc in range(NCH):
                S["v1"].wait(e, v1b + cc + 1)
                sq = self.scr1 if cc % 2 == 0 else self.scr2
                e.matmul(bk(6, 0, 1, 0, qw), self.ones_col, src_bf(cc),
                         start=(cc == 0), stop=(cc == NCH - 1))
                S["t1"].inc(e.matmul(bk(5, 0, 1, 0, qw), self.onesc_f, sq[:, 0:qw],
                                     start=(cc == 0), stop=(cc == NCH - 1)))
            S["v1"].wait(e, v1b + 9)
            S["v2"].wait(e, v2b + 1)
            e.matmul(bk(7, 0, 128, 0, qw), self.onesf2[0:1, :], rows[0:1, 512:512 + qw],
                     start=True, stop=True)
            S["t1"].inc(e.matmul(bk(5, 0, 128, 0, qw), self.onesf2[0:1, :],
                                 rows[0:1, 1024:1024 + qw], start=True, stop=True))

        va = S["v1"].n

        @blk.vector
        def _(e):
            S["t1"].wait(e, t1b + 9)
            for cc in range(NCH):
                src = src_f32(cc) if src_f32 is not None else src_bf(cc)
                e.tensor_tensor(self.scr1[:, 0:qw], src, bk(7, 0, 128, 0, qw), ALU.mult)
                if gamma is None:
                    ins = e.tensor_tensor(dst(cc), self.scr1[:, 0:qw],
                                          bk(5, 0, 128, 0, qw), ALU.subtract)
                else:
                    e.tensor_tensor(self.scr1[:, 0:qw], self.scr1[:, 0:qw],
                                    bk(5, 0, 128, 0, qw), ALU.subtract)
                    ins = e.tensor_scalar(dst(cc), self.scr1[:, 0:qw],
                                          gamma(cc), beta(cc), ALU.mult, ALU.add)
            S["v1"].inc(ins)

        return va + 1

    def angle_tables(self, trow, ncols, dstC, dstS, pre=None):
        S, blk, bk = self.S, self.blk, self.bk
        nt = (ncols + 511) // 512
        tb, vb, sb_ = S["t0"].n, S["v0"].n, S["s0"].n
        v1b = S["v1"].n

        @blk.tensor
        def _(e):
            if pre is not None:
                pre(e)
            S["v0"].wait(e, vb)
            S["v1"].wait(e, v1b)
            for i in range(nt):
                t0, w = i * 512, min(512, ncols - i * 512)
                if i > 0:
                    S["v0"].wait(e, vb + i)
                S["t0"].inc(e.matmul(bk(7, 0, 128, 0, w), self.fdup,
                                     trow[0:1, t0:t0 + w], start=True, stop=True))

        @blk.vector
        def _(e):
            for i in range(nt):
                t0, w = i * 512, min(512, ncols - i * 512)
                S["t0"].wait(e, tb + i + 1)
                if i > 0:
                    S["s0"].wait(e, sb_ + 2 * i)
                a = bk(7, 0, 128, 0, w)
                r = bk(6, 0, 128, 0, w)
                ab = bk(5, 0, 128, 0, w)
                nn = self.scr1[:, 0:w]
                nh = self.scr2[:, 0:w]
                e.tensor_scalar(nn, a, float(INV2PI), float(MAGIC), ALU.mult, ALU.add)
                e.tensor_single_scalar(nn, nn, float(MAGIC), ALU.subtract)
                e.tensor_scalar(nh, nn, float(np.float32(1.0 / 64.0)), float(MAGIC),
                                ALU.mult, ALU.add)
                e.tensor_scalar(nh, nh, float(MAGIC), 64.0, ALU.subtract, ALU.mult)
                e.scalar_tensor_tensor(r, nh, float(-D1c), a, ALU.mult, ALU.add)
                e.scalar_tensor_tensor(r, nh, float(-D2c), r, ALU.mult, ALU.add)
                e.tensor_tensor(nh, nn, nh, ALU.subtract)
                e.scalar_tensor_tensor(r, nh, float(-D1c), r, ALU.mult, ALU.add)
                e.scalar_tensor_tensor(r, nh, float(-D2c), r, ALU.mult, ALU.add)
                e.scalar_tensor_tensor(r, nn, float(-D3c), r, ALU.mult, ALU.add)
                e.tensor_single_scalar(r, r, float(PI), ALU.min)
                e.tensor_single_scalar(r, r, float(-PI), ALU.max)
                e.tensor_single_scalar(nn, r, -1.0, ALU.mult)
                S["v0"].inc(e.tensor_max(ab, nn, r))

        @blk.scalar
        def _(e):
            for i in range(nt):
                t0, w = i * 512, min(512, ncols - i * 512)
                S["v0"].wait(e, vb + i + 1)
                S["s0"].inc(e.activation(dstS[:, t0:t0 + w], bk(6, 0, 128, 0, w), AF.Sin))
                S["s0"].inc(e.activation(dstC[:, t0:t0 + w], bk(5, 0, 128, 0, w),
                                         AF.Sin, scale=self.misc[:, 203:204],
                                         bias=self.misc[:, 200:201]))

    def build(self):
        self.declare()
        nc, g = self.nc, self.g
        with ExitStack() as ctx:
            self.ctx = ctx
            S = self.S = {n: Sem(nc, ctx, n) for n in
                          ["dW", "dH", "dO", "t0", "t1", "t2", "t3", "s0", "s1",
                           "s2", "v0", "v1", "v2", "v3", "g0"]}
            self.psA = ctx.enter_context(nc.psum_tensor("psA", [128, 2048], F32))
            self.psB = ctx.enter_context(nc.psum_tensor("psB", [128, 2048], F32))

            def sbuf(name, shape, dt=F32):
                return ctx.enter_context(nc.sbuf_tensor(name, shape, dt))
            self.sbuf = sbuf
            self.blk = ctx.enter_context(nc.Block())
            blk = self.blk

            cst = sbuf("cst", [128, 200], BF16)
            self.R_sb = cst[:, 0:128]
            self.ones_col = cst[:, 128:129]
            self.IDE = cst[0:64, 132:196]
            self.onesrow = sbuf("onesrow", [1, 512], BF16)
            self.onesf2 = sbuf("onesf2", [128, 128], F32)
            self.onesc_f = self.onesf2[:, 0:1]
            self.lnA_sb = sbuf("lnA_sb", [128, 16])
            self.misc = sbuf("misc", [128, 256])
            self.rows = sbuf("rows", [1, 1536])
            self.scr1 = sbuf("scr1", [128, 512])
            self.scr2 = sbuf("scr2", [128, 512])
            self.tq1 = sbuf("tq1", [1, 512])
            self.dsl = sbuf("dsl", [128, 1024])

            @blk.sync
            def _(e):
                S["dH"].inc(e.dma_start(out=self.lnA_sb[:], in_=g["lnA"]), 16)
                S["dH"].inc(e.dma_start(out=self.misc[0:1, 0:1], in_=g["theta"]), 16)
                S["dH"].inc(e.dma_start(out=self.misc[0:1, 32:64], in_=g["fb"]), 16)


            @blk.gpsimd
            def _(e):
                S["dW"].inc(e.dma_start(out=self.R_sb, in_=g["Rm"]), 16)
                S["dW"].inc(e.dma_start(out=self.IDE, in_=g["IdeM"]), 16)
                e.memset(self.ones_col, 1.0)
                e.memset(self.onesrow[:], 1.0)
                e.memset(self.misc[:, 200:201], float(HALFPI))
                e.memset(self.misc[:, 201:202], float(EPS))
                e.memset(self.misc[:, 202:203], -0.5)
                e.memset(self.misc[:, 203:204], -1.0)
                e.memset(self.misc[:, 204:205], 0.125)
                e.memset(self.misc[:, 205:206], 1.0)
                e.memset(self.misc[:, 206:207], 8.0)
                e.memset(self.misc[:, 208:209], 1e-24)
                e.memset(self.misc[:, 209:210], -40.0)
                S["g0"].inc(e.memset(self.onesf2[:], 1.0))

            @blk.vector
            def _(e):
                S["dH"].wait(e)
                S["g0"].wait(e)
                for r in range(4):
                    e.tensor_scalar(self.misc[0:1, 64 + 32 * r:96 + 32 * r],
                                    self.misc[0:1, 32:64],
                                    self.misc[0:1, 0:1],
                                    float(np.float32(1.0 / 220.0)),
                                    ALU.mult, ALU.mult)
                S["v0"].inc(e.engine_nop() if hasattr(e, "engine_nop") else e.nop())
            self.fdup = self.misc[0:1, 64:192]

            stageA_stack = ExitStack()
            self.khat = stageA_stack.enter_context(nc.sbuf_tensor("khat", [128, NCH * T], BF16))
            self.vext = stageA_stack.enter_context(nc.sbuf_tensor("vext", [128, 32 * 1040], BF16))
            self.vsum = stageA_stack.enter_context(nc.sbuf_tensor("vsum", [1, 1056], BF16))
            with nc.sbuf_tensor("tabCk", [128, T], BF16) as tabCk, \
                 nc.sbuf_tensor("tabSk", [128, T], BF16) as tabSk:
                with nc.sbuf_tensor("tkrow", [1, T], F32) as tkrow:
                    dHb0 = S["dH"].n

                    @blk.sync
                    def _(e):
                        S["dH"].inc(e.dma_start(out=tkrow[0:1, :], in_=g["t_k"]), 16)

                    self.angle_tables(tkrow[0:1, :], T, tabCk, tabSk,
                                      pre=lambda e: S["dH"].wait(e, dHb0 + 16))
                if self.debug == "deep":
                    @blk.sync
                    def _(e):
                        S["s0"].wait(e)
                        S["dO"].inc(e.dma_start(out=g["d_tabCk"], in_=tabCk[:]), 16)
                        S["dO"].inc(e.dma_start(out=g["d_tabSk"], in_=tabSk[:]), 16)
                self.p1_hbuild()
                self.p2_khat(tabCk, tabSk)
            self.p3_vext()
            if self.debug == "deep":
                @blk.sync
                def _(e):
                    S["v0"].wait(e)
                    S["dO"].inc(e.dma_start(out=g["d_khat"], in_=self.khat[:]), 16)
                    S["dO"].inc(e.dma_start(out=g["d_vext"], in_=self.vext[:, 0:32 * 1040]), 16)
                    S["dO"].inc(e.dma_start(out=g["d_vsum"], in_=self.vsum[:]), 16)
            self.p4_attention()
            stageA_stack.close()

            if self.debug != "x1":
                self.inner_layers()
                self.p9_final()

            @blk.sync
            def _(e):
                S["dO"].wait(e)
        return nc, g

    def p1_hbuild(self):
        S, blk, g = self.S, self.blk, self.g
        with self.nc.sbuf_tensor("xb2", [128, 2 * NCH * 512], BF16) as xb2, \
             self.nc.sbuf_tensor("hloc", [128, 2 * NCH * 512], BF16) as hloc:
            dWb = S["dW"].n
            applies = []

            @blk.gpsimd
            def _(e):
                S["t0"].wait(e)
                for tt in range(8):
                    S["dW"].inc(e.dma_start(
                        out=xb2[:, (tt % 2) * NCH * 512:(tt % 2) * NCH * 512 + NCH * 512].rearrange("p (c t) -> p c t", t=512),
                        in_=g["x_fm"].rearrange("(c p) t -> p c t", p=128)[:, :, tt * 512:(tt + 1) * 512]), 16)


            for tt in range(8):
                base = (tt % 2) * NCH * 512

                @blk.gpsimd
                def _(e, tt=tt, base=base):
                    if tt == 0:
                        S["t0"].wait(e)
                    if tt >= 2:
                        S["v1"].wait(e, applies[tt - 2])
                    S["dW"].inc(e.dma_start(
                        out=xb2[:, base:base + NCH * 512].rearrange("p (c t) -> p c t", t=512),
                        in_=g["x_fm"].rearrange("(c p) t -> p c t", p=128)[:, :, tt * 512:(tt + 1) * 512]), 16)
                dma_mark = S["dW"].n

                def pre_wait(e, dma_mark=dma_mark, tt=tt):
                    S["dW"].wait(e, dma_mark)

                va = self.emit_ln(
                    512,
                    src_f32=None,
                    src_bf=lambda cc, base=base: xb2[:, base + cc * 512:base + cc * 512 + 512],
                    dst=lambda cc, base=base: hloc[:, base + cc * 512:base + cc * 512 + 512],
                    gamma=lambda cc: self.lnA_sb[:, cc:cc + 1],
                    beta=lambda cc: self.lnA_sb[:, 8 + cc:9 + cc],
                    pre_wait=pre_wait)
                applies.append(va)

                @blk.sync
                def _(e, tt=tt, base=base, va=va):
                    S["v1"].wait(e, va)
                    S["dH"].inc(e.dma_start(
                        out=g["h_dram"].rearrange("p (c t) -> p c t", t=T)[:, :, tt * 512:(tt + 1) * 512],
                        in_=hloc[:, base:base + NCH * 512].rearrange("p (c t) -> p c t", t=512)), 16)

            @blk.sync
            def _(e):
                S["dH"].wait(e)

    def p2_khat(self, tabCk, tabSk):
        S, blk, bk, g = self.S, self.blk, self.bk, self.g
        with self.nc.sbuf_tensor("wk_sb", [128, NCH * 1024], BF16) as wk_sb, \
             self.nc.sbuf_tensor("hti", [128, 2 * NCH * 512], BF16) as hti, \
             self.nc.sbuf_tensor("kraw", [128, 2 * 512], BF16) as kraw, \
             self.nc.sbuf_tensor("t1sb", [128, 2 * 512], F32) as t1sb:
            dWb, dHb = S["dW"].n, S["dH"].n
            t0b, t1b, s0b, v0b = S["t0"].n, S["t1"].n, S["s0"].n, S["v0"].n

            @blk.gpsimd
            def _(e):
                S["v1"].wait(e)
                S["dH"].wait(e)
                for oc in range(NCH):
                    S["dW"].inc(e.dma_start(out=wk_sb[:, oc * 1024:(oc + 1) * 1024],
                                            in_=g["wkA"][oc]), 16)

            @blk.sync
            def _(e):
                S["v1"].wait(e)
                for tt in range(8):
                    if tt >= 2:
                        S["t0"].wait(e, t0b + (tt - 1) * 8)
                    S["dH"].inc(e.dma_start(
                        out=hti[:, (tt % 2) * NCH * 512:(tt % 2) * NCH * 512 + NCH * 512].rearrange("p (c t) -> p c t", t=512),
                        in_=g["h_dram"].rearrange("p (c t) -> p c t", t=T)[:, :, tt * 512:(tt + 1) * 512]), 16)

            @blk.tensor
            def _(e):
                S["dW"].wait(e, dWb + 16 * NCH)
                S["s0"].wait(e)
                i = 0
                for tt in range(8):
                    S["dH"].wait(e, dHb + 16 * (tt + 1))
                    hb = (tt % 2) * NCH * 512
                    for oc in range(NCH):
                        if i >= 2:
                            S["s0"].wait(e, s0b + i - 1)
                        for kc in range(NCH):
                            ins = e.matmul(bk(i % 2),
                                           wk_sb[:, oc * 1024 + kc * 128:oc * 1024 + kc * 128 + 128],
                                           hti[:, hb + kc * 512:hb + kc * 512 + 512],
                                           start=(kc == 0), stop=(kc == NCH - 1))
                        S["t0"].inc(ins)
                        if i >= 1:
                            S["s0"].wait(e, s0b + i)
                            S["v0"].wait(e, v0b + max(0, i - 1))
                            S["t1"].inc(e.matmul(bk(2 + (i - 1) % 2), self.R_sb,
                                                 kraw[:, ((i - 1) % 2) * 512:((i - 1) % 2) * 512 + 512],
                                                 start=True, stop=True))
                        i += 1
                S["s0"].wait(e, s0b + 64)
                S["v0"].wait(e, v0b + 62)
                S["t1"].inc(e.matmul(bk(3), self.R_sb, kraw[:, 512:1024],
                                     start=True, stop=True))

            @blk.scalar
            def _(e):
                for i in range(64):
                    S["t0"].wait(e, t0b + i + 1)
                    if i >= 2:
                        S["v0"].wait(e, v0b + i - 1)
                    S["s0"].inc(e.activation(kraw[:, (i % 2) * 512:(i % 2) * 512 + 512],
                                             bk(i % 2), AF.Copy))

            @blk.vector
            def _(e):
                for i in range(64):
                    tt, oc = i // 8, i % 8
                    S["t1"].wait(e, t1b + i + 1)
                    kb = kraw[:, (i % 2) * 512:(i % 2) * 512 + 512]
                    dst = self.khat[:, oc * T + tt * 512:oc * T + tt * 512 + 512]
                    tb = t1sb[:, (i % 2) * 512:(i % 2) * 512 + 512]
                    e.tensor_tensor(tb, kb, tabCk[:, tt * 512:tt * 512 + 512], ALU.mult)
                    e.tensor_tensor(self.scr2[:], bk(2 + i % 2), tabSk[:, tt * 512:tt * 512 + 512], ALU.mult)
                    S["v0"].inc(e.tensor_tensor(dst, tb, self.scr2[:], ALU.add))

    def p3_vext(self):
        S, blk, bk, g = self.S, self.blk, self.bk, self.g
        with self.nc.sbuf_tensor("wv_sb", [128, NCH * 1024], BF16) as wv_sb, \
             self.nc.sbuf_tensor("hti3", [128, 2 * NCH * 512], BF16) as hti:
            dWb, dHb = S["dW"].n, S["dH"].n
            t0b, v0b, g0b = S["t0"].n, S["v0"].n, S["g0"].n

            @blk.gpsimd
            def _(e):
                S["v0"].wait(e)
                S["t0"].wait(e)
                ap = self.vext.rearrange("p (a c) -> p a c", c=65)
                S["g0"].inc(e.memset(ap[:, :, 64:65], 1.0))
                for kc in range(NCH):
                    S["dW"].inc(e.dma_start(out=wv_sb[:, kc * 1024:(kc + 1) * 1024],
                                            in_=g["wvA"][kc * 128:(kc + 1) * 128, :]), 16)

            @blk.sync
            def _(e):
                S["t0"].wait(e)
                S["v0"].wait(e)
                for tt in range(8):
                    if tt >= 2:
                        S["t0"].wait(e, t0b + (tt - 1) * 8)
                    S["dH"].inc(e.dma_start(
                        out=hti[:, (tt % 2) * NCH * 512:(tt % 2) * NCH * 512 + NCH * 512].rearrange("p (c t) -> p c t", t=512),
                        in_=g["h_dram"].rearrange("p (c t) -> p c t", t=T)[:, :, tt * 512:(tt + 1) * 512]), 16)

            @blk.tensor
            def _(e):
                S["dW"].wait(e, dWb + 16 * NCH)
                S["g0"].wait(e, g0b + 1)
                i = 0
                for tt in range(8):
                    S["dH"].wait(e, dHb + 16 * (tt + 1))
                    hb = (tt % 2) * NCH * 512
                    for ts in range(4):
                        for et in range(2):
                            if i >= 3:
                                S["v0"].wait(e, v0b + i - 2)
                            for kc in range(NCH):
                                ins = e.matmul(
                                    bk(4 + i % 3),
                                    hti[:, hb + kc * 512 + ts * 128:hb + kc * 512 + ts * 128 + 128],
                                    wv_sb[:, kc * 1024 + et * 512:kc * 1024 + et * 512 + 512],
                                    start=(kc == 0), stop=(kc == NCH - 1))
                            S["t0"].inc(ins)
                            i += 1

            @blk.vector
            def _(e):
                i = 0
                for tt in range(8):
                    for ts in range(4):
                        for et in range(2):
                            S["t0"].wait(e, t0b + i + 1)
                            gts = tt * 4 + ts
                            dst = self.vext.rearrange("p (a c) -> p a c", c=65)[
                                :, gts * 16 + et * 8:gts * 16 + et * 8 + 8, 0:64]
                            S["v0"].inc(e.tensor_copy(
                                dst, bk(4 + i % 3).rearrange("p (h c) -> p h c", c=64)))
                            i += 1

            t0b2 = S["t0"].n

            @blk.tensor
            def _(e):
                S["v0"].wait(e)
                for ts in range(32):
                    for pi, (c0, w) in enumerate([(0, 512), (512, 512), (1024, 16)]):
                        ins = e.matmul(bk(pi, 0, 1, 0, w), self.ones_col,
                                       self.vext[:, ts * 1040 + c0:ts * 1040 + c0 + w],
                                       start=(ts == 0), stop=(ts == 31))
                S["t0"].inc(ins)

            @blk.vector
            def _(e):
                S["t0"].wait(e, t0b2 + 1)
                e.tensor_copy(self.vsum[0:1, 0:512], bk(0, 0, 1, 0, 512))
                e.tensor_copy(self.vsum[0:1, 512:1024], bk(1, 0, 1, 0, 512))
                S["v0"].inc(e.tensor_copy(self.vsum[0:1, 1024:1040], bk(2, 0, 1, 0, 16)))

    def p4_attention(self):
        nc = self.nc
        with nc.sbuf_tensor("xt", [128, NCH * 512], F32) as xt, \
             nc.sbuf_tensor("xtb", [128, NCH * 512], BF16) as xtb, \
             nc.sbuf_tensor("hq", [128, NCH * 512], BF16) as hq, \
             nc.sbuf_tensor("qhat", [128, NCH * 512], BF16) as qhat, \
             nc.sbuf_tensor("qraw", [128, 2 * 512], BF16) as qraw, \
             nc.sbuf_tensor("wq2", [128, 2 * 1024], BF16) as wq2, \
             nc.sbuf_tensor("wo2", [128, 2 * 1024], BF16) as wo2, \
             nc.sbuf_tensor("u2", [128, 4 * 512], BF16) as u2, \
             nc.sbuf_tensor("qtC", [128, 512], BF16) as qtC, \
             nc.sbuf_tensor("qtS", [128, 512], BF16) as qtS:
            osb = hq
            dsl = self.dsl
            for qi, (q0, qw) in enumerate(QT):
                self.p4_qtile(qi, q0, qw, xt, xtb, hq, qhat, qraw, wq2, wo2,
                              u2, osb, dsl, qtC, qtS)

    def p4_qtile(self, qi, q0, qw, xt, xtb, hq, qhat, qraw, wq2, wo2,
                 u2, osb, dsl, qtC, qtS):
        S, blk, bk, g = self.S, self.blk, self.bk, self.g
        dHb = S["dH"].n

        @blk.sync
        def _(e):
            if qi == 0:
                S["t0"].wait(e)
                S["v0"].wait(e)
            if qi > 0:
                S["dO"].wait(e)
            S["dH"].inc(e.dma_start(out=self.tq1[0:1, 0:qw],
                                    in_=g["t_q"][0:1, q0:q0 + qw]), 16)
            S["dH"].inc(e.dma_start(
                out=xt[:, 0:NCH * qw].rearrange("p (c t) -> p c t", t=qw),
                in_=g["x_loc"].rearrange("(c p) t -> p c t", p=128)[:, :, q0:q0 + qw]), 16)

        self.angle_tables(self.tq1[0:1, 0:qw], qw, qtC, qtS,
                          pre=lambda e: S["dH"].wait(e, dHb + 16))

        def pre_wait(e):
            S["dH"].wait(e, dHb + 32)

        hva = self.emit_ln(
            qw,
            src_f32=lambda cc: xt[:, cc * qw:cc * qw + qw],
            src_bf=lambda cc: xtb[:, cc * 512:cc * 512 + qw],
            dst=lambda cc: hq[:, cc * 512:cc * 512 + qw],
            gamma=lambda cc: self.lnA_sb[:, cc:cc + 1],
            beta=lambda cc: self.lnA_sb[:, 8 + cc:9 + cc],
            pre_wait=pre_wait)

        dWb = S["dW"].n
        t0b, t1b2, s0b, v0b = S["t0"].n, S["t1"].n, S["s0"].n, S["v0"].n

        @blk.gpsimd
        def _(e):
            S["t0"].wait(e)
            for oc in range(NCH):
                if oc >= 2:
                    S["t0"].wait(e, t0b + oc - 1)
                S["dW"].inc(e.dma_start(out=wq2[:, (oc % 2) * 1024:(oc % 2) * 1024 + 1024],
                                        in_=g["wqA"][oc]), 16)

        @blk.tensor
        def _(e):
            S["v1"].wait(e, hva)
            for oc in range(NCH):
                S["dW"].wait(e, dWb + 16 * (oc + 1))
                if oc >= 2:
                    S["s0"].wait(e, s0b + oc - 1)
                for kc in range(NCH):
                    ins = e.matmul(bk(oc % 2, 0, 128, 0, qw),
                                   wq2[:, (oc % 2) * 1024 + kc * 128:(oc % 2) * 1024 + kc * 128 + 128],
                                   hq[:, kc * 512:kc * 512 + qw],
                                   start=(kc == 0), stop=(kc == NCH - 1))
                S["t0"].inc(ins)
                if oc >= 1:
                    S["s0"].wait(e, s0b + oc)
                    S["v0"].wait(e, v0b + max(0, oc - 2))
                    S["t1"].inc(e.matmul(bk(2 + (oc - 1) % 2, 0, 128, 0, qw), self.R_sb,
                                         qraw[:, ((oc - 1) % 2) * 512:((oc - 1) % 2) * 512 + qw],
                                         start=True, stop=True))
            S["s0"].wait(e, s0b + 8)
            S["v0"].wait(e, v0b + 6)
            S["t1"].inc(e.matmul(bk(3, 0, 128, 0, qw), self.R_sb,
                                 qraw[:, 512:512 + qw], start=True, stop=True))

        @blk.scalar
        def _(e):
            for oc in range(NCH):
                S["t0"].wait(e, t0b + oc + 1)
                if oc >= 2:
                    S["v0"].wait(e, v0b + oc - 1)
                S["s0"].inc(e.activation(qraw[:, (oc % 2) * 512:(oc % 2) * 512 + qw],
                                         bk(oc % 2, 0, 128, 0, qw), AF.Copy))

        @blk.vector
        def _(e):
            for oc in range(NCH):
                S["t1"].wait(e, t1b2 + oc + 1)
                qb = qraw[:, (oc % 2) * 512:(oc % 2) * 512 + qw]
                dst = qhat[:, oc * 512:oc * 512 + qw]
                e.tensor_tensor(self.scr1[:, 0:qw], qb, qtC[:, 0:qw], ALU.mult)
                e.tensor_tensor(self.scr2[:, 0:qw], bk(2 + (oc % 2), 0, 128, 0, qw),
                                qtS[:, 0:qw], ALU.mult)
                S["v0"].inc(e.tensor_tensor(dst, self.scr1[:, 0:qw], self.scr2[:, 0:qw], ALU.add))

        if self.debug == "deep" and qi == 0:
            @blk.sync
            def _(e):
                S["v0"].wait(e)
                S["dO"].inc(e.dma_start(out=g["d_qhat"], in_=qhat[:]), 16)

        def u2s0(j, qw=None, xtb=xtb, u2=u2):
            r = (u2[:, 0:512], u2[:, 512:1024], xtb[:, 0:512])[j % 3]
            return r[:, 0:qw] if qw else r

        def u2s1(j, qw=None, xtb=xtb, u2=u2):
            r = (u2[:, 1024:1536], u2[:, 1536:2048], xtb[:, 512:1024])[j % 3]
            return r[:, 0:qw] if qw else r

        for hp in range(NCH):
            h0, h1 = 2 * hp, 2 * hp + 1
            t0b3 = S["t0"].n
            s0b3 = S["s0"].n
            t2b3 = S["t2"].n

            @blk.tensor
            def _(e, hp=hp, t0b3=t0b3, s0b3=s0b3):
                S["v0"].wait(e)
                e.matmul(bk(4, 0, 65, 0, qw), self.vsum[0:1, h0 * 65:h0 * 65 + 65],
                         self.onesrow[0:1, 0:qw], start=True, stop=False)
                e.matmul(bk(5, 0, 65, 0, qw), self.vsum[0:1, h1 * 65:h1 * 65 + 65],
                         self.onesrow[0:1, 0:qw], start=True, stop=False)
                for ks in range(32):
                    if ks >= 2:
                        S["s0"].wait(e, s0b3 + 2 * (ks - 1))
                    S["t0"].inc(e.matmul(bk(ks % 2, 0, 128, 0, qw),
                                         self.khat[0:64, hp * T + ks * 128:hp * T + ks * 128 + 128],
                                         qhat[0:64, hp * 512:hp * 512 + qw],
                                         start=True, stop=True))
                    S["t0"].inc(e.matmul(bk(2 + ks % 2, 0, 128, 0, qw),
                                         self.khat[64:128, hp * T + ks * 128:hp * T + ks * 128 + 128],
                                         qhat[64:128, hp * 512:hp * 512 + qw],
                                         start=True, stop=True))
                    if ks >= 2:
                        j = ks - 2
                        e.matmul(bk(4, 0, 65, 0, qw),
                                 self.vext[:, j * 1040 + h0 * 65:j * 1040 + h0 * 65 + 65],
                                 u2s0(j, qw),
                                 start=False, stop=False)
                        S["t2"].inc(e.matmul(bk(5, 0, 65, 0, qw),
                                             self.vext[:, j * 1040 + h1 * 65:j * 1040 + h1 * 65 + 65],
                                             u2s1(j, qw),
                                             start=False, stop=False))
                for j in (30, 31):
                    S["s0"].wait(e, s0b3 + 2 * (j + 1))
                    e.matmul(bk(4, 0, 65, 0, qw),
                             self.vext[:, j * 1040 + h0 * 65:j * 1040 + h0 * 65 + 65],
                             u2s0(j, qw),
                             start=False, stop=(j == 31))
                    S["t2"].inc(e.matmul(bk(5, 0, 65, 0, qw),
                                         self.vext[:, j * 1040 + h1 * 65:j * 1040 + h1 * 65 + 65],
                                         u2s1(j, qw),
                                         start=False, stop=(j == 31)))

            @blk.scalar
            def _(e, hp=hp, t0b3=t0b3, t2b3=t2b3):
                for ks in range(32):
                    S["t0"].wait(e, t0b3 + 2 * ks + 2)
                    if ks >= 3:
                        S["t2"].wait(e, t2b3 + ks - 2)
                    S["s0"].inc(e.activation(u2s0(ks, qw),
                                             bk(ks % 2, 0, 128, 0, qw), AF.Square,
                                             scale=self.misc[:, 204:205],
                                             bias=self.misc[:, 205:206]))
                    S["s0"].inc(e.activation(u2s1(ks, qw),
                                             bk(2 + ks % 2, 0, 128, 0, qw), AF.Square,
                                             scale=self.misc[:, 204:205],
                                             bias=self.misc[:, 205:206]))
                S["t2"].wait(e, t2b3 + 32)
                e.activation(dsl[64:65, 0:qw], bk(4, 64, 65, 0, qw), AF.Ln)
                e.activation(dsl[64:65, 512:512 + qw], bk(5, 64, 65, 0, qw), AF.Ln)
                e.activation(dsl[64:65, 0:qw], dsl[64:65, 0:qw], AF.Exp,
                             scale=self.misc[64:65, 203:204])
                S["s1"].inc(e.activation(dsl[64:65, 512:512 + qw], dsl[64:65, 512:512 + qw],
                                         AF.Exp, scale=self.misc[64:65, 203:204]))

            s1b3 = S["s1"].n
            t1b3 = S["t1"].n

            @blk.tensor
            def _(e, s1b3=s1b3):
                S["s1"].wait(e, s1b3)
                e.matmul(bk(6, 0, 64, 0, qw), self.onesf2[64:65, 0:64], dsl[64:65, 0:qw],
                         start=True, stop=True)
                S["t1"].inc(e.matmul(bk(7, 0, 64, 0, qw), self.onesf2[64:65, 0:64],
                                     dsl[64:65, 512:512 + qw], start=True, stop=True))

            @blk.vector
            def _(e, hp=hp, t1b3=t1b3):
                S["t1"].wait(e, t1b3 + 1)
                e.tensor_copy(self.scr1[0:64, 0:qw], bk(6, 0, 64, 0, qw))
                e.tensor_copy(self.scr2[0:64, 0:qw], bk(7, 0, 64, 0, qw))
                e.tensor_tensor(osb[0:64, hp * 512:hp * 512 + qw], bk(4, 0, 64, 0, qw),
                                self.scr1[0:64, 0:qw], ALU.mult)
                S["v0"].inc(e.tensor_tensor(qraw[0:64, 0:qw],
                                            bk(5, 0, 64, 0, qw), self.scr2[0:64, 0:qw],
                                            ALU.mult))

            v0b4 = S["v0"].n
            t1b4 = S["t1"].n

            @blk.tensor
            def _(e, v0b4=v0b4):
                S["v0"].wait(e, v0b4)
                S["t1"].inc(e.matmul(bk(6, 64, 128, 0, qw), self.IDE,
                                     qraw[0:64, 0:qw], start=True, stop=True,
                                     tile_position=(0, 64)))

            @blk.vector
            def _(e, hp=hp, t1b4=t1b4):
                S["t1"].wait(e, t1b4 + 1)
                S["v0"].inc(e.tensor_copy(osb[64:128, hp * 512:hp * 512 + qw],
                                          bk(6, 64, 128, 0, qw)))

        if self.debug == "deep" and qi == 0:
            @blk.sync
            def _(e):
                S["v0"].wait(e)
                S["dO"].inc(e.dma_start(out=g["d_osb"], in_=osb[:]), 16)

        t0b4 = S["t0"].n
        dWb2 = S["dW"].n
        v1b4 = S["v1"].n

        @blk.gpsimd
        def _(e):
            S["t0"].wait(e)
            for dc in range(NCH):
                if dc >= 2:
                    S["t0"].wait(e, t0b4 + dc - 1)
                S["dW"].inc(e.dma_start(out=wo2[:, (dc % 2) * 1024:(dc % 2) * 1024 + 1024],
                                        in_=g["woA"][dc]), 16)

        @blk.tensor
        def _(e):
            S["v0"].wait(e)
            for dc in range(NCH):
                S["dW"].wait(e, dWb2 + 16 * (dc + 1))
                if dc >= 2:
                    S["v1"].wait(e, v1b4 + dc - 1)
                for ec in range(NCH):
                    ins = e.matmul(bk(dc % 2, 0, 128, 0, qw),
                                   wo2[:, (dc % 2) * 1024 + ec * 128:(dc % 2) * 1024 + ec * 128 + 128],
                                   osb[:, ec * 512:ec * 512 + qw],
                                   start=(ec == 0), stop=(ec == NCH - 1))
                S["t0"].inc(ins)

        @blk.vector
        def _(e):
            if qi > 0:
                S["dO"].wait(e)
            for dc in range(NCH):
                S["t0"].wait(e, t0b4 + dc + 1)
                S["v1"].inc(e.tensor_tensor(xt[:, dc * qw:dc * qw + qw],
                                            xt[:, dc * qw:dc * qw + qw],
                                            bk(dc % 2, 0, 128, 0, qw), ALU.add))

        self.emit_ln(
            qw,
            src_f32=lambda cc: xt[:, cc * qw:cc * qw + qw],
            src_bf=lambda cc: xtb[:, cc * 512:cc * 512 + qw],
            dst=lambda cc: xt[:, cc * qw:cc * qw + qw],
            gamma=None, beta=None,
            pre_wait=lambda e: S["v1"].wait(e))

        @blk.sync
        def _(e):
            S["v1"].wait(e)
            dsts = ["y_dram", "x1_dram"] + (["dbg"] if self.debug in ("x1", "inner") else [])
            for name in dsts:
                S["dO"].inc(e.dma_start(
                    out=g[name].rearrange("(c p) t -> p c t", p=128)[:, :, q0:q0 + qw],
                    in_=xt[:, 0:NCH * qw].rearrange("p (c t) -> p c t", t=qw)), 16)

    def inner_layers(self):
        S, blk, g = self.S, self.blk, self.g
        nc = self.nc
        with ExitStack() as inner:
            def isb(name, shape, dt=F32):
                return inner.enter_context(nc.sbuf_tensor(name, shape, dt))
            self.pkAq = isb("pkAq", [128, LOC], BF16)
            self.pkBq = isb("pkBq", [128, LOC], BF16)
            self.pkAk = isb("pkAk", [128, LOC], BF16)
            self.pkBk = isb("pkBk", [128, LOC], BF16)
            self.yt = isb("yt", [128, 4 * 512])
            self.yt2 = isb("yt2", [128, 4 * 513])
            self.xsb = isb("xsb", [128, NCH * 512], BF16)
            self.osbi = isb("osbi", [128, NCH * 512], BF16)
            self.kroll = isb("kroll", [128, NCH * 768], BF16)
            self.vroll = isb("vroll", [128, 6 * 1040], BF16)
            self.u2i = isb("u2i", [128, 2 * 2048], BF16)
            self.rnb = isb("rnb", [16, 1024], BF16)
            self.lc = isb("lc", [128, 96])
            self.expd = isb("expd", [16, NCH * 128], BF16)
            self.IND_i = isb("IND_i", [128, 128], BF16)

            @blk.gpsimd
            def _(e):
                S["t0"].wait(e)
                S["v1"].wait(e)
                S["dW"].inc(e.dma_start(out=self.pkAq[:], in_=g["rpAq"]), 16)
                S["dW"].inc(e.dma_start(out=self.pkBq[:], in_=g["rpBq"]), 16)
                S["dW"].inc(e.dma_start(out=self.pkAk[:], in_=g["rpAk"]), 16)
                S["dW"].inc(e.dma_start(out=self.pkBk[:], in_=g["rpBk"]), 16)
                S["dW"].inc(e.dma_start(out=self.expd[:], in_=g["EXPD"]), 16)
                S["dW"].inc(e.dma_start(out=self.IND_i[:], in_=g["IND"]), 16)
                ap = self.vroll.rearrange("p (a c) -> p a c", c=65)
                S["g0"].inc(e.memset(ap[:, :, 64:65], 1.0))

            for li in range(NL):
                self._layer_consts(li)
                with nc.sbuf_tensor(f"wqk_i{li}", [128, 16 * 1024], BF16) as wqk_i, \
                     nc.sbuf_tensor(f"wv_i{li}", [128, NCH * 1024], BF16) as wv_i, \
                     nc.sbuf_tensor(f"wo_i{li}", [128, NCH * 1024], BF16) as wo_i, \
                     nc.sbuf_tensor(f"qkbf{li}", [128, 16 * 512], BF16) as qkbf, \
                     nc.sbuf_tensor(f"q2b{li}", [128, 2 * 512], BF16) as q2b:
                    self._inner_attn(li, wqk_i, wv_i, wo_i, qkbf, q2b)
                with nc.sbuf_tensor(f"xn_i{li}", [128, NCH * LOC], BF16) as xn_i, \
                     nc.sbuf_tensor(f"w1p{li}", [128, 4 * 1024], BF16) as w1p, \
                     nc.sbuf_tensor(f"glb{li}", [128, 512], BF16) as glb, \
                     nc.sbuf_tensor(f"hmb{li}", [128, 2 * 512], BF16) as hmb, \
                     nc.sbuf_tensor(f"xb2b{li}", [128, NCH * 512], BF16) as xb2b:
                    self._inner_ffn_b01(li, xn_i, w1p, glb, hmb, xb2b)
                with nc.sbuf_tensor(f"w2r{li}", [128, NCH * NJ * 128], BF16) as w2r, \
                     nc.sbuf_tensor(f"hmt{li}", [128, NJ * 512], BF16) as hmt, \
                     nc.sbuf_tensor(f"yres{li}", [128, NCH * 512], F32) as yres:
                    self._inner_ffn_b2(li, w2r, hmt, yres)

    def _layer_consts(self, li):
        S, blk, g = self.S, self.blk, self.g

        @blk.sync
        def _(e):
            S["v0"].wait(e)
            S["v1"].wait(e)
            S["v3"].wait(e)
            S["t3"].wait(e)
            S["dH"].inc(e.dma_start(out=self.lc[:, 0:44], in_=g["b1c"][li]), 16)
            S["dH"].inc(e.dma_start(out=self.lc[:, 44:48], in_=g["qkv4"][li]), 16)
            S["dH"].inc(e.dma_start(out=self.lc[:, 48:56], in_=g["b2c"][li]), 16)
            S["dH"].inc(e.dma_start(out=self.lc[:, 56:64], in_=g["lng"][li]), 16)
            S["dH"].inc(e.dma_start(out=self.lc[:, 64:72], in_=g["lnb6"][li]), 16)
            S["dH"].inc(e.dma_start(out=self.misc[0:1, 0:128], in_=g["qkabs"][li]), 16)

        vmb = S["v3"].n

        @blk.vector
        def _(e):
            S["dH"].wait(e)
            S["v3"].inc(e.tensor_reduce(self.misc[0:1, 192:193], self.misc[0:1, 0:64],
                                        mybir.AxisListType.X, ALU.max))
            S["v3"].inc(e.tensor_reduce(self.misc[0:1, 193:194], self.misc[0:1, 64:128],
                                        mybir.AxisListType.X, ALU.max))

        t3b = S["t3"].n

        @blk.tensor
        def _(e):
            S["v3"].wait(e, vmb + 2)
            S["v0"].wait(e)
            S["v1"].wait(e)
            S["t3"].inc(e.matmul(self.bk(7, 0, 128, 0, 2), self.onesf2[0:1, :],
                                 self.misc[0:1, 192:194], start=True, stop=True))

        vcb = S["v3"].n

        @blk.vector
        def _(e):
            S["t3"].wait(e, t3b + 1)
            S["v3"].inc(e.tensor_tensor(self.misc[0:1, 194:195], self.misc[0:1, 192:193],
                                        self.bk(7, 0, 1, 1, 2), ALU.mult))

        s1b = S["s1"].n

        @blk.scalar
        def _(e):
            S["v3"].wait(e, vcb + 1)
            S["s1"].inc(e.activation(self.misc[0:1, 195:196], self.misc[0:1, 194:195],
                                     AF.Relu, scale=self.misc[0:1, 206:207],
                                     bias=self.misc[0:1, 209:210]))

        vdb = S["v3"].n

        @blk.vector
        def _(e):
            S["s1"].wait(e, s1b + 1)
            S["v3"].inc(e.tensor_single_scalar(self.misc[0:1, 196:197],
                                               self.misc[0:1, 195:196], -1.0, ALU.mult))

        t3c = S["t3"].n

        @blk.tensor
        def _(e):
            S["v3"].wait(e, vdb + 1)
            S["t3"].inc(e.matmul(self.bk(7, 0, 128, 0, 1), self.onesf2[0:1, :],
                                 self.misc[0:1, 196:197], start=True, stop=True))

        @blk.vector
        def _(e):
            S["t3"].wait(e, t3c + 1)
            S["v3"].inc(e.tensor_copy(self.misc[:, 207:208], self.bk(7, 0, 128, 0, 1)))

    def _inner_attn(self, li, wqk_i, wv_i, wo_i, qkbf, q2b):
        S, blk, bk, g = self.S, self.blk, self.bk, self.g
        itmp = q2b[:, 0:256]

        @blk.gpsimd
        def _(e):
            S["t0"].wait(e)
            S["v0"].wait(e)
            for oc in range(16):
                S["dW"].inc(e.dma_start(out=wqk_i[:, oc * 1024:(oc + 1) * 1024],
                                        in_=g["Wqk"][li][oc]), 16)
            for kc in range(NCH):
                S["dW"].inc(e.dma_start(out=wv_i[:, kc * 1024:(kc + 1) * 1024],
                                        in_=g["Wv6"][li][kc * 128:(kc + 1) * 128, :]), 16)
            for dc in range(NCH):
                S["dW"].inc(e.dma_start(out=wo_i[:, dc * 1024:(dc + 1) * 1024],
                                        in_=g["Wo6"][li][dc]), 16)
        wdma = S["dW"].n

        for qi, (t0_, qw) in enumerate(QT):
            self._ia_tile(li, qi, t0_, qw, wqk_i, wv_i, wo_i, qkbf, q2b, itmp, wdma)

    def _ia_tile(self, li, qi, t0_, qw, wqk_i, wv_i, wo_i, qkbf, q2b, itmp, wdma):
        S, blk, bk, g = self.S, self.blk, self.bk, self.g
        yt, yt2, xsb, osbi = self.yt, self.yt2, self.xsb, self.osbi
        kroll, vroll, u2i, rnb, lc = self.kroll, self.vroll, self.u2i, self.rnb, self.lc
        nw = qw // 256
        nts = qw // 128
        src_name = "y_dram"
        dst_name = "ya_dram"

        dHb = S["dH"].n

        @blk.sync
        def _(e):
            S["dO"].wait(e)
            S["v0"].wait(e)
            S["dH"].inc(e.dma_start(
                out=yt[:, 0:4 * qw].rearrange("p (c t) -> p c t", t=qw),
                in_=g[src_name].rearrange("(c p) t -> p c t", p=128)[:, 0:4, t0_:t0_ + qw]), 16)
            if qi == 0:
                S["dH"].inc(e.dma_start(
                    out=yt2.rearrange("p (c t) -> p c t", t=513)[:, :, 1:qw + 1],
                    in_=g[src_name].rearrange("(c p) t -> p c t", p=128)[:, 4:8, 0:qw]), 16)
            else:
                S["dH"].inc(e.dma_start(
                    out=yt2.rearrange("p (c t) -> p c t", t=513)[:, :, 0:qw + 1],
                    in_=g[src_name].rearrange("(c p) t -> p c t", p=128)[:, 4:8, t0_ - 1:t0_ + qw]), 16)

        vcb = S["v1"].n

        @blk.vector
        def _(e):
            S["dH"].wait(e, dHb + 32)
            if qi == 0:
                e.memset(yt2.rearrange("p (c t) -> p c t", t=513)[:, :, 0:1], 0.0)
            for cc in range(4):
                e.tensor_copy(xsb[:, cc * 512:cc * 512 + qw], yt[:, cc * qw:cc * qw + qw])
            for cc in range(4):
                ins = e.tensor_copy(xsb[:, (4 + cc) * 512:(4 + cc) * 512 + qw],
                                    yt2[:, cc * 513:cc * 513 + qw])
            S["v1"].inc(ins)

        t0b, t1b, t2b, s0b, v0b = S["t0"].n, S["t1"].n, S["t2"].n, S["s0"].n, S["v0"].n

        @blk.tensor
        def _(e):
            S["dW"].wait(e, wdma)
            S["v1"].wait(e, vcb + 1)
            S["s1"].wait(e)
            for oc in range(16):
                if oc >= 2:
                    S["s0"].wait(e, s0b + oc - 1)
                for kc in range(NCH):
                    ins = e.matmul(bk(oc % 2, 0, 128, 0, qw),
                                   wqk_i[:, oc * 1024 + kc * 128:oc * 1024 + kc * 128 + 128],
                                   xsb[:, kc * 512:kc * 512 + qw],
                                   start=(kc == 0), stop=(kc == NCH - 1))
                S["t0"].inc(ins)
                if oc >= 1:
                    S["s0"].wait(e, s0b + oc)
                    S["v0"].wait(e, v0b + max(0, oc - 2))
                    S["t1"].inc(e.matmul(bk(2 + (oc - 1) % 2, 0, 128, 0, qw), self.R_sb,
                                         qkbf[:, (oc - 1) * 512:(oc - 1) * 512 + qw],
                                         start=True, stop=True))
                    side = 4 if (oc - 1) < 8 else 5
                    S["t2"].inc(e.matmul(bk(side, 0, 16, 0, qw),
                                         self.IND_i[:, ((oc - 1) % 8) * 16:((oc - 1) % 8) * 16 + 16],
                                         q2b[:, ((oc - 1) % 2) * 512:((oc - 1) % 2) * 512 + qw],
                                         start=((oc - 1) % 8 == 0), stop=((oc - 1) % 8 == 7)))
            S["s0"].wait(e, s0b + 16)
            S["v0"].wait(e, v0b + 14)
            S["t1"].inc(e.matmul(bk(2 + 1, 0, 128, 0, qw), self.R_sb,
                                 qkbf[:, 15 * 512:15 * 512 + qw], start=True, stop=True))
            S["t2"].inc(e.matmul(bk(5, 0, 16, 0, qw),
                                 self.IND_i[:, 7 * 16:8 * 16],
                                 q2b[:, 1 * 512:1 * 512 + qw],
                                 start=False, stop=True))

        s1b = S["s1"].n

        @blk.scalar
        def _(e):
            for oc in range(16):
                S["t0"].wait(e, t0b + oc + 1)
                if oc >= 2:
                    S["t2"].wait(e, t2b + oc - 1)
                e.activation(qkbf[:, oc * 512:oc * 512 + qw], bk(oc % 2, 0, 128, 0, qw), AF.Copy)
                S["s0"].inc(e.activation(q2b[:, (oc % 2) * 512:(oc % 2) * 512 + qw],
                                         bk(oc % 2, 0, 128, 0, qw), AF.Square))
            S["t2"].wait(e, t2b + 16)
            e.activation(self.dsl[0:16, 0:qw], bk(4, 0, 16, 0, qw), AF.Ln,
                         bias=self.misc[0:16, 208:209])
            e.activation(self.rnb[0:16, 0:qw], self.dsl[0:16, 0:qw], AF.Exp,
                         scale=self.misc[0:16, 202:203])
            e.activation(self.dsl[0:16, 512:512 + qw], bk(5, 0, 16, 0, qw), AF.Ln,
                         bias=self.misc[0:16, 208:209])
            S["s1"].inc(e.activation(self.rnb[0:16, 512:512 + qw], self.dsl[0:16, 512:512 + qw],
                                     AF.Exp, scale=self.misc[0:16, 202:203]))

        t3b = S["t3"].n
        t0vb = S["t0"].n

        v3pre = S["v3"].n

        @blk.tensor
        def _(e):
            S["s1"].wait(e, s1b + 1)
            for oc in range(16):
                if oc >= 2:
                    S["v3"].wait(e, v3pre + oc - 1)
                side_off = 0 if oc < 8 else 512
                S["t3"].inc(e.matmul(bk(6 + oc % 2, 0, 128, 0, qw),
                                     self.expd[:, (oc % 8) * 128:(oc % 8) * 128 + 128],
                                     self.rnb[0:16, side_off:side_off + qw],
                                     start=True, stop=True))
            i = 0
            for ts in range(nts):
                for et in range(2):
                    if i >= 2:
                        S["v1"].wait(e, vcb + 1 + i - 1)
                    for kc in range(NCH):
                        ins = e.matmul(bk(i % 2, 0, 128, 0, 512),
                                       xsb[:, kc * 512 + ts * 128:kc * 512 + ts * 128 + 128],
                                       wv_i[:, kc * 1024 + et * 512:kc * 1024 + et * 512 + 512],
                                       start=(kc == 0), stop=(kc == NCH - 1))
                    S["t0"].inc(ins)
                    i += 1

        v0b2 = S["v0"].n
        v3b2 = S["v3"].n

        @blk.vector
        def _(e):
            for oc in range(16):
                S["t1"].wait(e, t1b + oc + 1)
                qsd = lc[:, 44:45] if oc < 8 else lc[:, 46:47]
                qsr = lc[:, 45:46] if oc < 8 else lc[:, 47:48]
                pkA = self.pkAq if oc < 8 else self.pkAk
                pkB = self.pkBq if oc < 8 else self.pkBk
                qb = qkbf[:, oc * 512:oc * 512 + qw]
                e.scalar_tensor_tensor(self.scr1[:, 0:qw], qb, qsd,
                                       pkA[:, t0_:t0_ + qw], ALU.mult, ALU.mult)
                e.scalar_tensor_tensor(self.scr2[:, 0:qw], bk(2 + (oc % 2), 0, 128, 0, qw),
                                       qsr, pkB[:, t0_:t0_ + qw], ALU.mult, ALU.mult)
                S["v0"].inc(e.tensor_tensor(qb, self.scr1[:, 0:qw], self.scr2[:, 0:qw], ALU.add))
            for oc in range(16):
                S["t3"].wait(e, t3b + oc + 1)
                qb = qkbf[:, oc * 512:oc * 512 + qw]
                if oc < 8:
                    dst = qb
                else:
                    dst = kroll[:, (oc - 8) * 768 + 256:(oc - 8) * 768 + 256 + qw]
                S["v3"].inc(e.tensor_tensor(dst, qb, bk(6 + oc % 2, 0, 128, 0, qw), ALU.mult))
            i = 0
            for ts in range(nts):
                for et in range(2):
                    S["t0"].wait(e, t0vb + i + 1)
                    dst = vroll.rearrange("p (a c) -> p a c", c=65)[
                        :, (2 + ts) * 16 + et * 8:(2 + ts) * 16 + et * 8 + 8, 0:64]
                    S["v1"].inc(e.tensor_copy(
                        dst, bk(i % 2, 0, 128, 0, 512).rearrange("p (h c) -> p h c", c=64)))
                    i += 1
        v1ev = S["v1"].n

        self._ia_windows(li, qi, qw, qkbf, q2b, v1ev)

        t0b4 = S["t0"].n
        v1b4 = S["v1"].n

        @blk.tensor
        def _(e):
            S["v0"].wait(e)
            for dc in range(NCH):
                if dc >= 2:
                    S["v1"].wait(e, v1b4 + dc - 1)
                for ec in range(NCH):
                    ins = e.matmul(bk(dc % 2, 0, 128, 0, qw),
                                   wo_i[:, dc * 1024 + ec * 128:dc * 1024 + ec * 128 + 128],
                                   osbi[:, ec * 512:ec * 512 + qw],
                                   start=(ec == 0), stop=(ec == NCH - 1))
                S["t0"].inc(ins)

        @blk.vector
        def _(e):
            for dc in range(NCH):
                S["t0"].wait(e, t0b4 + dc + 1)
                if dc < 4:
                    dst = yt[:, dc * qw:dc * qw + qw]
                else:
                    dst = yt2[:, (dc - 4) * 513 + 1:(dc - 4) * 513 + 1 + qw]
                S["v1"].inc(e.tensor_tensor(dst, dst, bk(dc % 2, 0, 128, 0, qw), ALU.add))
            if qi < len(QT) - 1:
                e.tensor_copy(
                    kroll.rearrange("p (c t) -> p c t", t=768)[:, :, 0:256],
                    kroll.rearrange("p (c t) -> p c t", t=768)[:, :, qw:qw + 256])
                S["v0"].inc(e.tensor_copy(
                    vroll[:, 0:2 * 1040],
                    vroll[:, (nts - 2 + 2) * 1040:(nts + 2) * 1040]))

        @blk.sync
        def _(e):
            S["v1"].wait(e)
            dns = [dst_name] + ([f"d_ya{li}"] if self.debug == "inner" else [])
            for dn in dns:
                S["dO"].inc(e.dma_start(
                    out=g[dn].rearrange("(c p) t -> p c t", p=128)[:, 0:4, t0_:t0_ + qw],
                    in_=yt[:, 0:4 * qw].rearrange("p (c t) -> p c t", t=qw)), 16)
                S["dO"].inc(e.dma_start(
                    out=g[dn].rearrange("(c p) t -> p c t", p=128)[:, 4:8, t0_:t0_ + qw],
                    in_=yt2.rearrange("p (c t) -> p c t", t=513)[:, :, 1:qw + 1]), 16)

    def _ia_windows(self, li, qi, qw, qkbf, q2b, v1ev):
        S, blk, bk = self.S, self.blk, self.bk
        kroll, vroll, u2i, osbi = self.kroll, self.vroll, self.u2i, self.osbi
        nw = qw // 256
        nch_t = nw * NCH
        AVB = (6, 0)
        BCB = (7, 1)
        itmp = (q2b[:, 0:256], q2b[:, 512:768])
        t1t0 = S["t1"].n
        v0b0 = S["v0"].n
        v3b0 = S["v3"].n
        m = {k: {} for k in ("t0", "s2e", "g0", "t1a", "s2r", "t3", "v0d", "t1i", "v0c")}

        def chain(c):
            w, hp = c // NCH, c % NCH
            ks_lo = 2 if (qi == 0 and w == 0) else 0
            return w, hp, ks_lo, c % 2

        def S1(c):
            w, hp, ks_lo, p = chain(c)

            @blk.tensor
            def _(e, c=c, w=w, hp=hp, ks_lo=ks_lo):
                if c == 0:
                    S["v0"].wait(e, v0b0)
                    S["v3"].wait(e, v3b0)
                else:
                    S["s2"].wait(e, m["s2e"][c - 1])
                qcol = hp * 512 + w * 256
                for ks in range(ks_lo, 4):
                    kc0 = hp * 768 + w * 256 + ks * 128
                    S["t0"].inc(e.matmul(
                        bk(2 + (ks // 2), 0, 128, (ks % 2) * 256, (ks % 2) * 256 + 256),
                        kroll[0:64, kc0:kc0 + 128],
                        qkbf[0:64, qcol:qcol + 256], start=True, stop=True))
                    S["t0"].inc(e.matmul(
                        bk(4 + (ks // 2), 0, 128, (ks % 2) * 256, (ks % 2) * 256 + 256),
                        kroll[64:128, kc0:kc0 + 128],
                        qkbf[64:128, qcol:qcol + 256], start=True, stop=True))
            m["t0"][c] = S["t0"].n

        def S2(c):
            w, hp, ks_lo, p = chain(c)
            bu = p * 2048

            @blk.scalar
            def _(e, c=c, ks_lo=ks_lo, bu=bu):
                S["t0"].wait(e, m["t0"][c])
                S["t1"].wait(e, m["t1a"][c - 2] if c >= 2 else t1t0)
                if ks_lo == 0:
                    e.activation(u2i[:, bu:bu + 1024], self.psA[:, 1024:2048], AF.Exp,
                                 scale=self.misc[:, 206:207], bias=self.misc[:, 207:208])
                    S["s2"].inc(e.activation(u2i[:, bu + 1024:bu + 2048], self.psB[:, 0:1024],
                                             AF.Exp, scale=self.misc[:, 206:207],
                                             bias=self.misc[:, 207:208]))
                else:
                    e.activation(u2i[:, bu + 512:bu + 1024], self.psA[:, 1536:2048], AF.Exp,
                                 scale=self.misc[:, 206:207], bias=self.misc[:, 207:208])
                    S["s2"].inc(e.activation(u2i[:, bu + 1536:bu + 2048], self.psB[:, 512:1024],
                                             AF.Exp, scale=self.misc[:, 206:207],
                                             bias=self.misc[:, 207:208]))
            m["s2e"][c] = S["s2"].n

        def S3(c):
            w, hp, ks_lo, p = chain(c)
            bu = p * 2048

            @blk.gpsimd
            def _(e, c=c, ks_lo=ks_lo, bu=bu):
                S["s2"].wait(e, m["s2e"][c])
                for hh in range(2):
                    base = bu + hh * 1024
                    if ks_lo == 0:
                        S["g0"].inc(e.affine_select(
                            u2i[:, base:base + 512], u2i[:, base:base + 512],
                            [[128, 2], [-1, 256]], ALU.is_ge, 0.0,
                            base=0, channel_multiplier=1))
                    S["g0"].inc(e.affine_select(
                        u2i[:, base + 512:base + 1024], u2i[:, base + 512:base + 1024],
                        [[-128, 2], [1, 256]], ALU.is_ge, 0.0,
                        base=0, channel_multiplier=-1))
            m["g0"][c] = S["g0"].n

        def S4(c):
            w, hp, ks_lo, p = chain(c)
            bu = p * 2048
            ab = AVB[p]

            @blk.tensor
            def _(e, c=c, w=w, hp=hp, ks_lo=ks_lo, bu=bu, ab=ab):
                S["g0"].wait(e, m["g0"][c])
                if c >= 2:
                    S["v0"].wait(e, m["v0d"][c - 2])
                else:
                    S["v1"].wait(e, v1ev)
                first = True
                for ks in range(ks_lo, 4):
                    tchunk = 2 * w + ks
                    e.matmul(bk(ab, 0, 65, 0, 256),
                             vroll[:, tchunk * 1040 + (2 * hp) * 65:tchunk * 1040 + (2 * hp) * 65 + 65],
                             u2i[:, bu + ks * 256:bu + ks * 256 + 256],
                             start=first, stop=False)
                    first = False
                    ins = e.matmul(bk(ab, 0, 65, 256, 512),
                                   vroll[:, tchunk * 1040 + (2 * hp + 1) * 65:tchunk * 1040 + (2 * hp + 1) * 65 + 65],
                                   u2i[:, bu + 1024 + ks * 256:bu + 1024 + ks * 256 + 256],
                                   start=False, stop=(ks == 3))
                S["t1"].inc(ins)
            m["t1a"][c] = S["t1"].n

        def S5(c):
            w, hp, ks_lo, p = chain(c)
            ab = AVB[p]
            ds = self.dsl[64:65, p * 512:p * 512 + 512]

            @blk.scalar
            def _(e, c=c, ab=ab, ds=ds):
                S["t1"].wait(e, m["t1a"][c])
                e.activation(ds, bk(ab, 64, 65, 0, 512), AF.Ln)
                S["s2"].inc(e.activation(ds, ds, AF.Exp, scale=self.misc[64:65, 203:204]))
            m["s2r"][c] = S["s2"].n

        def S6(c):
            w, hp, ks_lo, p = chain(c)
            bb = BCB[p]
            ds = self.dsl[64:65, p * 512:p * 512 + 512]

            @blk.tensor
            def _(e, c=c, bb=bb, ds=ds):
                S["s2"].wait(e, m["s2r"][c])
                if c >= 2:
                    S["v0"].wait(e, m["v0c"][c - 2])
                elif bb == 1:
                    S["v1"].wait(e, v1ev)
                S["t3"].inc(e.matmul(bk(bb, 0, 64, 0, 512), self.onesf2[64:65, 0:64],
                                     ds, start=True, stop=True))
            m["t3"][c] = S["t3"].n

        def S7(c):
            w, hp, ks_lo, p = chain(c)
            ab, bb = AVB[p], BCB[p]
            qcol = hp * 512 + w * 256

            @blk.vector
            def _(e, c=c, ab=ab, bb=bb, qcol=qcol, p=p):
                S["t3"].wait(e, m["t3"][c])
                e.tensor_copy(self.scr1[0:64, 0:512], bk(bb, 0, 64, 0, 512))
                e.tensor_tensor(osbi[0:64, qcol:qcol + 256], bk(ab, 0, 64, 0, 256),
                                self.scr1[0:64, 0:256], ALU.mult)
                S["v0"].inc(e.tensor_tensor(itmp[p][0:64, 0:256], bk(ab, 0, 64, 256, 512),
                                            self.scr1[0:64, 256:512], ALU.mult))
            m["v0d"][c] = S["v0"].n

        def S8(c):
            w, hp, ks_lo, p = chain(c)
            bb = BCB[p]

            @blk.tensor
            def _(e, c=c, bb=bb, p=p):
                S["v0"].wait(e, m["v0d"][c])
                S["t1"].inc(e.matmul(bk(bb, 64, 128, 0, 256), self.IDE, itmp[p][0:64, 0:256],
                                     start=True, stop=True, tile_position=(0, 64)))
            m["t1i"][c] = S["t1"].n

        def S9(c):
            w, hp, ks_lo, p = chain(c)
            bb = BCB[p]
            qcol = hp * 512 + w * 256

            @blk.vector
            def _(e, c=c, bb=bb, qcol=qcol):
                S["t1"].wait(e, m["t1i"][c])
                S["v0"].inc(e.tensor_copy(osbi[64:128, qcol:qcol + 256],
                                          bk(bb, 64, 128, 0, 256)))
            m["v0c"][c] = S["v0"].n

        for s in range(nch_t + 3):
            if s < nch_t:
                S1(s); S2(s); S3(s)
            if 1 <= s <= nch_t:
                S4(s - 1); S5(s - 1)
            if 2 <= s <= nch_t + 1:
                S6(s - 2); S7(s - 2)
            if 3 <= s <= nch_t + 2:
                S8(s - 3); S9(s - 3)

    def _inner_ffn_b01(self, li, xn_i, w1p, glb, hmb, xb2b):
        S, blk, bk, g = self.S, self.blk, self.bk, self.g
        yt, yt2, xsb, lc = self.yt, self.yt2, self.xsb, self.lc

        v0pre = S["v0"].n
        v1pre = S["v1"].n
        t0pre0 = S["t0"].n
        xnm = []

        wdb = S["dW"].n
        t0pre = t0pre0
        ntile = len(QT)

        @blk.gpsimd
        def _(e):
            S["t0"].wait(e, t0pre)
            S["v0"].wait(e, v0pre)
            S["v1"].wait(e, v1pre)
            for j in (0, 1):
                S["dW"].inc(e.dma_start(out=w1p[:, (j % 2) * 2048:(j % 2) * 2048 + 1024],
                                        in_=g["W1t"][li][j]), 16)
                S["dW"].inc(e.dma_start(out=w1p[:, (j % 2) * 2048 + 1024:(j % 2) * 2048 + 2048],
                                        in_=g["W1t"][li][NJ + j]), 16)

        for qi, (t0_, qw) in enumerate(QT):
            xb = xsb if qi % 2 == 0 else xb2b
            dHb = S["dH"].n

            @blk.gpsimd
            def _(e, qi=qi, t0_=t0_, qw=qw, xb=xb):
                if qi == 0:
                    S["t0"].wait(e)
                    S["v0"].wait(e)
                    S["v1"].wait(e, v1pre)
                if qi >= 2:
                    S["v1"].wait(e, xnm[qi - 2])
                S["dH"].inc(e.dma_start(
                    out=xb[:, 0:4 * 512].rearrange("p (c t) -> p c t", t=512)[:, :, 0:qw],
                    in_=g["ya_dram"].rearrange("(c p) t -> p c t", p=128)[:, 0:4, t0_:t0_ + qw]), 16)
                if qi == 0:
                    S["dH"].inc(e.dma_start(
                        out=xb[:, 4 * 512:8 * 512].rearrange("p (c t) -> p c t", t=512)[:, :, 1:qw],
                        in_=g["ya_dram"].rearrange("(c p) t -> p c t", p=128)[:, 4:8, 0:qw - 1]), 16)
                else:
                    S["dH"].inc(e.dma_start(
                        out=xb[:, 4 * 512:8 * 512].rearrange("p (c t) -> p c t", t=512)[:, :, 0:qw],
                        in_=g["ya_dram"].rearrange("(c p) t -> p c t", p=128)[:, 4:8, t0_ - 1:t0_ + qw - 1]), 16)

            if qi == 0:
                @blk.vector
                def _(e):
                    S["t0"].wait(e, t0pre0)
                    S["v2"].inc(e.memset(
                        xsb[:, 4 * 512:8 * 512].rearrange("p (c t) -> p c t", t=512)[:, :, 0:1], 0.0))

            xnm.append(self.emit_ln(
                qw,
                src_f32=None,
                src_bf=lambda cc, xb=xb: xb[:, cc * 512:cc * 512 + qw],
                dst=lambda cc, t0_=t0_, qw=qw: xn_i[:, cc * LOC + t0_:cc * LOC + t0_ + qw],
                gamma=lambda cc: lc[:, 56 + cc:57 + cc],
                beta=lambda cc: lc[:, 64 + cc:65 + cc],
                pre_wait=lambda e, dHb=dHb: S["dH"].wait(e, dHb + 32)))

        @blk.gpsimd
        def _(e):
            for j in range(2, NJ):
                S["t0"].wait(e, t0pre + (j - 1) * ntile)
                S["dW"].inc(e.dma_start(out=w1p[:, (j % 2) * 2048:(j % 2) * 2048 + 1024],
                                        in_=g["W1t"][li][j]), 16)
                S["dW"].inc(e.dma_start(out=w1p[:, (j % 2) * 2048 + 1024:(j % 2) * 2048 + 2048],
                                        in_=g["W1t"][li][NJ + j]), 16)

        t0b = S["t0"].n
        s2b = S["s2"].n
        v0b = S["v0"].n

        @blk.tensor
        def _(e):
            i = 0
            for j in range(NJ):
                S["dW"].wait(e, wdb + 32 * (j + 1))
                for qi, (t0_, qw) in enumerate(QT):
                    if j == 0:
                        S["v1"].wait(e, xnm[qi])
                    if i >= 2:
                        S["s2"].wait(e, s2b + i - 1)
                        S["v0"].wait(e, v0b + max(0, i - 1))
                    pb = 2 * (i % 2)
                    for kc in range(NCH):
                        e.matmul(bk(pb, 0, 128, 0, qw),
                                 w1p[:, (j % 2) * 2048 + kc * 128:(j % 2) * 2048 + kc * 128 + 128],
                                 xn_i[:, kc * LOC + t0_:kc * LOC + t0_ + qw],
                                 start=(kc == 0), stop=(kc == NCH - 1))
                        ins = e.matmul(bk(pb + 1, 0, 128, 0, qw),
                                       w1p[:, (j % 2) * 2048 + 1024 + kc * 128:(j % 2) * 2048 + 1024 + kc * 128 + 128],
                                       xn_i[:, kc * LOC + t0_:kc * LOC + t0_ + qw],
                                       start=(kc == 0), stop=(kc == NCH - 1))
                    S["t0"].inc(ins)
                    i += 1

        @blk.scalar
        def _(e):
            i = 0
            for j in range(NJ):
                for qi, (t0_, qw) in enumerate(QT):
                    S["t0"].wait(e, t0b + i + 1)
                    if i >= 1:
                        S["v0"].wait(e, v0b + i - 1)
                    S["s2"].inc(e.activation(glb[:, 0:qw], bk(2 * (i % 2) + 1, 0, 128, 0, qw),
                                             AF.Gelu, bias=lc[:, 22 + j:23 + j]))
                    i += 1

        dOb = S["dO"].n

        @blk.vector
        def _(e):
            i = 0
            for j in range(NJ):
                for qi, (t0_, qw) in enumerate(QT):
                    S["s2"].wait(e, s2b + i + 1)
                    if i >= 2:
                        S["dO"].wait(e, dOb + 16 * (i - 1))
                    S["v0"].inc(e.scalar_tensor_tensor(
                        hmb[:, (i % 2) * 512:(i % 2) * 512 + qw],
                        bk(2 * (i % 2), 0, 128, 0, qw), lc[:, j:j + 1], glb[:, 0:qw],
                        ALU.add, ALU.mult))
                    i += 1

        @blk.sync
        def _(e):
            i = 0
            for j in range(NJ):
                for qi, (t0_, qw) in enumerate(QT):
                    S["v0"].wait(e, v0b + i + 1)
                    S["dO"].inc(e.dma_start(
                        out=g["hm_dram"][:, j * LOC + t0_:j * LOC + t0_ + qw],
                        in_=hmb[:, (i % 2) * 512:(i % 2) * 512 + qw]), 16)
                    i += 1

    def _inner_ffn_b2(self, li, w2r, hmt, yres):
        S, blk, bk, g = self.S, self.blk, self.bk, self.g
        lc = self.lc

        wdb = S["dW"].n

        @blk.gpsimd
        def _(e):
            S["t0"].wait(e)
            S["v0"].wait(e)
            for oc in range(NCH):
                S["dW"].inc(e.dma_start(out=w2r[:, oc * NJ * 128:(oc + 1) * NJ * 128],
                                        in_=g["W2t"][li][oc]), 16)

        for qi, (t0_, qw) in enumerate(QT):
            dHb = S["dH"].n

            @blk.sync
            def _(e, qi=qi, t0_=t0_, qw=qw, dHb=dHb):
                S["v1"].wait(e)
                S["dH"].inc(e.dma_start(
                    out=hmt[:, 0:NJ * qw].rearrange("p (j t) -> p j t", t=qw),
                    in_=g["hm_dram"].rearrange("p (j t) -> p j t", t=LOC)[:, :, t0_:t0_ + qw]), 16)
                S["dH"].inc(e.dma_start(
                    out=yres[:, 0:NCH * qw].rearrange("p (c t) -> p c t", t=qw),
                    in_=g["ya_dram"].rearrange("(c p) t -> p c t", p=128)[:, :, t0_:t0_ + qw]), 16)

            t0b = S["t0"].n
            v1b = S["v1"].n

            @blk.tensor
            def _(e, qi=qi, t0_=t0_, qw=qw, dHb=dHb, t0b=t0b, v1b=v1b):
                S["dW"].wait(e, wdb + 16 * NCH)
                S["dH"].wait(e, dHb + 32)
                for oc in range(NCH):
                    if oc >= 2:
                        S["v1"].wait(e, v1b + oc - 1)
                    for j in range(NJ):
                        ins = e.matmul(bk(oc % 2, 0, 128, 0, qw),
                                       w2r[:, oc * NJ * 128 + j * 128:oc * NJ * 128 + j * 128 + 128],
                                       hmt[:, j * qw:j * qw + qw],
                                       start=(j == 0), stop=(j == NJ - 1))
                    S["t0"].inc(ins)

            @blk.vector
            def _(e, qi=qi, t0_=t0_, qw=qw, t0b=t0b):
                for oc in range(NCH):
                    S["t0"].wait(e, t0b + oc + 1)
                    S["v1"].inc(e.scalar_tensor_tensor(
                        yres[:, oc * qw:oc * qw + qw], bk(oc % 2, 0, 128, 0, qw),
                        lc[:, 48 + oc:49 + oc], yres[:, oc * qw:oc * qw + qw],
                        ALU.add, ALU.add))

            @blk.sync
            def _(e, qi=qi, t0_=t0_, qw=qw):
                S["v1"].wait(e)
                dns = ["y_dram"] + ([f"d_y{li}"] if self.debug == "inner" else [])
                for dn in dns:
                    S["dO"].inc(e.dma_start(
                        out=g[dn].rearrange("(c p) t -> p c t", p=128)[:, :, t0_:t0_ + qw],
                        in_=yres[:, 0:NCH * qw].rearrange("p (c t) -> p c t", t=qw)), 16)

    def p9_final(self):
        S, blk, g = self.S, self.blk, self.g
        nc = self.nc
        with nc.sbuf_tensor("f1", [128, NCH * 512], F32) as f1, \
             nc.sbuf_tensor("f2", [128, NCH * 512], F32) as f2, \
             nc.sbuf_tensor("fbc", [128, NCH * 512], BF16) as fb:
            for qi, (t0_, qw) in enumerate(QT):
                dHb = S["dH"].n

                @blk.sync
                def _(e, qi=qi, t0_=t0_, qw=qw):
                    if qi == 0:
                        S["t0"].wait(e)
                        S["v0"].wait(e)
                    S["v1"].wait(e)
                    S["dO"].wait(e)
                    S["dH"].inc(e.dma_start(
                        out=f1[:, 0:NCH * qw].rearrange("p (c t) -> p c t", t=qw),
                        in_=g["y_dram"].rearrange("(c p) t -> p c t", p=128)[:, :, t0_:t0_ + qw]), 16)
                    S["dH"].inc(e.dma_start(
                        out=f2[:, 0:NCH * qw].rearrange("p (c t) -> p c t", t=qw),
                        in_=g["x1_dram"].rearrange("(c p) t -> p c t", p=128)[:, :, t0_:t0_ + qw]), 16)

                vsb = S["v2"].n

                @blk.vector
                def _(e, dHb=dHb, qw=qw):
                    S["dH"].wait(e, dHb + 32)
                    for cc in range(NCH):
                        ins = e.tensor_tensor(f1[:, cc * qw:cc * qw + qw],
                                              f1[:, cc * qw:cc * qw + qw],
                                              f2[:, cc * qw:cc * qw + qw], ALU.add)
                    S["v2"].inc(ins)

                self.emit_ln(
                    qw,
                    src_f32=lambda cc, qw=qw: f1[:, cc * qw:cc * qw + qw],
                    src_bf=lambda cc: fb[:, cc * 512:cc * 512 + qw],
                    dst=lambda cc, qw=qw: f1[:, cc * qw:cc * qw + qw],
                    gamma=None, beta=None,
                    pre_wait=lambda e, vsb=vsb: S["v2"].wait(e, vsb + 1))

                @blk.sync
                def _(e, qi=qi, t0_=t0_, qw=qw):
                    S["v1"].wait(e)
                    S["dO"].inc(e.dma_start(
                        out=g["out"].rearrange("(c p) t -> p c t", p=128)[:, :, t0_:t0_ + qw],
                        in_=f1[:, 0:NCH * qw].rearrange("p (c t) -> p c t", t=qw)), 16)




def prepare_inputs(ins):
    x = ins["L0_attn_x"]
    w_q = ins["L0_attn_w_q"]; w_kv = ins["L0_attn_w_kv"]; w_o = ins["L0_attn_w_o"]
    base = {}
    base["wqA"] = tile_weight(w_q[:, PERM_A])
    base["wkA"] = tile_weight(w_kv[:, :D][:, PERM_A])
    base["wvA"] = np.ascontiguousarray(w_kv[:, D:])
    base["woA"] = tile_weight(w_o)
    lnA = np.zeros((128, 16), np.float32)
    lnA[:, 0:8] = ins["L0_attn_ln_w"].reshape(8, 128).T
    lnA[:, 8:16] = ins["L0_attn_ln_b"].reshape(8, 128).T
    base["lnA"] = lnA
    base["theta"] = np.asarray(ins["L0_attn_theta"], np.float32).reshape(1, 1)
    base["fb"] = FB.reshape(1, 32)
    base["t_k"] = np.arange(T, dtype=np.float32).reshape(1, T)
    base["Rm"] = rot_lhsT()
    base["IdeM"] = np.eye(64, dtype=np.float32)
    base["IND"] = head_indicator()
    base["EXPD"] = head_expander()
    Wqkv = ins["L0_ffn_Wqkv"]
    base["Wqk"] = np.stack([tile_weight(Wqkv[li][:, :2 * D]) for li in range(NL)])
    base["Wv6"] = np.ascontiguousarray(Wqkv[:, :, 2 * D:])
    base["Wo6"] = np.stack([tile_weight(ins["L0_ffn_Wo"][li]) for li in range(NL)])
    W1 = ins["L0_ffn_W1"]; W2 = ins["L0_ffn_W2"]
    w1t = []
    for li in range(NL):
        a = tile_weight(W1[li][:, :2730], mc=NJ)
        gt = tile_weight(W1[li][:, 2730:], mc=NJ)
        w1t.append(np.concatenate([a, gt], 0))
    base["W1t"] = np.stack(w1t)
    base["W2t"] = np.stack([tile_weight(W2[li], kc=NJ) for li in range(NL)])
    b1 = ins["L0_ffn_b1"]
    b1c = np.zeros((NL, 128, 44), np.float32)
    for li in range(NL):
        a = np.zeros(NJ * 128, np.float32); a[:2730] = b1[li][:2730]
        gt = np.zeros(NJ * 128, np.float32); gt[:2730] = b1[li][2730:]
        b1c[li, :, 0:22] = a.reshape(NJ, 128).T
        b1c[li, :, 22:44] = gt.reshape(NJ, 128).T
    base["b1c"] = b1c
    base["b2c"] = np.stack([ins["L0_ffn_b2"][li].reshape(8, 128).T for li in range(NL)])
    base["lng"] = np.stack([ins["L0_ffn_ln_g"][li].reshape(8, 128).T for li in range(NL)])
    base["lnb6"] = np.stack([ins["L0_ffn_ln_b"][li].reshape(8, 128).T for li in range(NL)])
    qs, ks = ins["L0_ffn_q_scale"], ins["L0_ffn_k_scale"]
    qkv4 = np.zeros((NL, 128, 4), np.float32)
    qkabs = np.zeros((NL, 1, 128), np.float32)
    for li in range(NL):
        qkv4[li, :, 0] = np.tile(qs[li], 2)
        qkv4[li, :, 1] = np.tile(np.concatenate([qs[li][32:], qs[li][:32]]), 2)
        qkv4[li, :, 2] = np.tile(ks[li], 2)
        qkv4[li, :, 3] = np.tile(np.concatenate([ks[li][32:], ks[li][:32]]), 2)
        qkabs[li, 0, 0:64] = np.abs(qs[li])
        qkabs[li, 0, 64:128] = np.abs(ks[li])
    base["qkv4"] = qkv4
    base["qkabs"] = qkabs

    maps = []
    for c in range(8):
        b, par = c // 2, c % 2
        toff = (0, 256)[par]
        m = dict(base)
        xb = np.ascontiguousarray(x[b].T.astype(np.float32))
        m["x_fm"] = xb
        m["x_loc"] = np.ascontiguousarray(xb[:, toff:toff + LOC])
        m["t_q"] = np.arange(toff, toff + LOC, dtype=np.float32).reshape(1, LOC)
        pos = np.arange(toff, toff + LOC)
        m["rpAq"] = np.ascontiguousarray(np.tile((ICOS[pos] * ISCALE[pos]).T, (2, 1)).astype(ml_dtypes.bfloat16))
        m["rpBq"] = np.ascontiguousarray(np.tile((ISIN[pos] * ISCALE[pos]).T, (2, 1)).astype(ml_dtypes.bfloat16))
        m["rpAk"] = np.ascontiguousarray(np.tile((ICOS[pos] / ISCALE[pos]).T, (2, 1)).astype(ml_dtypes.bfloat16))
        m["rpBk"] = np.ascontiguousarray(np.tile((ISIN[pos] / ISCALE[pos]).T, (2, 1)).astype(ml_dtypes.bfloat16))
        maps.append(m)
    return maps


def kernel(**inputs):
    nc, g = Builder(debug=None).build()
    maps = prepare_inputs(inputs)
    res = run_bass_kernel_spmd(nc, maps, core_ids=list(range(8)))
    out = np.empty((4, T, D), np.float32)
    for c in range(8):
        b, par = c // 2, c % 2
        o = res.results[c]["out"]
        if par == 0:
            out[b, 0:2048] = o[:, 0:2048].T
        else:
            out[b, 2048:4096] = o[:, 2048 - 256:4096 - 256].T
    return out
